# version 28
# baseline (speedup 1.0000x reference)
import numpy as np
import ml_dtypes
from contextlib import ExitStack

import concourse.bass as bass
import concourse.mybir as mybir
from concourse.bass_utils import run_bass_kernel_spmd

F32 = mybir.dt.float32
BF16 = mybir.dt.bfloat16
AF = mybir.ActivationFunctionType
OP = mybir.AluOpType

B, T, C, H, D = 2, 16384, 1024, 16, 64
NCORES = 8
TPC = (B * T) // NCORES
CHUNK = 2048
NCHUNK = TPC // CHUNK
NPAIR = H // 2
NE = 16
MFF = 512
EPS = 1e-5
NKC = C // 128
ECAP = 512
NTC = CHUNK // 128


class Ctr:
    def __init__(self, nc, name, step=1):
        self.sem = nc.semaphore(name).__enter__()
        self.step = step
        self.n = 0

    def inc(self, instr):
        instr.then_inc(self.sem, self.step)
        self.n += self.step
        return self.n


def unit_list():
    return [("b1", s) for s in range(4)] + [("b2", s) for s in range(2)] + [("b3", 0)]


def unit_rects(branch, seg, head):
    rects = []
    if branch in ("b1", "b3"):
        if branch == "b1":
            kstart = lambda kc: (512 * seg + 128 * kc, 1)
            qstart = lambda h: (512 * seg + 256 * h, 1)
            vtile = lambda kc: (4 * seg + kc, False)
        else:
            c = head % 4
            kstart = lambda kc: (c + 512 * kc, 4)
            qstart = lambda h: (c + 1024 * h, 4)
            vtile = lambda kc: (4 * c + kc, True)
        for h in (0, 1):
            for kc in range(2 * (h + 1)):
                o = 128 * kc - 256 * h
                masks = [(0, o)] if 0 <= o < 256 else []
                rects.append(dict(h=h, k=kstart(kc), q=qstart(h), v=vtile(kc),
                                  start=max(0, o), masks=masks))
    else:
        p = head % 2
        for h in (0, 1):
            for h2 in range(h + 1):
                for dp in (0, 1):
                    c = p + 2 * dp
                    masks = [(1 + 2 * dp + t, 128 * t) for t in (0, 1)] if h == h2 else []
                    rects.append(dict(h=h, k=(c + 1024 * seg + 512 * h2, 4),
                                      q=(1024 * seg + p + 512 * h, 2),
                                      v=(4 * c + 2 * seg + h2, True),
                                      start=0, masks=masks))
    return rects


def unit_ycols(branch, seg, head):
    if branch == "b1":
        return (512 * seg, 1)
    if branch == "b2":
        return (1024 * seg + head % 2, 2)
    return (head % 4, 4)


def _t3(src2d, p=128):
    return src2d.rearrange("(rc p) c -> p rc c", p=p)


def _d3(dstbuf, ncols):
    return dstbuf[:].rearrange("p (rc c) -> p rc c", c=ncols)


def build(taps=()):
    taps = set(taps)
    nc = bass.Bass("TRN2", target_bir_lowering=False, debug=False)

    x_in = nc.dram_tensor("x", [TPC, C], F32, kind="ExternalInput").ap()
    out_hbm = nc.dram_tensor("out", [TPC, C], F32, kind="ExternalOutput").ap()
    caw = nc.dram_tensor("caw", [C, 3 * C], F32, kind="ExternalInput").ap()
    cab = nc.dram_tensor("cab", [3 * C], F32, kind="ExternalInput").ap()
    qw = nc.dram_tensor("qw", [C, C], F32, kind="ExternalInput").ap()
    qb = nc.dram_tensor("qb", [C], F32, kind="ExternalInput").ap()
    kw = nc.dram_tensor("kw", [C, C], F32, kind="ExternalInput").ap()
    kb = nc.dram_tensor("kb", [C], F32, kind="ExternalInput").ap()
    vw = nc.dram_tensor("vw", [C, C], F32, kind="ExternalInput").ap()
    vb = nc.dram_tensor("vb", [C], F32, kind="ExternalInput").ap()
    ow = nc.dram_tensor("ow", [C, C], F32, kind="ExternalInput").ap()
    ob = nc.dram_tensor("ob", [C], F32, kind="ExternalInput").ap()
    gw_in = nc.dram_tensor("gw", [NE, C], F32, kind="ExternalInput").ap()
    wg_in = nc.dram_tensor("wg", [NE, C, MFF], F32, kind="ExternalInput").ap()
    wu_in = nc.dram_tensor("wu", [NE, C, MFF], F32, kind="ExternalInput").ap()
    wd_in = nc.dram_tensor("wd", [NE, MFF, C], F32, kind="ExternalInput").ap()
    wsg_in = nc.dram_tensor("wsg", [C, MFF], F32, kind="ExternalInput").ap()
    wsu_in = nc.dram_tensor("wsu", [C, MFF], F32, kind="ExternalInput").ap()
    wsd_in = nc.dram_tensor("wsd", [MFF, C], F32, kind="ExternalInput").ap()
    id32_in = nc.dram_tensor("id32", [128, 128], F32, kind="ExternalInput").ap()
    id16_in = nc.dram_tensor("id16", [128, 128], BF16, kind="ExternalInput").ap()
    ones_in = nc.dram_tensor("ones", [128, 128], BF16, kind="ExternalInput").ap()
    masks_in = nc.dram_tensor("masks", [128, 5 * 128], BF16, kind="ExternalInput").ap()
    sel_in = nc.dram_tensor("sel", [16, NE * 128], BF16, kind="ExternalInput").ap()

    wq_h = nc.dram_tensor("wq_h", [128, NKC * C], BF16, kind="Internal").ap()
    wk_h = nc.dram_tensor("wk_h", [128, NKC * C], BF16, kind="Internal").ap()
    wv_h = nc.dram_tensor("wv_h", [128, NKC * C], BF16, kind="Internal").ap()
    ow_h = nc.dram_tensor("ow_h", [C, C], BF16, kind="Internal").ap()
    wg_h = nc.dram_tensor("wg_h", [NE * 128, NKC * MFF], BF16, kind="Internal").ap()
    wu_h = nc.dram_tensor("wu_h", [NE * 128, NKC * MFF], BF16, kind="Internal").ap()
    wd_h = nc.dram_tensor("wd_h", [NE * 128, 4 * C], BF16, kind="Internal").ap()
    wsg_h = nc.dram_tensor("wsg_h", [128, NKC * MFF], BF16, kind="Internal").ap()
    wsu_h = nc.dram_tensor("wsu_h", [128, NKC * MFF], BF16, kind="Internal").ap()
    wsd_h = nc.dram_tensor("wsd_h", [128, 4 * C], BF16, kind="Internal").ap()
    x2r_h = [nc.dram_tensor(f"x2r_h{i}", [CHUNK + 1, C], BF16, kind="Internal").ap()
             for i in range(NCHUNK)]
    disp_h = [nc.dram_tensor(f"disp_h{i}", [NE * ECAP + 128, 2], F32, kind="Internal").ap()
              for i in range(NCHUNK)]
    acc_h = [nc.dram_tensor(f"acc_h{i}", [CHUNK + 1, C], F32, kind="Internal").ap()
             for i in range(NCHUNK)]
    eo_h = [nc.dram_tensor(f"eo_h{i}", [NE * ECAP + 128, C], BF16, kind="Internal").ap()
            for i in range(NCHUNK)]
    ltx_in = nc.dram_tensor("ltx", [128, 128], BF16, kind="ExternalInput").ap()
    iota_in = nc.dram_tensor("iota", [128, NTC], F32, kind="ExternalInput").ap()
    dinit_in = nc.dram_tensor("dinit", [NE * ECAP + 128, 2], F32, kind="ExternalInput").ap()
    zrow_in = nc.dram_tensor("zrow", [1, C], BF16, kind="ExternalInput").ap()
    zrows_in = nc.dram_tensor("zrows", [128, C], BF16, kind="ExternalInput").ap()
    trashc_in = nc.dram_tensor("trashc", [128, 2 * NTC], F32, kind="ExternalInput").ap()

    tap_t = {}
    for name, shape, dt in [
        ("t_xT", [128, NKC * CHUNK], BF16), ("t_q", [128, CHUNK], BF16),
        ("t_k", [128, CHUNK], BF16), ("t_vpl", [128, NTC * C], BF16),
        ("t_vsr", [128, NTC * C], BF16), ("t_y", [128, CHUNK], F32),
        ("t_s", [128, CHUNK], F32), ("t_Y", [128, NKC * CHUNK], BF16),
        ("t_r1", [128, NKC * CHUNK], BF16), ("t_x2", [128, NKC * CHUNK], BF16),
        ("t_route", [NE, CHUNK], BF16), ("t_acc", [128, NKC * CHUNK], F32),
        ("t_weff", [128, NKC * C], BF16),
    ]:
        if name in taps:
            tap_t[name] = nc.dram_tensor(name, shape, dt, kind="ExternalOutput").ap()

    stk = ExitStack()
    sb = lambda name, shape, dt: stk.enter_context(nc.sbuf_tensor(name, shape, dt))

    id32 = sb("id32c", [128, 128], F32)
    id16 = sb("id16c", [128, 128], BF16)
    ones = sb("onesc", [128, 128], BF16)
    masks = sb("masksc", [128, 5 * 128], BF16)
    bqe = sb("bqe", [128, NKC], F32)
    bke = sb("bke", [128, NKC], F32)
    bve = sb("bve", [128, NKC], F32)
    obs = sb("obs", [128, NKC], F32)
    gwT = sb("gwT", [128, 3 * NKC * NE], BF16)
    sel = sb("selc", [16, NE * 128], BF16)
    ltx = sb("ltxc", [128, 128], BF16)
    iotac = sb("iotac", [128, NTC], F32)
    trashc = sb("trashc_s", [128, 2 * NTC], F32)
    Dki2 = [[sb(f"Dki_{i}_{k}", [128, NTC], mybir.dt.int32) for k in range(2)]
            for i in range(NCHUNK)]

    c_g = Ctr(nc, "c_g", 16)
    c_ld1 = Ctr(nc, "c_ld1", 16)
    c_g2 = Ctr(nc, "c_g2", 16)
    c_x = Ctr(nc, "c_x", 16)
    c_wf = Ctr(nc, "c_wf", 16)
    c_wv = Ctr(nc, "c_wv", 16)
    c_wq = [Ctr(nc, "c_wq0", 16), Ctr(nc, "c_wq1", 16)]
    c_ow = Ctr(nc, "c_ow", 16)
    c_e = [Ctr(nc, "c_e0", 16), Ctr(nc, "c_e1", 16)]
    c_o = [Ctr(nc, "c_o0", 16), Ctr(nc, "c_o1", 16)]
    c_tap = Ctr(nc, "c_tap", 16)
    c_x2r = Ctr(nc, "c_x2r", 16)
    c_ds = Ctr(nc, "c_ds", 16)
    c_gx = [Ctr(nc, f"c_gx{i}", 16) for i in range(4)]
    c_tr = [Ctr(nc, "c_tr0", 16), Ctr(nc, "c_tr1", 16)]
    c_id = [Ctr(nc, f"c_id{i}", 16) for i in range(4)]
    c_ws = Ctr(nc, "c_ws", 16)
    c_eo = [Ctr(nc, "c_eo0", 16), Ctr(nc, "c_eo1", 16)]
    c_eg = [Ctr(nc, "c_eg0", 16), Ctr(nc, "c_eg1", 16)]
    c_idc = Ctr(nc, "c_idc", 1)
    c_shw = [Ctr(nc, "c_shw0", 16), Ctr(nc, "c_shw1", 16)]
    c_ac = [Ctr(nc, "c_ac0", 16), Ctr(nc, "c_ac1", 16)]
    c_p = Ctr(nc, "c_p", 1)
    c_a = Ctr(nc, "c_a", 1)
    c_v = Ctr(nc, "c_v", 1)
    c_q = Ctr(nc, "c_q", 1)

    blk_cm = nc.Block()
    block = blk_cm.__enter__()

    def tw(eng):
        if c_tap.n:
            eng.wait_ge(c_tap.sem, c_tap.n)

    x0s = ExitStack()
    x_rows0 = x0s.enter_context(nc.sbuf_tensor("x_rows0", [128, NTC * C], BF16, side="right"))
    setup = ExitStack()
    sb2 = lambda name, shape, dt: setup.enter_context(nc.sbuf_tensor(name, shape, dt))
    ps2 = lambda name, shape, dt: setup.enter_context(nc.psum_tensor(name, shape, dt))
    caw_s = sb2("caw_s", [128, NKC * 3 * C], BF16)
    qw_s = sb2("qw_s", [128, NKC * C], BF16)
    kw_s = sb2("kw_s", [128, NKC * C], BF16)
    vw_s = sb2("vw_s", [128, NKC * C], BF16)
    cab_s = sb2("cab_s", [128, 24], BF16)
    qb_s = sb2("qb_s", [128, NKC], F32)
    kb_s = sb2("kb_s", [128, NKC], F32)
    vb_s = sb2("vb_s", [128, NKC], F32)
    cawT = sb2("cawT", [128, 24 * C], BF16)
    weff_st = sb2("weff_st", [128, NKC * C], BF16)
    pt = [ps2(f"pt{i}", [128, 1024], BF16) for i in range(2)]
    pw = [ps2(f"pw{i}", [128, 512], F32) for i in range(4)]
    pb8 = ps2("pb8", [128, NKC], F32)

    def g_setup(g):
        c_ld1.inc(g.dma_start(out=id16[:], in_=id16_in))
        c_ld1.inc(g.dma_start(out=_d3(caw_s, 3 * C), in_=_t3(caw)))
        c_g.inc(g.dma_start(out=_d3(qw_s, C), in_=_t3(qw)))
        c_g.inc(g.dma_start(out=_d3(kw_s, C), in_=_t3(kw)))
        c_g.inc(g.dma_start(out=_d3(vw_s, C), in_=_t3(vw)))
        c_g.inc(g.dma_start(out=cab_s[:], in_=cab.rearrange("(j p) -> p j", p=128), allow_slow_non_contiguous=True))
        c_g.inc(g.dma_start(out=qb_s[:], in_=qb.rearrange("(j p) -> p j", p=128), allow_slow_non_contiguous=True))
        c_g.inc(g.dma_start(out=kb_s[:], in_=kb.rearrange("(j p) -> p j", p=128), allow_slow_non_contiguous=True))
        c_g.inc(g.dma_start(out=vb_s[:], in_=vb.rearrange("(j p) -> p j", p=128), allow_slow_non_contiguous=True))
        c_g.inc(g.dma_start(out=obs[:], in_=ob.rearrange("(j p) -> p j", p=128), allow_slow_non_contiguous=True))
        c_g.inc(g.dma_start(out=id32[:], in_=id32_in))
        c_g.inc(g.dma_start(out=ones[:], in_=ones_in))
        c_g.inc(g.dma_start(out=sel[:], in_=sel_in))
        c_g.inc(g.dma_start(out=ltx[:], in_=ltx_in))
        c_g.inc(g.dma_start(out=iotac[:], in_=iota_in))
        c_g.inc(g.dma_start(out=masks[:], in_=masks_in))
        c_g.inc(g.dma_start(out=trashc[:], in_=trashc_in))
        c_x.inc(g.dma_start(
            out=_d3(x_rows0, C),
            in_=_t3(x_in[0:CHUNK, :])))
        c_g2.inc(g.dma_start(out=ow_h, in_=ow))
        wgf = wg_in.rearrange("e c m -> (e c) m")
        wuf = wu_in.rearrange("e c m -> (e c) m")
        wdf = wd_in.rearrange("e m c -> (e m) c")
        for e_ in range(NE):
            c_g2.inc(g.dma_start(
                out=wg_h[128 * e_:128 * (e_ + 1), :].rearrange("p (rc c) -> p rc c", c=MFF),
                in_=wgf[e_ * C:(e_ + 1) * C, :].rearrange("(rc p) c -> p rc c", p=128)))
            c_g2.inc(g.dma_start(
                out=wu_h[128 * e_:128 * (e_ + 1), :].rearrange("p (rc c) -> p rc c", c=MFF),
                in_=wuf[e_ * C:(e_ + 1) * C, :].rearrange("(rc p) c -> p rc c", p=128)))
            c_g2.inc(g.dma_start(
                out=wd_h[128 * e_:128 * (e_ + 1), :].rearrange("p (mc c) -> p mc c", c=C),
                in_=wdf[e_ * MFF:(e_ + 1) * MFF, :].rearrange("(mc p) c -> p mc c", p=128)))
        c_g2.inc(g.dma_start(
            out=wsg_h.rearrange("p (rc c) -> p rc c", c=MFF),
            in_=wsg_in.rearrange("(rc p) c -> p rc c", p=128)))
        c_g2.inc(g.dma_start(
            out=wsu_h.rearrange("p (rc c) -> p rc c", c=MFF),
            in_=wsu_in.rearrange("(rc p) c -> p rc c", p=128)))
        c_g2.inc(g.dma_start(
            out=wsd_h.rearrange("p (mc c) -> p mc c", c=C),
            in_=wsd_in.rearrange("(mc p) c -> p mc c", p=128)))
        for i in range(NCHUNK):
            c_g2.inc(g.dma_start(out=eo_h[i][NE * ECAP:NE * ECAP + 128, :], in_=zrows_in))
            c_g2.inc(g.dma_start(out=disp_h[i], in_=dinit_in))
            c_g2.inc(g.dma_start(out=x2r_h[i][CHUNK:CHUNK + 1, :], in_=zrow_in))
    block.gpsimd(g_setup)
    n_ld0, n_cast = 15 * 16, 58 * 16

    v_t0 = c_v.n
    p_t0 = c_p.n

    def pe_cawT(t):
        t.wait_ge(c_ld1.sem, 32)
        for mc in range(24):
            if mc >= 2:
                t.wait_ge(c_v.sem, v_t0 + mc - 1)
            for cc in range(NKC):
                mm = t.transpose(pt[mc % 2][:, 128 * cc:128 * (cc + 1)],
                                 caw_s[:, cc * 3 * C + 128 * mc: cc * 3 * C + 128 * (mc + 1)],
                                 id16[:])
            c_p.inc(mm)
    block.tensor(pe_cawT)

    def dv_cawT(v):
        for mc in range(24):
            v.wait_ge(c_p.sem, p_t0 + mc + 1)
            c_v.inc(v.tensor_copy(cawT[:, mc * C:(mc + 1) * C], pt[mc % 2][:]))
    block.vector(dv_cawT)

    wsrc = {0: qw_s, 1: kw_s, 2: vw_s}
    whbm = {0: wq_h, 1: wk_h, 2: wv_h}
    bsrc = {0: qb_s, 1: kb_s, 2: vb_s}
    bdst = {0: bqe, 1: bke, 2: bve}
    a_w0 = c_a.n
    v_w0 = c_v.n
    s_w0 = c_wf.n
    p_w0 = c_p.n

    def pe_weff(t):
        t.wait_ge(c_g.sem, n_ld0)
        t.wait_ge(c_v.sem, v_t0 + 24)
        nslot = 0
        for wi in range(3):
            if wi > 0:
                t.wait_ge(c_v.sem, v_w0 + (2 if wi == 1 else 3))
                t.wait_ge(c_wf.sem, s_w0 + 16 * wi)
            for cc in range(NKC):
                for fh in range(2):
                    if nslot >= 4:
                        t.wait_ge(c_a.sem, a_w0 + nslot - 3)
                    for mc in range(NKC):
                        mcg = wi * NKC + mc
                        mm = t.matmul(pw[nslot % 4][:],
                                      cawT[:, mcg * C + 128 * cc: mcg * C + 128 * (cc + 1)],
                                      wsrc[wi][:, mc * C + 512 * fh: mc * C + 512 * (fh + 1)],
                                      start=(mc == 0), stop=(mc == NKC - 1))
                    c_p.inc(mm)
                    nslot += 1
            for fc in range(NKC):
                for mc in range(NKC):
                    mcg = wi * NKC + mc
                    mm = t.matmul(pb8[:, fc:fc + 1],
                                  wsrc[wi][:, mc * C + 128 * fc: mc * C + 128 * (fc + 1)],
                                  cab_s[:, mcg:mcg + 1],
                                  start=(mc == 0), stop=(mc == NKC - 1))
            c_p.inc(mm)
    block.tensor(pe_weff)

    def act_weff(a):
        nslot = 0
        for wi in range(3):
            for cc in range(NKC):
                for fh in range(2):
                    nslot += 1
                    a.wait_ge(c_p.sem, p_w0 + nslot + wi)
                    c_a.inc(a.activation(
                        out=weff_st[:, cc * C + 512 * fh: cc * C + 512 * (fh + 1)],
                        in_=pw[(nslot - 1) % 4][:], func=AF.Copy))
    block.scalar(act_weff)

    def dv_weff(v):
        for wi in range(3):
            v.wait_ge(c_p.sem, p_w0 + (wi + 1) * 17)
            if wi == 0:
                c_v.inc(v.tensor_scalar_mul(qb_s[:], qb_s[:], 0.125))
                v.drain()
                c_v.inc(v.scalar_tensor_tensor(
                    out=bqe[:], in0=pb8[:], scalar=0.125, in1=qb_s[:],
                    op0=OP.mult, op1=OP.add))
            else:
                c_v.inc(v.tensor_tensor(out=bdst[wi][:], in0=pb8[:], in1=bsrc[wi][:],
                                        op=OP.add))
    block.vector(dv_weff)

    def sy_weff(s):
        for wi in range(3):
            s.wait_ge(c_a.sem, a_w0 + (wi + 1) * 16)
            c_wf.inc(s.dma_start(out=whbm[wi], in_=weff_st[:]))
    block.sync(sy_weff)
    n_weff = c_wf.n

    if "t_weff" in tap_t:
        def sy_tw(s):
            s.wait_ge(c_a.sem, c_a.n)
            c_tap.inc(s.dma_start(out=tap_t["t_weff"], in_=weff_st[:]))
        block.sync(sy_tw)

    setup.close()

    gws = ExitStack()
    gw_s = gws.enter_context(nc.sbuf_tensor("gw_s", [16, C], F32))
    gwhi = gws.enter_context(nc.sbuf_tensor("gwhi", [16, C], BF16))
    gwmd = gws.enter_context(nc.sbuf_tensor("gwmd", [16, C], BF16))
    gwlo = gws.enter_context(nc.sbuf_tensor("gwlo", [16, C], BF16))
    ptg = gws.enter_context(nc.psum_tensor("ptg", [128, 384], BF16))
    c_gw = Ctr(nc, "c_gw", 16)

    def g_gw(g):
        g.wait_ge(c_wf.sem, n_weff)
        c_gw.inc(g.dma_start(out=gw_s[:], in_=gw_in))
    block.gpsimd(g_gw)

    def dv_gw(v):
        v.wait_ge(c_gw.sem, 16)
        v.wait_ge(c_a.sem, c_a.n)
        v.drain()
        v.tensor_copy(gwhi[:], gw_s[:])
        v.drain()
        v.tensor_tensor(out=gw_s[:], in0=gw_s[:], in1=gwhi[:], op=OP.subtract)
        v.drain()
        v.tensor_copy(gwmd[:], gw_s[:])
        v.drain()
        v.tensor_tensor(out=gw_s[:], in0=gw_s[:], in1=gwmd[:], op=OP.subtract)
        v.drain()
        c_v.inc(v.tensor_copy(gwlo[:], gw_s[:]))
    block.vector(dv_gw)
    v_gw = c_v.n

    def pe_gw(t):
        t.wait_ge(c_v.sem, v_gw)
        for hl, gsrc in enumerate((gwhi, gwmd, gwlo)):
            for cc in range(NKC):
                mm = t.transpose(ptg[:, 128 * hl + 16 * cc:128 * hl + 16 * (cc + 1)],
                                 gsrc[0:16, 128 * cc:128 * (cc + 1)], id16[0:16, 0:16])
        c_p.inc(mm)
    block.tensor(pe_gw)
    p_gw = c_p.n

    def dv_gwT(v):
        v.wait_ge(c_p.sem, p_gw)
        v.drain()
        last = None
        for hl in range(3):
            last = v.tensor_copy(
                gwT[:].rearrange("p (cc th) -> p cc th", th=48)[:, :, 16 * hl:16 * (hl + 1)],
                ptg[:, 128 * hl:128 * (hl + 1)].rearrange("p (cc s) -> p cc s", s=16))
        c_v.inc(last)
    block.vector(dv_gwT)
    n_gwT = c_v.n
    gws.close()

    prev_chunk_done = dict(v=c_v.n, o0=c_o[0].n, o1=c_o[1].n, p=c_p.n, a=c_a.n,
                           shw0=0, shw1=0, ds=0, x2r=0)
    ch_marks = []

    for ch in range(NCHUNK):
        chs = ExitStack()
        sbc = lambda name, shape, dt: chs.enter_context(
            nc.sbuf_tensor(f"{name}{ch}", shape, dt, side="right"))
        psc = lambda st_, name, shape, dt: st_.enter_context(nc.psum_tensor(f"{name}{ch}", shape, dt))

        xys = ExitStack()
        xT = xys.enter_context(nc.sbuf_tensor(f"xT{ch}", [128, NKC * CHUNK], BF16))
        Y = xys.enter_context(nc.sbuf_tensor(f"Y{ch}", [128, NKC * CHUNK], BF16))

        p1s = ExitStack()
        if ch == 0:
            x_rows = x_rows0
        else:
            x_rows = p1s.enter_context(
                nc.sbuf_tensor(f"x_rows{ch}", [128, NTC * C], BF16, side="right"))
        px = [psc(p1s, f"px{i}", [128, 2048], BF16) for i in range(2)]

        pcd = prev_chunk_done

        if ch > 0:
            def g_p1(g):
                g.wait_ge(c_v.sem, pcd["v"])
                g.wait_ge(c_shw[0].sem, pcd["shw0"])
                g.wait_ge(c_shw[1].sem, pcd["shw1"])
                g.wait_ge(c_ds.sem, pcd["ds"])
                g.wait_ge(c_x2r.sem, pcd["x2r"])
                c_x.inc(g.dma_start(
                    out=_d3(x_rows, C),
                    in_=_t3(x_in[ch * CHUNK:(ch + 1) * CHUNK, :])))
            block.gpsimd(g_p1)
        n_x = c_x.n

        p_b1 = c_p.n
        v_b1 = c_v.n

        def pe_p1(t):
            tw(t)
            t.wait_ge(c_x.sem, n_x)
            t.wait_ge(c_v.sem, pcd["v"])
            t.wait_ge(c_a.sem, pcd["a"])
            for cc in range(NKC):
                if cc >= 2:
                    t.wait_ge(c_v.sem, v_b1 + cc - 1)
                for tc in range(NTC):
                    mm = t.transpose(px[cc % 2][:, 128 * tc:128 * (tc + 1)],
                                     x_rows[:, tc * C + 128 * cc: tc * C + 128 * (cc + 1)],
                                     id16[:])
                c_p.inc(mm)
        block.tensor(pe_p1)

        def dv_p1(v):
            for cc in range(NKC):
                v.wait_ge(c_p.sem, p_b1 + cc + 1)
                c_v.inc(v.tensor_copy(xT[:, cc * CHUNK:(cc + 1) * CHUNK], px[cc % 2][:]))
        block.vector(dv_p1)
        n_xT = c_v.n
        p1s.close()
        if ch == 0:
            x0s.close()

        if "t_xT" in tap_t and ch == 0:
            def sy_t1(s):
                s.wait_ge(c_v.sem, n_xT)
                c_tap.inc(s.dma_start(out=tap_t["t_xT"], in_=xT[:]))
            block.sync(sy_t1)

        att = ExitStack()
        sba = lambda name, shape, dt: att.enter_context(nc.sbuf_tensor(f"{name}{ch}", shape, dt))
        vpl = sba("vpl", [128, NTC * C], BF16)
        vsr = sba("vsr", [128, NTC * C], BF16)
        y_pair = sba("y_pair", [128, CHUNK], F32)
        s_tot = sba("s_tot", [128, CHUNK], F32)
        p_bufs2 = [[sba(f"p_buf{up}_{hh}", [128, 1536], BF16) for hh in range(2)]
                   for up in range(2)]

        p2s = ExitStack()
        wvb = p2s.enter_context(nc.sbuf_tensor(f"wvb{ch}", [128, NKC * C], BF16))
        pv = [psc(p2s, f"pv{i}", [128, 512], F32) for i in range(4)]

        def sy_p2(s):
            s.wait_ge(c_wf.sem, n_weff)
            s.wait_ge(c_v.sem, n_xT)
            s.wait_ge(c_v.sem, pcd["v"])
            s.wait_ge(c_shw[0].sem, pcd["shw0"])
            s.wait_ge(c_shw[1].sem, pcd["shw1"])
            c_wv.inc(s.dma_start(out=wvb[:], in_=wv_h))
        block.sync(sy_p2)
        n_wv = c_wv.n

        p_b2 = c_p.n
        v_b2 = c_v.n

        def pe_p2(t):
            tw(t)
            t.wait_ge(c_wv.sem, n_wv)
            t.wait_ge(c_v.sem, n_xT)
            nslot = 0
            for form in range(2):
                for tc in range(NTC):
                    for fh in range(2):
                        if nslot >= 4:
                            t.wait_ge(c_v.sem, v_b2 + nslot - 3)
                        for cc in range(NKC):
                            if form == 0:
                                st_ap = xT[:, cc * CHUNK + 128 * tc: cc * CHUNK + 128 * (tc + 1)]
                            else:
                                cls, sblk = tc // 4, tc % 4
                                b0 = cc * CHUNK + cls + 512 * sblk
                                st_ap = xT[:, b0: b0 + 509: 4]
                            mm = t.matmul(pv[nslot % 4][:], st_ap,
                                          wvb[:, cc * C + 512 * fh: cc * C + 512 * (fh + 1)],
                                          start=(cc == 0), stop=(cc == NKC - 1))
                        c_p.inc(mm)
                        nslot += 1
        block.tensor(pe_p2)

        def dv_p2(v):
            nslot = 0
            for form in range(2):
                dst = vpl if form == 0 else vsr
                for tc in range(NTC):
                    for fh in range(2):
                        v.wait_ge(c_p.sem, p_b2 + nslot + 1)
                        c_v.inc(v.tensor_copy(
                            dst[:, tc * C + 512 * fh: tc * C + 512 * (fh + 1)],
                            pv[nslot % 4][:]))
                        nslot += 1
        block.vector(dv_p2)
        n_v = c_v.n
        p2s.close()

        if ch == 0:
            for nm, buf in (("t_vpl", vpl), ("t_vsr", vsr)):
                if nm in tap_t:
                    def sy_t2(s, nm=nm, buf=buf):
                        s.wait_ge(c_v.sem, n_v)
                        c_tap.inc(s.dma_start(out=tap_t[nm], in_=buf[:]))
                    block.sync(sy_t2)

        q_bufs = [sba(f"q_buf{i}", [128, CHUNK], BF16) for i in range(2)]
        k_bufs = [sba(f"k_buf{i}", [128, CHUNK], BF16) for i in range(2)]
        wqkb2 = [sba(f"wqkb{i}", [128, 2 * NKC * 128], BF16) for i in range(2)]
        p3ps = ExitStack()
        pqk = [psc(p3ps, f"pqk{i}", [128, 512], F32) for i in range(2)]
        SAb = [psc(p3ps, f"SA{hh}", [128, 1024], F32) for hh in range(2)]
        ob = psc(p3ps, "ob", [128, 512], F32)
        sb_ = psc(p3ps, "sb", [128, 512], F32)

        qk_pe = {}
        qk_dv = {}
        wq_marks = {}

        def sy_qk(s, pair):
            if pair == 0:
                s.wait_ge(c_wf.sem, n_weff)
                s.wait_ge(c_p.sem, p_b2 + 64)
            if pair >= 2:
                s.wait_ge(c_p.sem, qk_pe[(pair - 2, 7)])
            cw = c_wq[pair % 2]
            cw.inc(s.dma_start(
                out=wqkb2[pair % 2][:, 0:NKC * 128].rearrange("p (cc f) -> p cc f", f=128),
                in_=wq_h.rearrange("p (cc f) -> p cc f", f=C)[:, :, 128 * pair:128 * (pair + 1)]))
            cw.inc(s.dma_start(
                out=wqkb2[pair % 2][:, NKC * 128:].rearrange("p (cc f) -> p cc f", f=128),
                in_=wk_h.rearrange("p (cc f) -> p cc f", f=C)[:, :, 128 * pair:128 * (pair + 1)]))
            wq_marks[pair] = (pair % 2, cw.n)

        def pe_qk_slot(t, pair, slot):
            t.wait_ge(c_wq[wq_marks[pair][0]].sem, wq_marks[pair][1])
            if (pair, slot - 2) in qk_dv:
                t.wait_ge(c_a.sem, qk_dv[(pair, slot - 2)])
            elif slot < 2 and (pair - 1, 6 + slot) in qk_dv:
                t.wait_ge(c_a.sem, qk_dv[(pair - 1, 6 + slot)])
            which, tt = slot // 4, slot % 4
            for cc in range(NKC):
                mm = t.matmul(
                    pqk[slot % 2][:],
                    wqkb2[pair % 2][:, which * NKC * 128 + 128 * cc:
                                    which * NKC * 128 + 128 * (cc + 1)],
                    xT[:, cc * CHUNK + 512 * tt: cc * CHUNK + 512 * (tt + 1)],
                    start=(cc == 0), stop=(cc == NKC - 1))
            n_ = c_p.inc(mm)
            if (pair, slot) in qk_pe:
                assert n_ == qk_pe[(pair, slot)], (pair, slot, n_, qk_pe[(pair, slot)])
            else:
                qk_pe[(pair, slot)] = n_

        def act_qk_slot(a, pair, slot):
            a.wait_ge(c_p.sem, qk_pe[(pair, slot)])
            which, tt = slot // 4, slot % 4
            if which == 0:
                op = a.activation(
                    out=q_bufs[pair % 2][:, 512 * tt:512 * (tt + 1)],
                    in_=pqk[slot % 2][:], func=AF.Identity, scale=0.125,
                    bias=bqe[:, pair:pair + 1])
            else:
                op = a.activation(
                    out=k_bufs[pair % 2][:, 512 * tt:512 * (tt + 1)],
                    in_=pqk[slot % 2][:], func=AF.Identity,
                    bias=bke[:, pair:pair + 1])
            n_ = c_a.inc(op)
            if (pair, slot) in qk_dv:
                assert n_ == qk_dv[(pair, slot)], (pair, slot, n_, qk_dv[(pair, slot)])
            else:
                qk_dv[(pair, slot)] = n_

        def sy_qk0(s):
            sy_qk(s, 0)
        block.sync(sy_qk0)

        for s_ in range(8):
            qk_dv[(0, s_)] = c_a.n + s_ + 1

        def pe_qk0(t):
            t.wait_ge(c_v.sem, n_v)
            for slot in range(8):
                pe_qk_slot(t, 0, slot)
        block.tensor(pe_qk0)

        def act_qk0(a):
            for slot in range(8):
                act_qk_slot(a, 0, slot)
        block.scalar(act_qk0)

        a_marks = {}
        v_marks = {}
        pe_marks = {}
        n_Y_v = None

        nA = c_a.n
        nV = c_v.n
        nP = c_p.n
        for pair in range(NPAIR):
            has_next = pair + 1 < NPAIR
            for u in range(7):
                for hh in range(2):
                    for h in range(2):
                        nA += 1
                        a_marks[(pair, u, hh, h)] = nA
                if has_next and u < 7:
                    nA += 1
                    qk_dv[(pair + 1, u)] = nA
            if has_next:
                nA += 1
                qk_dv[(pair + 1, 7)] = nA
            for u in range(7):
                nV += 1
                v_marks[(pair, "m", u)] = nV
                nV += 1
                v_marks[(pair, "y", u)] = nV
            nV += 2
            v_marks[(pair, "Y")] = nV
            for u in range(7):
                if u >= 1:
                    nP += 1
                    pe_marks[(pair, "os", u - 1)] = nP
                for hh in range(2):
                    for h in range(2):
                        nP += 1
                        pe_marks[(pair, u, hh, h)] = nP
                if has_next and u < 7:
                    nP += 1
                    qk_pe[(pair + 1, u)] = nP
            nP += 1
            pe_marks[(pair, "os", 6)] = nP
            if has_next:
                nP += 1
                qk_pe[(pair + 1, 7)] = nP

        for pair in range(NPAIR):
            if pair + 1 < NPAIR:
                def sy_qkn(s, pair=pair):
                    sy_qk(s, pair + 1)
                block.sync(sy_qkn)

            scheds = []
            for (br, seg) in unit_list():
                scheds.append((br, seg,
                               [unit_rects(br, seg, 2 * pair + hh) for hh in range(2)]))

            def pe_pair(t, pair=pair, scheds=scheds):
                if pair == 0:
                    t.wait_ge(c_v.sem, n_v)
                t.wait_ge(c_a.sem, qk_dv[(pair, 7)])
                for u, (br, seg, rr) in enumerate(scheds):
                    if u >= 1:
                        emit_os(t, pair, u - 1, scheds[u - 1])
                    for hh in range(2):
                        rb = 64 * hh
                        for h in range(2):
                            if h == 0:
                                key = (pair, u - 1, hh, 1) if u > 0 else (pair - 1, 6, hh, 1)
                                if key in a_marks:
                                    t.wait_ge(c_a.sem, a_marks[key])
                            else:
                                t.wait_ge(c_a.sem, a_marks[(pair, u, hh, 0)])
                            rects = [r for r in rr[hh] if r["h"] == h]
                            base_slot = 0 if h == 0 else 2
                            for k_, r in enumerate(rects):
                                slot = base_slot + k_
                                kst, kstep = r["k"]
                                qst, qstep = r["q"]
                                sa_col = 256 * (slot - base_slot) if h == 1 else 256 * slot
                                mm = t.matmul(
                                    SAb[hh][:, sa_col:sa_col + 256],
                                    k_bufs[pair % 2][rb:rb + 64, kst: kst + 127 * kstep + 1: kstep],
                                    q_bufs[pair % 2][rb:rb + 64, qst: qst + 255 * qstep + 1: qstep],
                                    start=True, stop=True, tile_position=(rb, 0))
                            assert c_p.inc(mm) == pe_marks[(pair, u, hh, h)]
                    if pair + 1 < NPAIR and u < 7:
                        pe_qk_slot(t, pair + 1, u)
                emit_os(t, pair, 6, scheds[6])
                if pair + 1 < NPAIR:
                    pe_qk_slot(t, pair + 1, 7)

            def emit_os(t, pair, u, sched):
                br, seg, rr = sched
                p_bufs = p_bufs2[(7 * pair + u) % 2]
                t.wait_ge(c_v.sem, v_marks[(pair, "m", u)])
                key = (pair, "y", u - 1) if u > 0 else (pair - 1, "y", 6)
                if key in v_marks:
                    t.wait_ge(c_v.sem, v_marks[key])
                for hh in range(2):
                    n_r = len(rr[hh])
                    tp = (0, 64 * hh)
                    for slot, r in enumerate(rr[hh]):
                        vt, srt = r["v"]
                        vb_ap = (vsr if srt else vpl)[:, vt * C + 128 * pair + 64 * hh:
                                                      vt * C + 128 * pair + 64 * (hh + 1)]
                        st = r["start"]
                        mv = p_bufs[hh][:, 256 * slot + st:256 * (slot + 1)]
                        t.matmul(ob[64 * hh:64 * hh + 64, 256 * r["h"] + st:256 * (r["h"] + 1)],
                                 vb_ap, mv, start=(slot == 0), stop=(slot == n_r - 1),
                                 tile_position=tp, skip_group_check=True)
                    for slot, r in enumerate(rr[hh]):
                        st = r["start"]
                        mv = p_bufs[hh][:, 256 * slot + st:256 * (slot + 1)]
                        mm = t.matmul(sb_[64 * hh:64 * hh + 64, 256 * r["h"] + st:256 * (r["h"] + 1)],
                                      ones[:, 0:64], mv,
                                      start=(slot == 0), stop=(slot == n_r - 1),
                                      tile_position=tp, skip_group_check=True)
                assert c_p.inc(mm) == pe_marks[(pair, "os", u)]

            def act_pair(a, pair=pair):
                for u in range(7):
                    g_ = 7 * pair + u
                    p_bufs = p_bufs2[g_ % 2]
                    pg2 = g_ - 2
                    okey = None
                    if pg2 >= 0:
                        okey = (pg2 // 7, "os", pg2 % 7)
                    for hh in range(2):
                        a.wait_ge(c_p.sem, pe_marks[(pair, u, hh, 0)])
                        if okey is not None and okey in pe_marks:
                            a.wait_ge(c_p.sem, pe_marks[okey])
                        c_a.inc(a.activation(out=p_bufs[hh][:, 0:512],
                                             in_=SAb[hh][:, 0:512], func=AF.Exp))
                        assert c_a.n == a_marks[(pair, u, hh, 0)]
                        a.wait_ge(c_p.sem, pe_marks[(pair, u, hh, 1)])
                        c_a.inc(a.activation(out=p_bufs[hh][:, 512:1536],
                                             in_=SAb[hh][:, 0:1024], func=AF.Exp))
                        assert c_a.n == a_marks[(pair, u, hh, 1)]
                    if pair + 1 < NPAIR and u < 7:
                        act_qk_slot(a, pair + 1, u)
                if pair + 1 < NPAIR:
                    act_qk_slot(a, pair + 1, 7)

            def dv_pair(v, pair=pair, scheds=scheds):
                for u, (br, seg, rr) in enumerate(scheds):
                    last = None
                    p_bufs = p_bufs2[(7 * pair + u) % 2]
                    for hh in range(2):
                        v.wait_ge(c_a.sem, a_marks[(pair, u, hh, 1)])
                        for slot, r in enumerate(rr[hh]):
                            for (mid, loc) in r["masks"]:
                                last = v.tensor_tensor(
                                    out=p_bufs[hh][:, 256 * slot + loc:256 * slot + loc + 128],
                                    in0=p_bufs[hh][:, 256 * slot + loc:256 * slot + loc + 128],
                                    in1=masks[:, 128 * mid:128 * (mid + 1)], op=OP.mult)
                    assert c_v.inc(last) == v_marks[(pair, "m", u)]
                    v.wait_ge(c_p.sem, pe_marks[(pair, "os", u)])
                    v.drain()
                    hd0 = 2 * pair
                    for hh in range(2):
                        yst, ystep = unit_ycols(br, seg, hd0 + hh)
                        rb = 64 * hh
                        ydst = y_pair[rb:rb + 64, yst: yst + 511 * ystep + 1: ystep]
                        sdst = s_tot[rb:rb + 64, yst: yst + 511 * ystep + 1: ystep]
                        if br == "b1":
                            v.tensor_copy(ydst, ob[rb:rb + 64, :])
                            last = v.tensor_copy(sdst, sb_[rb:rb + 64, :])
                        else:
                            v.tensor_tensor(out=ydst, in0=ydst, in1=ob[rb:rb + 64, :], op=OP.add)
                            last = v.tensor_tensor(out=sdst, in0=sdst,
                                                   in1=sb_[rb:rb + 64, :], op=OP.add)
                    assert c_v.inc(last) == v_marks[(pair, "y", u)]
                v.drain()
                c_v.inc(v.reciprocal(s_tot[:], s_tot[:]))
                v.drain()
                v.tensor_tensor(out=y_pair[:], in0=y_pair[:], in1=s_tot[:], op=OP.mult)
                v.drain()
                c_v.inc(v.tensor_scalar(
                    out=Y[:, pair * CHUNK:(pair + 1) * CHUNK], in0=y_pair[:],
                    scalar1=bve[:, pair:pair + 1], scalar2=None, op0=OP.add))
                assert c_v.n == v_marks[(pair, "Y")]

            block.tensor(pe_pair)
            block.scalar(act_pair)
            block.vector(dv_pair)

            if ch == 0 and pair == 0:
                nyv = c_v.n
                for nm, buf in (("t_q", q_bufs[0]), ("t_k", k_bufs[0]),
                                ("t_y", y_pair), ("t_s", s_tot)):
                    if nm in tap_t:
                        def sy_t4(s, nm=nm, buf=buf, nyv=nyv):
                            s.wait_ge(c_v.sem, nyv)
                            c_tap.inc(s.dma_start(out=tap_t[nm], in_=buf[:]))
                        block.sync(sy_t4)

        p3ps.close()
        n_Y_v = c_v.n
        n_att_v = c_v.n
        att.close()

        if "t_Y" in tap_t and ch == 0:
            def sy_t5(s):
                s.wait_ge(c_v.sem, n_Y_v)
                c_tap.inc(s.dma_start(out=tap_t["t_Y"], in_=Y[:]))
            block.sync(sy_t5)

        x2 = sbc("x2", [128, NKC * CHUNK], BF16)
        meanb = sbc("meanb", [128, CHUNK], F32)
        rstd = sbc("rstd", [128, CHUNK], F32)
        sq2 = sbc("sq2", [128, 2 * 512], BF16)
        r1s = ExitStack()
        r1b = r1s.enter_context(nc.sbuf_tensor(f"r1b{ch}", [128, NKC * CHUNK], BF16))

        p4s = ExitStack()
        owb = p4s.enter_context(nc.sbuf_tensor(f"owb{ch}", [128, NKC * C], BF16))
        po4 = [psc(p4s, f"po4{i}", [128, 512], F32) for i in range(4)]

        def sy_p4(s):
            s.wait_ge(c_g2.sem, n_cast)
            s.wait_ge(c_v.sem, n_Y_v)
            c_ow.inc(s.dma_start(out=_d3(owb, C), in_=_t3(ow_h)))
        block.sync(sy_p4)
        n_ow = c_ow.n

        p_b4 = c_p.n
        v_b4 = c_v.n

        def pe_p4(t):
            tw(t)
            t.wait_ge(c_ow.sem, n_ow)
            t.wait_ge(c_v.sem, n_Y_v)
            nslot = 0
            for fc in range(NKC):
                for tt in range(4):
                    if nslot >= 4:
                        t.wait_ge(c_v.sem, v_b4 + nslot - 3)
                    for cc in range(NKC):
                        mm = t.matmul(po4[nslot % 4][:],
                                      owb[:, cc * C + 128 * fc: cc * C + 128 * (fc + 1)],
                                      Y[:, cc * CHUNK + 512 * tt: cc * CHUNK + 512 * (tt + 1)],
                                      start=(cc == 0), stop=(cc == NKC - 1))
                    c_p.inc(mm)
                    nslot += 1
        block.tensor(pe_p4)

        def dv_p4(v):
            nslot = 0
            for fc in range(NKC):
                for tt in range(4):
                    v.wait_ge(c_p.sem, p_b4 + nslot + 1)
                    c_v.inc(v.scalar_tensor_tensor(
                        out=r1b[:, fc * CHUNK + 512 * tt: fc * CHUNK + 512 * (tt + 1)],
                        in0=po4[nslot % 4][:], scalar=obs[:, fc:fc + 1],
                        in1=xT[:, fc * CHUNK + 512 * tt: fc * CHUNK + 512 * (tt + 1)],
                        op0=OP.add, op1=OP.add))
                    nslot += 1
        block.vector(dv_p4)
        p4s.close()
        if "t_r1" in tap_t and ch == 0:
            def sy_t6(s, n=c_v.n):
                s.wait_ge(c_v.sem, n)
                c_tap.inc(s.dma_start(out=tap_t["t_r1"], in_=r1b[:]))
            block.sync(sy_t6)

        _ln_id = [0]

        def ln_emit(srcb, dst):
            lns = ExitStack()
            _ln_id[0] += 1
            pm = [psc(lns, f"pm{_ln_id[0]}_{i}", [128, 512], F32) for i in range(4)]
            v_r = c_v.n
            p_l0 = c_p.n

            def pe_ln(t):
                t.wait_ge(c_v.sem, v_r)
                for tt in range(4):
                    for cc in range(NKC):
                        mm = t.matmul(pm[tt][:], ones[:],
                                      srcb[:, cc * CHUNK + 512 * tt: cc * CHUNK + 512 * (tt + 1)],
                                      start=(cc == 0), stop=(cc == NKC - 1))
                    c_p.inc(mm)
            block.tensor(pe_ln)

            def act_ln(a):
                for tt in range(4):
                    a.wait_ge(c_p.sem, p_l0 + tt + 1)
                    c_a.inc(a.activation(out=meanb[:, 512 * tt:512 * (tt + 1)],
                                         in_=pm[tt][:], func=AF.Copy, scale=1.0 / C))
            block.scalar(act_ln)
            a_mean = c_a.n

            v_c0 = c_v.n

            def dv_lnc(v):
                v.wait_ge(c_a.sem, a_mean)
                v.drain()
                last = None
                for cc in range(NKC):
                    last = v.tensor_tensor(out=srcb[:, cc * CHUNK:(cc + 1) * CHUNK],
                                           in0=srcb[:, cc * CHUNK:(cc + 1) * CHUNK],
                                           in1=meanb[:], op=OP.subtract)
                c_v.inc(last)
            block.vector(dv_lnc)
            v_cent = c_v.n

            p_l1 = c_p.n
            a_l1 = c_a.n

            def act_lnsq(a):
                a.wait_ge(c_v.sem, v_cent)
                k = 0
                for tt in range(4):
                    for cc in range(NKC):
                        if k >= 2:
                            a.wait_ge(c_p.sem, p_l1 + k - 1)
                        sl = srcb[:, cc * CHUNK + 512 * tt: cc * CHUNK + 512 * (tt + 1)]
                        c_a.inc(a.activation(
                            out=sq2[:, 512 * (k % 2):512 * (k % 2 + 1)],
                            in_=sl, func=AF.Square))
                        k += 1
            block.scalar(act_lnsq)

            def pe_ln2(t):
                k = 0
                for tt in range(4):
                    for cc in range(NKC):
                        t.wait_ge(c_a.sem, a_l1 + k + 1)
                        mm = t.matmul(pm[tt][:], ones[:],
                                      sq2[:, 512 * (k % 2):512 * (k % 2 + 1)],
                                      start=(cc == 0), stop=(cc == NKC - 1))
                        c_p.inc(mm)
                        k += 1
            block.tensor(pe_ln2)

            def act_ln2(a):
                for tt in range(4):
                    a.wait_ge(c_p.sem, p_l1 + (tt + 1) * NKC)
                    c_a.inc(a.activation(out=rstd[:, 512 * tt:512 * (tt + 1)],
                                         in_=pm[tt][:], func=AF.Copy, scale=1.0 / C,
                                         bias=EPS))
            block.scalar(act_ln2)
            a_var = c_a.n

            def dv_lnr(v):
                v.wait_ge(c_a.sem, a_var)
                c_v.inc(v.reciprocal(rstd[:], rstd[:]))
            block.vector(dv_lnr)
            v_rcp = c_v.n

            def act_ln3(a):
                a.wait_ge(c_v.sem, v_rcp)
                c_a.inc(a.activation(out=rstd[:], in_=rstd[:], func=AF.Sqrt))
            block.scalar(act_ln3)
            a_rstd = c_a.n

            def dv_ln3(v):
                v.wait_ge(c_a.sem, a_rstd)
                v.drain()
                last = None
                for cc in range(NKC):
                    last = v.tensor_tensor(out=dst[:, cc * CHUNK:(cc + 1) * CHUNK],
                                           in0=srcb[:, cc * CHUNK:(cc + 1) * CHUNK],
                                           in1=rstd[:], op=OP.mult)
                c_v.inc(last)
            block.vector(dv_ln3)
            lns.close()

        ln_emit(r1b, x2)
        r1s.close()
        xys.close()
        n_x2 = c_v.n
        if "t_x2" in tap_t and ch == 0:
            def sy_t7(s):
                s.wait_ge(c_v.sem, n_x2)
                c_tap.inc(s.dma_start(out=tap_t["t_x2"], in_=x2[:]))
            block.sync(sy_t7)

        seq1 = sbc("seq1", [128, NTC * 16], BF16)
        seq2 = sbc("seq2", [128, NTC * 16], BF16)
        sw1 = sbc("sw1", [128, NTC], F32)
        sw2 = sbc("sw2", [128, NTC], F32)
        x2rows = sbc("x2rows", [128, NTC * C], BF16)

        p5a = ExitStack()
        px2 = [psc(p5a, f"px2_{i}", [128, 1024], BF16) for i in range(2)]
        p_b5a = c_p.n
        a_b5a = c_a.n

        def pe_x2r(t):
            t.wait_ge(c_v.sem, n_x2)
            for tc in range(NTC):
                if tc >= 2:
                    t.wait_ge(c_a.sem, a_b5a + tc - 1)
                for cc in range(NKC):
                    mm = t.transpose(px2[tc % 2][:, 128 * cc:128 * (cc + 1)],
                                     x2[:, cc * CHUNK + 128 * tc: cc * CHUNK + 128 * (tc + 1)],
                                     id16[:])
                c_p.inc(mm)
        block.tensor(pe_x2r)

        def act_x2r(a):
            for tc in range(NTC):
                a.wait_ge(c_p.sem, p_b5a + tc + 1)
                c_a.inc(a.activation(out=x2rows[:, tc * C:(tc + 1) * C],
                                     in_=px2[tc % 2][:], func=AF.Identity))
        block.scalar(act_x2r)
        n_x2r_a = c_a.n

        def sy_x2r(s):
            if ch == 0:
                s.wait_ge(c_g2.sem, n_cast)
            for tc in range(NTC):
                s.wait_ge(c_a.sem, a_b5a + tc + 1)
                c_x2r.inc(s.dma_start(out=x2r_h[ch][128 * tc:128 * (tc + 1), :],
                                      in_=x2rows[:, tc * C:(tc + 1) * C]))
        block.sync(sy_x2r)
        n_x2rw = c_x2r.n
        p5a.close()

        p5s = ExitStack()
        sc_rows = sbc("sc_rows", [128, NTC * 16], F32)
        sc48 = sbc("sc48", [128, NTC * 48], F32)
        rs2 = sbc("rs2", [128, 16], F32)
        rm1 = sbc("rm1", [128, 16], F32)
        rm2 = sbc("rm2", [128, 16], F32)
        rwt = sbc("rwt", [128, 16], F32)
        sg1 = sbc("sg1", [128, 16], F32)
        sg2 = sbc("sg2", [128, 16], F32)
        plg = psc(p5s, "plg", [128, NTC * 64], F32)

        def pe_p5(t):
            tw(t)
            t.wait_ge(c_a.sem, n_x2r_a)
            t.wait_ge(c_v.sem, n_gwT)
            for tc in range(NTC):
                for cc in range(NKC):
                    mm = t.matmul(plg[:, 64 * tc:64 * tc + 48],
                                  x2[:, cc * CHUNK + 128 * tc: cc * CHUNK + 128 * (tc + 1)],
                                  gwT[:, 48 * cc:48 * (cc + 1)],
                                  start=(cc == 0), stop=(cc == NKC - 1))
                c_p.inc(mm)
        block.tensor(pe_p5)
        p_lg = c_p.n

        def act_p5(a):
            a.wait_ge(c_p.sem, p_lg)
            c_a.inc(a.activation(
                out=sc48[:].rearrange("p (tc s) -> p tc s", s=48),
                in_=plg[:].rearrange("p (tc s) -> p tc s", s=64)[:, :, 0:48],
                func=AF.Identity))
        block.scalar(act_p5)
        a_sc = c_a.n

        def dv_p5(v):
            v.wait_ge(c_a.sem, a_sc)
            for tc in range(NTC):
                sc = sc_rows[:, 16 * tc:16 * (tc + 1)]
                v.drain()
                v.tensor_tensor(out=sc, in0=sc48[:, 48 * tc:48 * tc + 16],
                                in1=sc48[:, 48 * tc + 16:48 * tc + 32], op=OP.add)
                v.drain()
                v.tensor_tensor(out=sc, in0=sc,
                                in1=sc48[:, 48 * tc + 32:48 * tc + 48], op=OP.add)
                v.drain()
                v.reduce_max(rm1[:, tc:tc + 1], sc, axis=mybir.AxisListType.X)
                v.drain()
                v.tensor_scalar(out=seq1[:, 16 * tc:16 * (tc + 1)], in0=sc,
                                scalar1=rm1[:, tc:tc + 1],
                                scalar2=None, op0=OP.is_equal)
                v.drain()
                v.scalar_tensor_tensor(out=rs2[:], in0=seq1[:, 16 * tc:16 * (tc + 1)],
                                       scalar=-1e9, in1=sc, op0=OP.mult, op1=OP.add)
                v.drain()
                v.reduce_max(rm2[:, tc:tc + 1], rs2[:], axis=mybir.AxisListType.X)
                v.drain()
                v.tensor_scalar(out=seq2[:, 16 * tc:16 * (tc + 1)], in0=rs2[:],
                                scalar1=rm2[:, tc:tc + 1],
                                scalar2=None, op0=OP.is_equal)
            v.drain()
            c_v.inc(v.tensor_copy(rwt[:], rm2[:]))
        block.vector(dv_p5)
        v_max5 = c_v.n

        def act_p5b(a):
            a.wait_ge(c_v.sem, v_max5)
            a.activation(out=sg1[:], in_=rm1[:], func=AF.Sigmoid)
            c_a.inc(a.activation(out=sg2[:], in_=rm2[:], func=AF.Sigmoid))
        block.scalar(act_p5b)
        a_sig5 = c_a.n

        def dv_p5c(v):
            v.wait_ge(c_a.sem, a_sig5)
            v.drain()
            v.tensor_tensor(out=rwt[:], in0=sg1[:], in1=sg2[:], op=OP.add)
            v.drain()
            v.tensor_scalar_add(rwt[:], rwt[:], 1e-20)
            v.drain()
            v.reciprocal(rwt[:], rwt[:])
            v.drain()
            v.tensor_scalar_mul(rwt[:], rwt[:], 2.5)
            v.drain()
            v.tensor_tensor(out=sw1[:], in0=sg1[:], in1=rwt[:], op=OP.mult)
            last = v.tensor_tensor(out=sw2[:], in0=sg2[:], in1=rwt[:], op=OP.mult)
            c_v.inc(last)
        block.vector(dv_p5c)
        n_route = c_v.n
        p5s.close()

        Dk = [sbc(f"Dk{k}", [128, NTC], F32) for k in range(2)]
        Dki = Dki2[ch]
        wbuf = [sbc(f"wbuf{k}", [128, 2 * NTC], F32) for k in range(2)]
        cpre = sbc("cpre", [128, 16], F32)
        cpre2 = sbc("cpre2", [128, 16], F32)
        rnk = sbc("rnk", [128, 16], F32)
        tot1 = sbc("tot1", [128, 16], F32)
        p5b = ExitStack()
        pdp = [psc(p5b, f"pdp{i}", [128, 256], F32) for i in range(2)]

        p_b5b = c_p.n
        v_b5b = c_v.n

        def pe_p5b(t):
            t.wait_ge(c_v.sem, n_route)
            for k in range(2):
                if k == 1:
                    t.wait_ge(c_v.sem, v_b5b + 1)
                sq = seq1 if k == 0 else seq2
                for e in range(NE):
                    mm = t.matmul(pdp[0][:, 16 * e:16 * (e + 1)], ltx[:],
                                  sq[:, e: e + 16 * (NTC - 1) + 1: 16],
                                  start=True, stop=True)
                    mm = t.matmul(pdp[1][:, 16 * e:16 * (e + 1)], ones[:],
                                  sq[:, e: e + 16 * (NTC - 1) + 1: 16],
                                  start=True, stop=True)
                c_p.inc(mm)
        block.tensor(pe_p5b)

        def dv_p5b(v):
            for k in range(2):
                v.wait_ge(c_p.sem, p_b5b + k + 1)
                v.memset(Dk[k][:], 0.0)
                for e in range(NE):
                    pp = pdp[0][:, 16 * e:16 * (e + 1)]
                    cs = pdp[1][:, 16 * e:16 * (e + 1)]
                    v.drain()
                    v.memset(cpre[:, 0:1], 0.0)
                    v.tensor_copy(cpre[:, 1:16], cs[:, 0:15])
                    v.drain()
                    v.tensor_copy(cpre2[:, 0:1], cpre[:, 0:1])
                    v.tensor_tensor(out=cpre2[:, 1:16], in0=cpre[:, 1:16],
                                    in1=cpre[:, 0:15], op=OP.add)
                    v.drain()
                    v.tensor_copy(cpre[:, 0:2], cpre2[:, 0:2])
                    v.tensor_tensor(out=cpre[:, 2:16], in0=cpre2[:, 2:16],
                                    in1=cpre2[:, 0:14], op=OP.add)
                    v.drain()
                    v.tensor_copy(cpre2[:, 0:4], cpre[:, 0:4])
                    v.tensor_tensor(out=cpre2[:, 4:16], in0=cpre[:, 4:16],
                                    in1=cpre[:, 0:12], op=OP.add)
                    v.drain()
                    v.tensor_copy(cpre[:, 0:8], cpre2[:, 0:8])
                    v.tensor_tensor(out=cpre[:, 8:16], in0=cpre2[:, 8:16],
                                    in1=cpre2[:, 0:8], op=OP.add)
                    v.drain()
                    if k == 0:
                        v.tensor_tensor(out=tot1[:, e:e + 1], in0=cpre[:, 15:16],
                                        in1=cs[:, 15:16], op=OP.add)
                        v.drain()
                    v.tensor_tensor(out=rnk[:], in0=pp, in1=cpre[:], op=OP.add)
                    v.drain()
                    if k == 0:
                        v.tensor_scalar_add(rnk[:], rnk[:], float(ECAP * e))
                    else:
                        v.tensor_scalar(out=rnk[:], in0=rnk[:], scalar1=tot1[:, e:e + 1],
                                        scalar2=float(ECAP * e), op0=OP.add, op1=OP.add)
                    v.drain()
                    v.tensor_scalar(out=rs1[:] if False else cpre[:], in0=rnk[:],
                                    scalar1=float(ECAP * (e + 1)), scalar2=None,
                                    op0=OP.is_lt)
                    v.drain()
                    tsl = trashc[:, 16 * k:16 * (k + 1)]
                    v.tensor_tensor(out=rnk[:], in0=rnk[:], in1=tsl, op=OP.subtract)
                    v.drain()
                    v.tensor_tensor(out=rnk[:], in0=rnk[:], in1=cpre[:], op=OP.mult)
                    v.drain()
                    v.tensor_tensor(out=rnk[:], in0=rnk[:], in1=tsl, op=OP.add)
                    v.drain()
                    sq = seq1 if k == 0 else seq2
                    sel_ap = sq[:, e: e + 16 * (NTC - 1) + 1: 16]
                    v.tensor_tensor(out=rnk[:], in0=rnk[:], in1=sel_ap, op=OP.mult)
                    v.drain()
                    last = v.tensor_tensor(out=Dk[k][:], in0=Dk[k][:], in1=rnk[:], op=OP.add)
                    v.drain()
                v.tensor_copy(Dki[k][:], Dk[k][:])
                v.tensor_copy(wbuf[k][:, 0:2 * NTC:2], iotac[:])
                v.drain()
                sw = sw1 if k == 0 else sw2
                last = v.tensor_copy(wbuf[k][:, 1:2 * NTC:2], sw[:])
                c_v.inc(last)
        block.vector(dv_p5b)
        n_disp = c_v.n

        def gq_p5b(g):
            g.wait_ge(c_v.sem, n_disp)
            g.wait_ge(c_x2r.sem, n_x2rw)
            for k in range(2):
                for tc in range(NTC):
                    c_ds.inc(g.indirect_dma_start(
                        out=disp_h[ch],
                        out_offset=bass.IndirectOffsetOnAxis(ap=Dki[k][:, tc:tc + 1], axis=0),
                        in_=wbuf[k][:, 2 * tc:2 * tc + 2], in_offset=None))
        block.gpsimd(gq_p5b)
        n_ds = c_ds.n
        p5b.close()

        p6sx = ExitStack()
        wsg_sb = sbc("wsg_sb", [128, NKC * MFF], BF16)
        wsu_sb = sbc("wsu_sb", [128, NKC * MFF], BF16)
        wsd_sb = sbc("wsd_sb", [128, 4 * C], BF16)

        def sy_wsld(s):
            s.wait_ge(c_g2.sem, n_cast)
            c_ws.inc(s.dma_start(out=wsg_sb[:], in_=wsg_h))
            c_ws.inc(s.dma_start(out=wsu_sb[:], in_=wsu_h))
            c_ws.inc(s.dma_start(out=wsd_sb[:], in_=wsd_h))
        block.sync(sy_wsld)
        n_ws = c_ws.n

        sg_b = sbc("sg_b", [128, 512], F32)
        h_bf = sbc("h_bf", [128, 2 * 4 * 256], BF16)
        eo_sb = [[sbc(f"eo_sb{g2}_{i2}", [128, C], F32) for i2 in range(2)]
                 for g2 in range(2)]
        pdnh = [psc(p6sx, f"pdnh{i2}", [128, 512], F32) for i2 in range(2)]
        pga = [psc(p6sx, f"pga{i2}", [128, 512], F32) for i2 in range(2)]
        pua = [psc(p6sx, f"pua{i2}", [128, 512], F32) for i2 in range(2)]

        NTAU_S = 8
        sp0, sa0, sv0 = c_p.n, c_a.n, c_v.n
        s_pe, s_a, s_v = {}, {}, {}

        def s_order():
            seq = []
            for tau in range(NTAU_S):
                if tau >= 2:
                    seq.append(("down", tau - 2))
                seq.append(("gu", tau))
            seq.append(("down", NTAU_S - 2))
            seq.append(("down", NTAU_S - 1))
            return seq

        n = sp0
        for item in s_order():
            if item[0] == "gu":
                for j in range(4):
                    n += 1
                    s_pe[("gu", item[1], j)] = n
            else:
                for q in range(4):
                    n += 1
                    s_pe[("downq", item[1], q)] = n
                s_pe[("down", item[1])] = n
        n = sa0
        for tau in range(NTAU_S):
            for mc in range(4):
                n += 1
                s_a[("silu", tau, mc)] = n
        n = sv0
        for tau in range(NTAU_S):
            for mc in range(4):
                n += 1
                s_v[("h", tau, mc)] = n
            if tau >= 1:
                for q in range(4):
                    n += 1
                    s_v[("evacq", tau - 1, q)] = n
        for q in range(4):
            n += 1
            s_v[("evacq", NTAU_S - 1, q)] = n
        n_p6s_v = n

        shw_b = (c_shw[0].n, c_shw[1].n)

        def sy_shw(s):
            for tau in range(NTAU_S):
                s.wait_ge(c_v.sem, s_v[("evacq", tau, 3)])
                for tb in range(2):
                    tc_g = 2 * tau + tb
                    c_shw[tau % 2].inc(s.dma_start(
                        out=acc_h[ch][128 * tc_g:128 * (tc_g + 1), :],
                        in_=eo_sb[tau % 2][tb][:]))
        block.sync(sy_shw)

        def pe_p6s(t):
            tw(t)
            t.wait_ge(c_ws.sem, n_ws)
            t.wait_ge(c_v.sem, n_x2)
            fdown = [True]
            dq = [0]
            for item in s_order():
                if item[0] == "gu":
                    tau = item[1]
                    for half in range(2):
                        for w_i in range(2):
                            bank = (pga if w_i == 0 else pua)[half]
                            wb_ = wsg_sb if w_i == 0 else wsu_sb
                            if tau >= 1:
                                if w_i == 0:
                                    t.wait_ge(c_a.sem, s_a[("silu", tau - 1, 2 * half + 1)])
                                else:
                                    t.wait_ge(c_v.sem, s_v[("h", tau - 1, 2 * half + 1)])
                            for mcl in range(2):
                                mc = 2 * half + mcl
                                for cc in range(NKC):
                                    mm = t.matmul(
                                        bank[:, 256 * mcl:256 * (mcl + 1)],
                                        wb_[:, cc * MFF + 128 * mc: cc * MFF + 128 * (mc + 1)],
                                        x2[:, cc * CHUNK + 256 * tau: cc * CHUNK + 256 * (tau + 1)],
                                        start=(cc == 0), stop=(cc == NKC - 1))
                            c_p.inc(mm)
                else:
                    tau = item[1]
                    if fdown[0]:
                        t.wait_ge(c_v.sem, n_disp)
                        fdown[0] = False
                    t.wait_ge(c_v.sem, s_v[("h", tau, 3)])
                    hb = 1024 * (tau % 2)
                    for q in range(4):
                        tb, fh = q % 2, q // 2
                        if dq[0] >= 2:
                            pt_, pq_ = divmod(dq[0] - 2, 4)
                            t.wait_ge(c_v.sem, s_v[("evacq", pt_, pq_)])
                        for mc in range(4):
                            mm = t.matmul(
                                pdnh[q % 2][:],
                                h_bf[:, hb + mc * 256 + 128 * tb: hb + mc * 256 + 128 * (tb + 1)],
                                wsd_sb[:, mc * C + 512 * fh: mc * C + 512 * (fh + 1)],
                                start=(mc == 0), stop=(mc == 3))
                        n_ = c_p.inc(mm)
                        assert n_ == s_pe[("downq", tau, q)]
                        dq[0] += 1
        block.tensor(pe_p6s)

        def act_p6s(a):
            for tau in range(NTAU_S):
                for mc in range(4):
                    a.wait_ge(c_p.sem, s_pe[("gu", tau, 2 * (mc // 2))])
                    gl = tau * 4 + mc
                    if gl >= 2:
                        t2, m2 = divmod(gl - 2, 4)
                        a.wait_ge(c_v.sem, s_v[("h", t2, m2)])
                    c_a.inc(a.activation(
                        out=sg_b[:, 256 * (mc % 2):256 * (mc % 2 + 1)],
                        in_=pga[mc // 2][:, 256 * (mc % 2):256 * (mc % 2 + 1)],
                        func=AF.Silu))
        block.scalar(act_p6s)

        def s_evac(v, tau):
            if tau >= 2:
                par = tau % 2
                v.wait_ge(c_shw[par].sem, shw_b[par] + 32 * (tau // 2))
            v.drain()
            for q in range(4):
                tb, fh = q % 2, q // 2
                v.wait_ge(c_p.sem, s_pe[("downq", tau, q)])
                tc_g = 2 * tau + tb
                op = v.tensor_tensor(
                    out=eo_sb[tau % 2][tb][:, 512 * fh:512 * (fh + 1)],
                    in0=pdnh[q % 2][:],
                    in1=x2rows[:, tc_g * C + 512 * fh: tc_g * C + 512 * (fh + 1)],
                    op=OP.add)
                n_ = c_v.inc(op)
                assert n_ == s_v[("evacq", tau, q)]

        def dv_p6s(v):
            for tau in range(NTAU_S):
                if tau >= 2:
                    v.wait_ge(c_p.sem, s_pe[("down", tau - 2)])
                hb = 1024 * (tau % 2)
                for mc in range(4):
                    v.wait_ge(c_a.sem, s_a[("silu", tau, mc)])
                    v.wait_ge(c_p.sem, s_pe[("gu", tau, 2 * (mc // 2) + 1)])
                    c_v.inc(v.tensor_tensor(
                        out=h_bf[:, hb + mc * 256: hb + (mc + 1) * 256],
                        in0=sg_b[:, 256 * (mc % 2):256 * (mc % 2 + 1)],
                        in1=pua[mc // 2][:, 256 * (mc % 2):256 * (mc % 2 + 1)],
                        op=OP.mult))
                if tau >= 1:
                    s_evac(v, tau - 1)
            s_evac(v, NTAU_S - 1)
        block.vector(dv_p6s)
        assert c_v.n == n_p6s_v
        p6sx.close()

        ch_marks.append(dict(n_ds=n_ds, n_x2rw=n_x2rw, n_disp=n_disp, n_x2=n_x2))


        prev_chunk_done = dict(v=c_v.n, o0=c_o[0].n, o1=c_o[1].n, p=c_p.n, a=c_a.n,
                               shw0=c_shw[0].n, shw1=c_shw[1].n,
                               ds=c_ds.n, x2r=c_x2r.n)
        chs.close()

    pass1_v = c_v.n
    p6r = ExitStack()
    sb7 = lambda name, shape, dt: p6r.enter_context(nc.sbuf_tensor(name, shape, dt))
    ps7 = lambda name, shape, dt: p6r.enter_context(nc.psum_tensor(name, shape, dt))
    wgb = [sb7(f"wgb{i}", [128, NKC * MFF], BF16) for i in range(2)]
    wub = [sb7(f"wub{i}", [128, NKC * MFF], BF16) for i in range(2)]
    wdb = [sb7(f"wdb{i}", [128, 4 * C], BF16) for i in range(2)]
    xgT = [sb7(f"xgT{i}", [128, NKC * ECAP], BF16) for i in range(4)]
    xgr = [sb7(f"xgr{i}", [128, (ECAP // 128) * C], BF16) for i in range(4)]
    idw = [sb7(f"idw{i}", [128, (ECAP // 128) * 2], F32) for i in range(4)]
    idi = [sb7(f"idi{i}", [128, ECAP // 128], mybir.dt.int32) for i in range(4)]
    sg_b2 = sb7("sg_b2", [128, 512], F32)
    h_bf2 = sb7("h_bf2", [128, 2 * 4 * 256], BF16)
    eo_bf = [[sb7(f"eo_bf{g2}_{i2}", [128, C], BF16) for i2 in range(2)]
             for g2 in range(2)]
    pdn2 = [ps7(f"pdn2_{i}", [128, 512], F32) for i in range(2)]
    ptt = [ps7(f"ptt{i}", [128, 1024], BF16) for i in range(2)]
    pga2 = [ps7(f"pga2_{i}", [128, 512], F32) for i in range(2)]
    pua2 = [ps7(f"pua2_{i}", [128, 512], F32) for i in range(2)]

    NU = 2 * NE
    ue = lambda u: u // 2
    uc = lambda u: u % 2

    t_at_unit = {u: [] for u in range(NU)}
    t_at_unit[0] = [0, 1, 2]
    for u in range(1, NU - 2):
        t_at_unit[u].append(u + 2)

    r_pe, r_a, r_v = {}, {}, {}
    tr_seq = []
    dq_seq = []
    u_items = (("gu", 0), ("gu", 1), ("down", 0), ("down", 1))
    n = c_p.n
    for u in range(NU):
        for u_t in t_at_unit[u]:
            for j in range(4):
                n += 1
                r_pe[("tr", u_t, j)] = n
                tr_seq.append((u_t, j))
        for item in u_items:
            if item[0] == "gu":
                for j in range(4):
                    n += 1
                    r_pe[("gu", u, item[1], j)] = n
            else:
                for q in range(4):
                    n += 1
                    r_pe[("downq", u, item[1], q)] = n
                    dq_seq.append((u, item[1], q))
                r_pe[("down", u, item[1])] = n
    n = c_a.n
    for u in range(NU):
        for u_t in t_at_unit[u]:
            for j in range(4):
                n += 1
                r_a[("trv", u_t, j)] = n
        for tau in range(2):
            for mc in range(4):
                n += 1
                r_a[("silu", u, tau, mc)] = n
    n = c_v.n
    for u in range(NU):
        for tau in range(2):
            for mc in range(4):
                n += 1
                r_v[("h", u, tau, mc)] = n
            if tau >= 1:
                for q in range(4):
                    n += 1
                    r_v[("evacq", u, 0, q)] = n
        for q in range(4):
            n += 1
            r_v[("evacq", u, 1, q)] = n
    n_p6r_v = n

    gx_b2 = tuple(c_gx[i].n for i in range(4))
    gx_marks2 = {u: (u % 4, gx_b2[u % 4] + 64 * (u // 4 + 1)) for u in range(NU)}
    idc_b2 = c_idc.n
    v_idc2 = {u: idc_b2 + u + 1 for u in range(NU)}
    eo_b = (c_eo[0].n, c_eo[1].n)
    evac_consumer2 = {}
    _eo_n = [0, 0]
    for u in range(NU):
        for tau in range(2):
            g_ = 2 * u + tau
            _eo_n[g_ % 2] += 2
            evac_consumer2[g_] = (g_ % 2, eo_b[g_ % 2] + 16 * _eo_n[g_ % 2])

    sy_marks = {}
    id_marks = {}

    def sy_ffnr(s):
        s.wait_ge(c_v.sem, pass1_v)
        s.wait_ge(c_shw[0].sem, c_shw[0].n)
        s.wait_ge(c_shw[1].sem, c_shw[1].n)
        s.wait_ge(c_x2r.sem, c_x2r.n)
        s.wait_ge(c_ds.sem, c_ds.n)

        def wload(e):
            if e >= 2:
                s.wait_ge(c_p.sem, r_pe[("down", 2 * (e - 2) + 1, 1)])
            ce = c_e[e % 2]
            ce.inc(s.dma_start(out=wgb[e % 2][:], in_=wg_h[128 * e:128 * (e + 1), :]))
            ce.inc(s.dma_start(out=wub[e % 2][:], in_=wu_h[128 * e:128 * (e + 1), :]))
            ce.inc(s.dma_start(out=wdb[e % 2][:], in_=wd_h[128 * e:128 * (e + 1), :]))
            sy_marks[e] = (e % 2, ce.n)

        wload(0)
        wload(1)
        for u in range(2, NU):
            if u % 2 == 0 and ue(u) + 1 < NE:
                wload(ue(u) + 1)
    block.sync(sy_ffnr)

    def gq_p6r(g):
        g.wait_ge(c_v.sem, pass1_v)
        g.wait_ge(c_x2r.sem, c_x2r.n)
        g.wait_ge(c_ds.sem, c_ds.n)
        g.wait_ge(c_shw[0].sem, c_shw[0].n)
        g.wait_ge(c_shw[1].sem, c_shw[1].n)

        def idload(u):
            b = u % 4
            if u >= 4:
                g.wait_ge(c_v.sem, r_v[("evacq", u - 4, 1, 3)])
            c_id[b].inc(g.dma_start(
                out=idw[b][:].rearrange("p (r two) -> p r two", two=2),
                in_=_t3(disp_h[uc(u)][ECAP * ue(u):ECAP * (ue(u) + 1), :])))
            id_marks[u] = (b, c_id[b].n)

        def gathers(u):
            b = u % 4
            bb = u % 4
            g.wait_ge(c_x2r.sem, ch_marks[uc(u)]["n_x2rw"])
            g.wait_ge(c_idc.sem, v_idc2[u])
            if u >= 4:
                g.wait_ge(c_p.sem, r_pe[("tr", u - 4, 3)])
            for rc in range(ECAP // 128):
                c_gx[b].inc(g.indirect_dma_start(
                    out=xgr[bb][:, rc * C:(rc + 1) * C], out_offset=None,
                    in_=x2r_h[uc(u)],
                    in_offset=bass.IndirectOffsetOnAxis(ap=idi[bb][:, rc:rc + 1], axis=0)))
            assert gx_marks2[u] == (b, c_gx[b].n)

        def eowrites(u):
            for tau in range(2):
                g.wait_ge(c_v.sem, r_v[("evacq", u, tau, 3)])
                for tb in range(2):
                    rc = 2 * tau + tb
                    c_eo[tau % 2].inc(g.dma_start(
                        out=eo_h[uc(u)][ECAP * ue(u) + 128 * rc:
                                        ECAP * ue(u) + 128 * (rc + 1), :],
                        in_=eo_bf[tau % 2][tb][:]))

        for u in range(4):
            idload(u)
        gathers(0)
        gathers(1)
        gathers(2)
        gathers(3)
        for u in range(NU):
            eowrites(u)
            if u + 4 < NU:
                idload(u + 4)
                gathers(u + 4)
    block.gpsimd(gq_p6r)
    n_eo = (c_eo[0].n, c_eo[1].n)

    def pe_p6r(t):
        tw(t)
        t.wait_ge(c_v.sem, pass1_v)
        tr_i = [0]
        dq_i = [0]
        for u in range(NU):
            e = ue(u)
            t.wait_ge(c_e[sy_marks[e][0]].sem, sy_marks[e][1])
            for u_t in t_at_unit[u]:
                bt = u_t % 4
                t.wait_ge(c_gx[gx_marks2[u_t][0]].sem, gx_marks2[u_t][1])
                for j in range(4):
                    gi = tr_i[0]
                    if gi >= 2:
                        pu_, pj_ = tr_seq[gi - 2]
                        t.wait_ge(c_a.sem, r_a[("trv", pu_, pj_)])
                    for cc in range(NKC):
                        mm = t.transpose(
                            ptt[gi % 2][:, 128 * cc:128 * (cc + 1)],
                            xgr[bt][:, j * C + 128 * cc: j * C + 128 * (cc + 1)],
                            id16[:])
                    n_ = c_p.inc(mm)
                    assert n_ == r_pe[("tr", u_t, j)]
                    tr_i[0] += 1
            t.wait_ge(c_a.sem, r_a[("trv", u, 3)])
            for item in u_items:
                if item[0] == "gu":
                    tau = item[1]
                    prev = None
                    if (u, tau) != (0, 0):
                        prev = (u, 0) if tau == 1 else (u - 1, 1)
                    for half in range(2):
                        for w_i in range(2):
                            bank = (pga2 if w_i == 0 else pua2)[half]
                            wbuf_ = (wgb if w_i == 0 else wub)[e % 2]
                            if prev is not None:
                                if w_i == 0:
                                    t.wait_ge(c_a.sem,
                                              r_a[("silu",) + prev + (2 * half + 1,)])
                                else:
                                    t.wait_ge(c_v.sem,
                                              r_v[("h",) + prev + (2 * half + 1,)])
                            for mcl in range(2):
                                mc = 2 * half + mcl
                                for cc in range(NKC):
                                    mm = t.matmul(
                                        bank[:, 256 * mcl:256 * (mcl + 1)],
                                        wbuf_[:, cc * MFF + 128 * mc: cc * MFF + 128 * (mc + 1)],
                                        xgT[u % 4][:, cc * ECAP + 256 * tau:
                                                   cc * ECAP + 256 * (tau + 1)],
                                        start=(cc == 0), stop=(cc == NKC - 1))
                            c_p.inc(mm)
                else:
                    tau = item[1]
                    t.wait_ge(c_v.sem, r_v[("h", u, tau, 3)])
                    hb = 1024 * (tau % 2)
                    for q in range(4):
                        tb, fh = q % 2, q // 2
                        gi = dq_i[0]
                        if gi >= 2:
                            k_ = dq_seq[gi - 2]
                            t.wait_ge(c_v.sem, r_v[("evacq",) + k_])
                        for mc in range(4):
                            mm = t.matmul(
                                pdn2[gi % 2][:],
                                h_bf2[:, hb + mc * 256 + 128 * tb:
                                      hb + mc * 256 + 128 * (tb + 1)],
                                wdb[e % 2][:, mc * C + 512 * fh: mc * C + 512 * (fh + 1)],
                                start=(mc == 0), stop=(mc == 3))
                        n_ = c_p.inc(mm)
                        assert n_ == r_pe[("downq", u, tau, q)]
                        dq_i[0] += 1
    block.tensor(pe_p6r)

    def act_p6r(a):
        tr_i = [0]
        for u in range(NU):
            for u_t in t_at_unit[u]:
                bt = u_t % 4
                if u_t >= 4:
                    a.wait_ge(c_p.sem, r_pe[("gu", u_t - 4, 1, 3)])
                for j in range(4):
                    a.wait_ge(c_p.sem, r_pe[("tr", u_t, j)])
                    n_ = c_a.inc(a.activation(
                        out=xgT[bt][:].rearrange("p (cc j2) -> p cc j2",
                                                 j2=ECAP)[:, :, 128 * j:128 * (j + 1)],
                        in_=ptt[tr_i[0] % 2][:], func=AF.Identity))
                    assert n_ == r_a[("trv", u_t, j)]
                    tr_i[0] += 1
            for tau in range(2):
                for mc in range(4):
                    a.wait_ge(c_p.sem, r_pe[("gu", u, tau, 2 * (mc // 2))])
                    gl = (2 * u + tau) * 4 + mc
                    if gl >= 2:
                        u2, r2 = divmod(gl - 2, 8)
                        tau2, mc2 = divmod(r2, 4)
                        a.wait_ge(c_v.sem, r_v[("h", u2, tau2, mc2)])
                    c_a.inc(a.activation(
                        out=sg_b2[:, 256 * (mc % 2):256 * (mc % 2 + 1)],
                        in_=pga2[mc // 2][:, 256 * (mc % 2):256 * (mc % 2 + 1)],
                        func=AF.Silu))
    block.scalar(act_p6r)

    def emit_cast2(v, u):
        b = u % 4
        v.wait_ge(c_id[id_marks[u][0]].sem, id_marks[u][1])
        cp = v.tensor_copy(idi[b][:], idw[b][:, 0:2 * (ECAP // 128):2])
        cp.then_inc(c_idc.sem, 1)
        c_idc.n += 1
        assert c_idc.n == v_idc2[u]

    def r_evac(v, u, tau):
        g_ = 2 * u + tau
        if g_ >= 2:
            par, cnt = evac_consumer2[g_ - 2]
            v.wait_ge(c_eo[par].sem, cnt)
        v.drain()
        for q in range(4):
            tb, fh = q % 2, q // 2
            v.wait_ge(c_p.sem, r_pe[("downq", u, tau, q)])
            rc = 2 * tau + tb
            op = v.tensor_scalar(
                out=eo_bf[tau % 2][tb][:, 512 * fh:512 * (fh + 1)],
                in0=pdn2[q % 2][:],
                scalar1=idw[u % 4][:, 2 * rc + 1:2 * rc + 2], scalar2=None,
                op0=OP.mult)
            n_ = c_v.inc(op)
            assert n_ == r_v[("evacq", u, tau, q)]

    def dv_p6r(v):
        emit_cast2(v, 0)
        emit_cast2(v, 1)
        emit_cast2(v, 2)
        emit_cast2(v, 3)
        for u in range(NU):
            for tau in range(2):
                gt_ = 2 * u + tau
                if gt_ >= 2:
                    u2, tau2 = divmod(gt_ - 2, 2)
                    v.wait_ge(c_p.sem, r_pe[("down", u2, tau2)])
                hb = 1024 * (tau % 2)
                for mc in range(4):
                    v.wait_ge(c_a.sem, r_a[("silu", u, tau, mc)])
                    v.wait_ge(c_p.sem, r_pe[("gu", u, tau, 2 * (mc // 2) + 1)])
                    c_v.inc(v.tensor_tensor(
                        out=h_bf2[:, hb + mc * 256: hb + (mc + 1) * 256],
                        in0=sg_b2[:, 256 * (mc % 2):256 * (mc % 2 + 1)],
                        in1=pua2[mc // 2][:, 256 * (mc % 2):256 * (mc % 2 + 1)],
                        op=OP.mult))
                if tau >= 1:
                    r_evac(v, u, 0)
            r_evac(v, u, 1)
            if u + 4 < NU:
                emit_cast2(v, u + 4)
    block.vector(dv_p6r)
    assert c_v.n == n_p6r_v, (c_v.n, n_p6r_v)
    p6r.close()

    p7s = ExitStack()
    accr = [p7s.enter_context(nc.sbuf_tensor(f"accr_{i}", [128, C], F32))
            for i in range(2)]
    outr = [p7s.enter_context(nc.sbuf_tensor(f"outr_{i}", [128, C], F32))
            for i in range(2)]
    bnst = p7s.enter_context(nc.sbuf_tensor("bnst", [128, 12], F32))
    bnag = p7s.enter_context(nc.sbuf_tensor("bnag", [128, 2], F32))
    rstd2 = p7s.enter_context(nc.sbuf_tensor("rstd2", [128, 1], F32))
    egb = [[p7s.enter_context(nc.sbuf_tensor(f"egb_{k}_{i}", [128, C], BF16))
            for i in range(2)] for k in range(2)]

    NT2 = NCHUNK * NTC
    v_b7 = c_v.n
    a_b7 = c_a.n
    o_b7 = [c_o[0].n, c_o[1].n]
    ac_b = [c_ac[0].n, c_ac[1].n]
    eg_b = [c_eg[0].n, c_eg[1].n]
    pe_last = r_pe[("down", NU - 1, 1)]

    def sy_p7(s):
        def store(ti):
            b = ti % 2
            s.wait_ge(c_v.sem, v_b7 + 2 * ti + 2)
            c_o[b].inc(s.dma_start(
                out=out_hbm[128 * ti: 128 * (ti + 1), :], in_=outr[b][:]))
        s.wait_ge(c_shw[0].sem, c_shw[0].n)
        s.wait_ge(c_shw[1].sem, c_shw[1].n)
        for cc_ in (c_e[0], c_e[1], c_gx[0], c_gx[1], c_gx[2], c_gx[3],
                    c_id[0], c_id[1], c_id[2], c_id[3], c_eo[0], c_eo[1]):
            s.wait_ge(cc_.sem, cc_.n)
        s.wait_ge(c_p.sem, pe_last)
        s.wait_ge(c_v.sem, n_p6r_v)
        for ti in range(NT2):
            b = ti % 2
            if ti >= 2:
                store(ti - 2)
            ch2, tc = divmod(ti, NTC)
            c_ac[b].inc(s.dma_start(out=accr[b][:],
                                    in_=acc_h[ch2][128 * tc:128 * (tc + 1), :]))
        store(NT2 - 2)
        store(NT2 - 1)
    block.sync(sy_p7)

    def gq_p7(g):
        g.wait_ge(c_eo[0].sem, n_eo[0])
        g.wait_ge(c_eo[1].sem, n_eo[1])
        for cc_ in (c_e[0], c_e[1], c_id[0], c_id[1], c_id[2], c_id[3]):
            g.wait_ge(cc_.sem, cc_.n)
        g.wait_ge(c_p.sem, pe_last)
        for ti in range(NT2):
            b = ti % 2
            ch2, tc = divmod(ti, NTC)
            if ti >= 2:
                g.wait_ge(c_v.sem, v_b7 + 2 * (ti - 2) + 1)
            for k in range(2):
                c_eg[b].inc(g.indirect_dma_start(
                    out=egb[k][b][:], out_offset=None, in_=eo_h[ch2],
                    in_offset=bass.IndirectOffsetOnAxis(ap=Dki2[ch2][k][:, tc:tc + 1],
                                                        axis=0)))
    block.gpsimd(gq_p7)

    def dv_p7(v):
        for ti in range(NT2):
            b = ti % 2
            v.wait_ge(c_ac[b].sem, ac_b[b] + 16 * (ti // 2 + 1))
            v.wait_ge(c_eg[b].sem, eg_b[b] + 32 * (ti // 2 + 1))
            v.drain()
            v.tensor_tensor(out=accr[b][:], in0=accr[b][:], in1=egb[0][b][:],
                            op=OP.add)
            v.drain()
            v.tensor_tensor(out=accr[b][:], in0=accr[b][:], in1=egb[1][b][:],
                            op=OP.add)
            v.drain()
            v.bn_stats(bnst[:, 0:6], accr[b][:, 0:512])
            v.bn_stats(bnst[:, 6:12], accr[b][:, 512:1024])
            v.drain()
            v.bn_aggr(bnag[:], bnst[:])
            v.drain()
            v.tensor_scalar_add(rstd2[:], bnag[:, 1:2], EPS)
            v.drain()
            c_v.inc(v.reciprocal(rstd2[:], rstd2[:]))
            v.wait_ge(c_a.sem, a_b7 + ti + 1)
            if ti >= 2:
                v.wait_ge(c_o[b].sem, o_b7[b] + 16 * (ti // 2))
            v.drain()
            c_v.inc(v.tensor_scalar(
                out=outr[b][:], in0=accr[b][:], scalar1=bnag[:, 0:1],
                scalar2=rstd2[:], op0=OP.subtract, op1=OP.mult))
    block.vector(dv_p7)

    def act_p7(a):
        for ti in range(NT2):
            a.wait_ge(c_v.sem, v_b7 + 2 * ti + 1)
            c_a.inc(a.activation(out=rstd2[:], in_=rstd2[:], func=AF.Sqrt))
    block.scalar(act_p7)
    p7s.close()


    def sy_final(s):
        s.wait_ge(c_o[0].sem, c_o[0].n)
        s.wait_ge(c_o[1].sem, c_o[1].n)
        if c_tap.n:
            s.wait_ge(c_tap.sem, c_tap.n)
    block.sync(sy_final)

    blk_cm.__exit__(None, None, None)
    stk.close()
    return nc


def _host_consts():
    i = np.arange(128)[:, None]
    j = np.arange(128)[None, :]
    m_list = [(j >= i)]
    for dp in (0, 1):
        for t_ in (0, 1):
            m_list.append((j + 128 * t_) >= (2 * i + dp))
    masks = np.concatenate([m.astype(np.float32) for m in m_list], axis=1)
    sel = np.zeros((16, 16 * 128), np.float32)
    for e in range(16):
        sel[e, 128 * e:128 * (e + 1)] = 1.0
    ltx = (np.arange(128)[:, None] < np.arange(128)[None, :]).astype(np.float32)
    iota = (np.arange(NTC)[None, :] * 128 + np.arange(128)[:, None]).astype(np.float32)
    dinit = np.zeros((NE * ECAP + 128, 2), np.float32)
    dinit[:, 0] = CHUNK
    return dict(
        id32=np.eye(128, dtype=np.float32),
        id16=np.eye(128, dtype=np.float32).astype(ml_dtypes.bfloat16),
        ones=np.ones((128, 128), dtype=np.float32).astype(ml_dtypes.bfloat16),
        masks=masks.astype(ml_dtypes.bfloat16),
        sel=sel.astype(ml_dtypes.bfloat16),
        ltx=ltx.astype(ml_dtypes.bfloat16),
        iota=iota,
        dinit=dinit,
        zrow=np.zeros((1, C), ml_dtypes.bfloat16),
        zrows=np.zeros((128, C), ml_dtypes.bfloat16),
        trashc=np.broadcast_to(
            NE * ECAP + np.arange(2 * NTC, dtype=np.float32)[None, :],
            (128, 2 * NTC)).copy(),
    )


def kernel(L0_attn_x, L0_attn_c_attn_w, L0_attn_c_attn_b, L0_attn_q_w, L0_attn_q_b,
           L0_attn_k_w, L0_attn_k_b, L0_attn_v_w, L0_attn_v_b, L0_attn_o_w, L0_attn_o_b,
           L0_ffn_gate_w, L0_ffn_w_gate, L0_ffn_w_up, L0_ffn_w_down,
           L0_ffn_ws_gate, L0_ffn_ws_up, L0_ffn_ws_down, _taps=(), _cores=None):
    nc = build(taps=_taps)
    xf = np.ascontiguousarray(np.asarray(L0_attn_x).reshape(B * T, C))
    consts = _host_consts()
    shared = dict(
        caw=np.ascontiguousarray(L0_attn_c_attn_w), cab=np.ascontiguousarray(L0_attn_c_attn_b),
        qw=np.ascontiguousarray(L0_attn_q_w), qb=np.ascontiguousarray(L0_attn_q_b),
        kw=np.ascontiguousarray(L0_attn_k_w), kb=np.ascontiguousarray(L0_attn_k_b),
        vw=np.ascontiguousarray(L0_attn_v_w), vb=np.ascontiguousarray(L0_attn_v_b),
        ow=np.ascontiguousarray(L0_attn_o_w), ob=np.ascontiguousarray(L0_attn_o_b),
        gw=np.ascontiguousarray(L0_ffn_gate_w), wg=np.ascontiguousarray(L0_ffn_w_gate),
        wu=np.ascontiguousarray(L0_ffn_w_up), wd=np.ascontiguousarray(L0_ffn_w_down),
        wsg=np.ascontiguousarray(L0_ffn_ws_gate), wsu=np.ascontiguousarray(L0_ffn_ws_up),
        wsd=np.ascontiguousarray(L0_ffn_ws_down), **consts)
    cores = list(range(NCORES)) if _cores is None else _cores
    in_maps = []
    for ci in range(len(cores)):
        m = dict(shared)
        m["x"] = np.ascontiguousarray(xf[ci * TPC:(ci + 1) * TPC])
        in_maps.append(m)
    res = run_bass_kernel_spmd(nc, in_maps, core_ids=cores)
    if _taps:
        return res
    out = np.concatenate([r["out"] for r in res.results], axis=0)
    return np.ascontiguousarray(out.reshape(B, T, C).astype(np.float32))


# version 29
# speedup vs baseline: 1.0145x; 1.0145x over previous
import numpy as np
import ml_dtypes
from contextlib import ExitStack

import concourse.bass as bass
import concourse.mybir as mybir
from concourse.bass_utils import run_bass_kernel_spmd

F32 = mybir.dt.float32
BF16 = mybir.dt.bfloat16
AF = mybir.ActivationFunctionType
OP = mybir.AluOpType

B, T, C, H, D = 2, 16384, 1024, 16, 64
NCORES = 8
TPC = (B * T) // NCORES
CHUNK = 2048
NCHUNK = TPC // CHUNK
NPAIR = H // 2
NE = 16
MFF = 512
EPS = 1e-5
NKC = C // 128
ECAP = 512
NTC = CHUNK // 128


class Ctr:
    def __init__(self, nc, name, step=1):
        self.sem = nc.semaphore(name).__enter__()
        self.step = step
        self.n = 0

    def inc(self, instr):
        instr.then_inc(self.sem, self.step)
        self.n += self.step
        return self.n


def unit_list():
    return [("b1", s) for s in range(4)] + [("b2", s) for s in range(2)] + [("b3", 0)]


def unit_rects(branch, seg, head):
    rects = []
    if branch in ("b1", "b3"):
        if branch == "b1":
            kstart = lambda kc: (512 * seg + 128 * kc, 1)
            qstart = lambda h: (512 * seg + 256 * h, 1)
            vtile = lambda kc: (4 * seg + kc, False)
        else:
            c = head % 4
            kstart = lambda kc: (c + 512 * kc, 4)
            qstart = lambda h: (c + 1024 * h, 4)
            vtile = lambda kc: (4 * c + kc, True)
        for h in (0, 1):
            for kc in range(2 * (h + 1)):
                o = 128 * kc - 256 * h
                masks = [(0, o)] if 0 <= o < 256 else []
                rects.append(dict(h=h, k=kstart(kc), q=qstart(h), v=vtile(kc),
                                  start=max(0, o), masks=masks))
    else:
        p = head % 2
        for h in (0, 1):
            for h2 in range(h + 1):
                for dp in (0, 1):
                    c = p + 2 * dp
                    masks = [(1 + 2 * dp + t, 128 * t) for t in (0, 1)] if h == h2 else []
                    rects.append(dict(h=h, k=(c + 1024 * seg + 512 * h2, 4),
                                      q=(1024 * seg + p + 512 * h, 2),
                                      v=(4 * c + 2 * seg + h2, True),
                                      start=0, masks=masks))
    return rects


def unit_ycols(branch, seg, head):
    if branch == "b1":
        return (512 * seg, 1)
    if branch == "b2":
        return (1024 * seg + head % 2, 2)
    return (head % 4, 4)


def _t3(src2d, p=128):
    return src2d.rearrange("(rc p) c -> p rc c", p=p)


def _d3(dstbuf, ncols):
    return dstbuf[:].rearrange("p (rc c) -> p rc c", c=ncols)


def build(taps=()):
    taps = set(taps)
    nc = bass.Bass("TRN2", target_bir_lowering=False, debug=False)

    x_in = nc.dram_tensor("x", [TPC, C], F32, kind="ExternalInput").ap()
    out_hbm = nc.dram_tensor("out", [TPC, C], F32, kind="ExternalOutput").ap()
    caw = nc.dram_tensor("caw", [C, 3 * C], F32, kind="ExternalInput").ap()
    cab = nc.dram_tensor("cab", [3 * C], F32, kind="ExternalInput").ap()
    qw = nc.dram_tensor("qw", [C, C], F32, kind="ExternalInput").ap()
    qb = nc.dram_tensor("qb", [C], F32, kind="ExternalInput").ap()
    kw = nc.dram_tensor("kw", [C, C], F32, kind="ExternalInput").ap()
    kb = nc.dram_tensor("kb", [C], F32, kind="ExternalInput").ap()
    vw = nc.dram_tensor("vw", [C, C], F32, kind="ExternalInput").ap()
    vb = nc.dram_tensor("vb", [C], F32, kind="ExternalInput").ap()
    ow = nc.dram_tensor("ow", [C, C], F32, kind="ExternalInput").ap()
    ob = nc.dram_tensor("ob", [C], F32, kind="ExternalInput").ap()
    gw_in = nc.dram_tensor("gw", [NE, C], F32, kind="ExternalInput").ap()
    wg_in = nc.dram_tensor("wg", [NE, C, MFF], F32, kind="ExternalInput").ap()
    wu_in = nc.dram_tensor("wu", [NE, C, MFF], F32, kind="ExternalInput").ap()
    wd_in = nc.dram_tensor("wd", [NE, MFF, C], F32, kind="ExternalInput").ap()
    wsg_in = nc.dram_tensor("wsg", [C, MFF], F32, kind="ExternalInput").ap()
    wsu_in = nc.dram_tensor("wsu", [C, MFF], F32, kind="ExternalInput").ap()
    wsd_in = nc.dram_tensor("wsd", [MFF, C], F32, kind="ExternalInput").ap()
    id32_in = nc.dram_tensor("id32", [128, 128], F32, kind="ExternalInput").ap()
    id16_in = nc.dram_tensor("id16", [128, 128], BF16, kind="ExternalInput").ap()
    ones_in = nc.dram_tensor("ones", [128, 128], BF16, kind="ExternalInput").ap()
    masks_in = nc.dram_tensor("masks", [128, 5 * 128], BF16, kind="ExternalInput").ap()
    sel_in = nc.dram_tensor("sel", [16, NE * 128], BF16, kind="ExternalInput").ap()

    wq_h = nc.dram_tensor("wq_h", [128, NKC * C], BF16, kind="Internal").ap()
    wk_h = nc.dram_tensor("wk_h", [128, NKC * C], BF16, kind="Internal").ap()
    wv_h = nc.dram_tensor("wv_h", [128, NKC * C], BF16, kind="Internal").ap()
    ow_h = nc.dram_tensor("ow_h", [C, C], BF16, kind="Internal").ap()
    wg_h = nc.dram_tensor("wg_h", [NE * 128, NKC * MFF], BF16, kind="Internal").ap()
    wu_h = nc.dram_tensor("wu_h", [NE * 128, NKC * MFF], BF16, kind="Internal").ap()
    wd_h = nc.dram_tensor("wd_h", [NE * 128, 4 * C], BF16, kind="Internal").ap()
    wsg_h = nc.dram_tensor("wsg_h", [128, NKC * MFF], BF16, kind="Internal").ap()
    wsu_h = nc.dram_tensor("wsu_h", [128, NKC * MFF], BF16, kind="Internal").ap()
    wsd_h = nc.dram_tensor("wsd_h", [128, 4 * C], BF16, kind="Internal").ap()
    x2r_h = [nc.dram_tensor(f"x2r_h{i}", [CHUNK + 1, C], BF16, kind="Internal").ap()
             for i in range(NCHUNK)]
    disp_h = [nc.dram_tensor(f"disp_h{i}", [NE * ECAP + 128, 2], F32, kind="Internal").ap()
              for i in range(NCHUNK)]
    acc_h = [nc.dram_tensor(f"acc_h{i}", [CHUNK + 1, C], F32, kind="Internal").ap()
             for i in range(NCHUNK)]
    eo_h = [nc.dram_tensor(f"eo_h{i}", [NE * ECAP + 128, C], BF16, kind="Internal").ap()
            for i in range(NCHUNK)]
    ltx_in = nc.dram_tensor("ltx", [128, 128], BF16, kind="ExternalInput").ap()
    iota_in = nc.dram_tensor("iota", [128, NTC], F32, kind="ExternalInput").ap()
    dinit_in = nc.dram_tensor("dinit", [NE * ECAP + 128, 2], F32, kind="ExternalInput").ap()
    zrow_in = nc.dram_tensor("zrow", [1, C], BF16, kind="ExternalInput").ap()
    zrows_in = nc.dram_tensor("zrows", [128, C], BF16, kind="ExternalInput").ap()
    trashc_in = nc.dram_tensor("trashc", [128, 2 * NTC], F32, kind="ExternalInput").ap()

    tap_t = {}
    for name, shape, dt in [
        ("t_xT", [128, NKC * CHUNK], BF16), ("t_q", [128, CHUNK], BF16),
        ("t_k", [128, CHUNK], BF16), ("t_vpl", [128, NTC * C], BF16),
        ("t_vsr", [128, NTC * C], BF16), ("t_y", [128, CHUNK], F32),
        ("t_s", [128, CHUNK], F32), ("t_Y", [128, NKC * CHUNK], BF16),
        ("t_r1", [128, NKC * CHUNK], BF16), ("t_x2", [128, NKC * CHUNK], BF16),
        ("t_route", [NE, CHUNK], BF16), ("t_acc", [128, NKC * CHUNK], F32),
        ("t_weff", [128, NKC * C], BF16),
    ]:
        if name in taps:
            tap_t[name] = nc.dram_tensor(name, shape, dt, kind="ExternalOutput").ap()

    stk = ExitStack()
    sb = lambda name, shape, dt: stk.enter_context(nc.sbuf_tensor(name, shape, dt))

    id32 = sb("id32c", [128, 128], F32)
    id16 = sb("id16c", [128, 128], BF16)
    ones = sb("onesc", [128, 128], BF16)
    masks = sb("masksc", [128, 5 * 128], BF16)
    bqe = sb("bqe", [128, NKC], F32)
    bke = sb("bke", [128, NKC], F32)
    bve = sb("bve", [128, NKC], F32)
    obs = sb("obs", [128, NKC], F32)
    gwT = sb("gwT", [128, 3 * NKC * NE], BF16)
    sel = sb("selc", [16, NE * 128], BF16)
    ltx = sb("ltxc", [128, 128], BF16)
    iotac = sb("iotac", [128, NTC], F32)
    trashc = sb("trashc_s", [128, 2 * NTC], F32)
    Dki2 = [[sb(f"Dki_{i}_{k}", [128, NTC], mybir.dt.int32) for k in range(2)]
            for i in range(NCHUNK)]

    c_g = Ctr(nc, "c_g", 16)
    c_ld1 = Ctr(nc, "c_ld1", 16)
    c_g2 = Ctr(nc, "c_g2", 16)
    c_x = Ctr(nc, "c_x", 16)
    c_wf = Ctr(nc, "c_wf", 16)
    c_wv = Ctr(nc, "c_wv", 16)
    c_wq = [Ctr(nc, "c_wq0", 16), Ctr(nc, "c_wq1", 16)]
    c_ow = Ctr(nc, "c_ow", 16)
    c_e = [Ctr(nc, "c_e0", 16), Ctr(nc, "c_e1", 16)]
    c_o = [Ctr(nc, "c_o0", 16), Ctr(nc, "c_o1", 16)]
    c_tap = Ctr(nc, "c_tap", 16)
    c_x2r = Ctr(nc, "c_x2r", 16)
    c_ds = Ctr(nc, "c_ds", 16)
    c_gx = [Ctr(nc, f"c_gx{i}", 16) for i in range(4)]
    c_tr = [Ctr(nc, "c_tr0", 16), Ctr(nc, "c_tr1", 16)]
    c_id = [Ctr(nc, f"c_id{i}", 16) for i in range(4)]
    c_ws = Ctr(nc, "c_ws", 16)
    c_eo = [Ctr(nc, "c_eo0", 16), Ctr(nc, "c_eo1", 16)]
    c_eg = [Ctr(nc, "c_eg0", 16), Ctr(nc, "c_eg1", 16)]
    c_idc = Ctr(nc, "c_idc", 1)
    c_shw = [Ctr(nc, "c_shw0", 16), Ctr(nc, "c_shw1", 16)]
    c_ac = [Ctr(nc, "c_ac0", 16), Ctr(nc, "c_ac1", 16)]
    c_p = Ctr(nc, "c_p", 1)
    c_a = Ctr(nc, "c_a", 1)
    c_v = Ctr(nc, "c_v", 1)
    c_q = Ctr(nc, "c_q", 1)

    blk_cm = nc.Block()
    block = blk_cm.__enter__()

    def tw(eng):
        if c_tap.n:
            eng.wait_ge(c_tap.sem, c_tap.n)

    x0s = ExitStack()
    x_rows0 = x0s.enter_context(nc.sbuf_tensor("x_rows0", [128, NTC * C], BF16, side="right"))
    setup = ExitStack()
    sb2 = lambda name, shape, dt: setup.enter_context(nc.sbuf_tensor(name, shape, dt))
    ps2 = lambda name, shape, dt: setup.enter_context(nc.psum_tensor(name, shape, dt))
    caw_s = sb2("caw_s", [128, NKC * 3 * C], BF16)
    qw_s = sb2("qw_s", [128, NKC * C], BF16)
    kw_s = sb2("kw_s", [128, NKC * C], BF16)
    vw_s = sb2("vw_s", [128, NKC * C], BF16)
    cab_s = sb2("cab_s", [128, 24], BF16)
    qb_s = sb2("qb_s", [128, NKC], F32)
    kb_s = sb2("kb_s", [128, NKC], F32)
    vb_s = sb2("vb_s", [128, NKC], F32)
    cawT = sb2("cawT", [128, 24 * C], BF16)
    weff_st = sb2("weff_st", [128, NKC * C], BF16)
    pt = [ps2(f"pt{i}", [128, 1024], BF16) for i in range(2)]
    pw = [ps2(f"pw{i}", [128, 512], F32) for i in range(4)]
    pb8 = ps2("pb8", [128, NKC], F32)

    def g_setup(g):
        c_ld1.inc(g.dma_start(out=id16[:], in_=id16_in))
        c_ld1.inc(g.dma_start(out=_d3(caw_s, 3 * C), in_=_t3(caw)))
        c_g.inc(g.dma_start(out=_d3(qw_s, C), in_=_t3(qw)))
        c_g.inc(g.dma_start(out=_d3(kw_s, C), in_=_t3(kw)))
        c_g.inc(g.dma_start(out=_d3(vw_s, C), in_=_t3(vw)))
        c_g.inc(g.dma_start(out=cab_s[:], in_=cab.rearrange("(j p) -> p j", p=128), allow_slow_non_contiguous=True))
        c_g.inc(g.dma_start(out=qb_s[:], in_=qb.rearrange("(j p) -> p j", p=128), allow_slow_non_contiguous=True))
        c_g.inc(g.dma_start(out=kb_s[:], in_=kb.rearrange("(j p) -> p j", p=128), allow_slow_non_contiguous=True))
        c_g.inc(g.dma_start(out=vb_s[:], in_=vb.rearrange("(j p) -> p j", p=128), allow_slow_non_contiguous=True))
        c_g.inc(g.dma_start(out=obs[:], in_=ob.rearrange("(j p) -> p j", p=128), allow_slow_non_contiguous=True))
        c_g.inc(g.dma_start(out=id32[:], in_=id32_in))
        c_g.inc(g.dma_start(out=ones[:], in_=ones_in))
        c_g.inc(g.dma_start(out=sel[:], in_=sel_in))
        c_g.inc(g.dma_start(out=ltx[:], in_=ltx_in))
        c_g.inc(g.dma_start(out=iotac[:], in_=iota_in))
        c_g.inc(g.dma_start(out=masks[:], in_=masks_in))
        c_g.inc(g.dma_start(out=trashc[:], in_=trashc_in))
        c_x.inc(g.dma_start(
            out=_d3(x_rows0, C),
            in_=_t3(x_in[0:CHUNK, :])))
        c_g2.inc(g.dma_start(out=ow_h, in_=ow))
        wgf = wg_in.rearrange("e c m -> (e c) m")
        wuf = wu_in.rearrange("e c m -> (e c) m")
        wdf = wd_in.rearrange("e m c -> (e m) c")
        for e_ in range(NE):
            c_g2.inc(g.dma_start(
                out=wg_h[128 * e_:128 * (e_ + 1), :].rearrange("p (rc c) -> p rc c", c=MFF),
                in_=wgf[e_ * C:(e_ + 1) * C, :].rearrange("(rc p) c -> p rc c", p=128)))
            c_g2.inc(g.dma_start(
                out=wu_h[128 * e_:128 * (e_ + 1), :].rearrange("p (rc c) -> p rc c", c=MFF),
                in_=wuf[e_ * C:(e_ + 1) * C, :].rearrange("(rc p) c -> p rc c", p=128)))
            c_g2.inc(g.dma_start(
                out=wd_h[128 * e_:128 * (e_ + 1), :].rearrange("p (mc c) -> p mc c", c=C),
                in_=wdf[e_ * MFF:(e_ + 1) * MFF, :].rearrange("(mc p) c -> p mc c", p=128)))
        c_g2.inc(g.dma_start(
            out=wsg_h.rearrange("p (rc c) -> p rc c", c=MFF),
            in_=wsg_in.rearrange("(rc p) c -> p rc c", p=128)))
        c_g2.inc(g.dma_start(
            out=wsu_h.rearrange("p (rc c) -> p rc c", c=MFF),
            in_=wsu_in.rearrange("(rc p) c -> p rc c", p=128)))
        c_g2.inc(g.dma_start(
            out=wsd_h.rearrange("p (mc c) -> p mc c", c=C),
            in_=wsd_in.rearrange("(mc p) c -> p mc c", p=128)))
        for i in range(NCHUNK):
            c_g2.inc(g.dma_start(out=eo_h[i][NE * ECAP:NE * ECAP + 128, :], in_=zrows_in))
            c_g2.inc(g.dma_start(out=disp_h[i], in_=dinit_in))
            c_g2.inc(g.dma_start(out=x2r_h[i][CHUNK:CHUNK + 1, :], in_=zrow_in))
    block.gpsimd(g_setup)
    n_ld0, n_cast = 15 * 16, 58 * 16

    v_t0 = c_v.n
    p_t0 = c_p.n

    def pe_cawT(t):
        t.wait_ge(c_ld1.sem, 32)
        for mc in range(24):
            if mc >= 2:
                t.wait_ge(c_v.sem, v_t0 + mc - 1)
            for cc in range(NKC):
                mm = t.transpose(pt[mc % 2][:, 128 * cc:128 * (cc + 1)],
                                 caw_s[:, cc * 3 * C + 128 * mc: cc * 3 * C + 128 * (mc + 1)],
                                 id16[:])
            c_p.inc(mm)
    block.tensor(pe_cawT)

    def dv_cawT(v):
        for mc in range(24):
            v.wait_ge(c_p.sem, p_t0 + mc + 1)
            c_v.inc(v.tensor_copy(cawT[:, mc * C:(mc + 1) * C], pt[mc % 2][:]))
    block.vector(dv_cawT)

    wsrc = {0: qw_s, 1: kw_s, 2: vw_s}
    whbm = {0: wq_h, 1: wk_h, 2: wv_h}
    bsrc = {0: qb_s, 1: kb_s, 2: vb_s}
    bdst = {0: bqe, 1: bke, 2: bve}
    a_w0 = c_a.n
    v_w0 = c_v.n
    s_w0 = c_wf.n
    p_w0 = c_p.n

    def pe_weff(t):
        t.wait_ge(c_g.sem, n_ld0)
        t.wait_ge(c_v.sem, v_t0 + 24)
        nslot = 0
        for wi in range(3):
            if wi > 0:
                t.wait_ge(c_v.sem, v_w0 + (2 if wi == 1 else 3))
                t.wait_ge(c_wf.sem, s_w0 + 16 * wi)
            for cc in range(NKC):
                for fh in range(2):
                    if nslot >= 4:
                        t.wait_ge(c_a.sem, a_w0 + nslot - 3)
                    for mc in range(NKC):
                        mcg = wi * NKC + mc
                        mm = t.matmul(pw[nslot % 4][:],
                                      cawT[:, mcg * C + 128 * cc: mcg * C + 128 * (cc + 1)],
                                      wsrc[wi][:, mc * C + 512 * fh: mc * C + 512 * (fh + 1)],
                                      start=(mc == 0), stop=(mc == NKC - 1))
                    c_p.inc(mm)
                    nslot += 1
            for fc in range(NKC):
                for mc in range(NKC):
                    mcg = wi * NKC + mc
                    mm = t.matmul(pb8[:, fc:fc + 1],
                                  wsrc[wi][:, mc * C + 128 * fc: mc * C + 128 * (fc + 1)],
                                  cab_s[:, mcg:mcg + 1],
                                  start=(mc == 0), stop=(mc == NKC - 1))
            c_p.inc(mm)
    block.tensor(pe_weff)

    def act_weff(a):
        nslot = 0
        for wi in range(3):
            for cc in range(NKC):
                for fh in range(2):
                    nslot += 1
                    a.wait_ge(c_p.sem, p_w0 + nslot + wi)
                    c_a.inc(a.activation(
                        out=weff_st[:, cc * C + 512 * fh: cc * C + 512 * (fh + 1)],
                        in_=pw[(nslot - 1) % 4][:], func=AF.Copy))
    block.scalar(act_weff)

    def dv_weff(v):
        for wi in range(3):
            v.wait_ge(c_p.sem, p_w0 + (wi + 1) * 17)
            if wi == 0:
                c_v.inc(v.tensor_scalar_mul(qb_s[:], qb_s[:], 0.125))
                v.drain()
                c_v.inc(v.scalar_tensor_tensor(
                    out=bqe[:], in0=pb8[:], scalar=0.125, in1=qb_s[:],
                    op0=OP.mult, op1=OP.add))
            else:
                c_v.inc(v.tensor_tensor(out=bdst[wi][:], in0=pb8[:], in1=bsrc[wi][:],
                                        op=OP.add))
    block.vector(dv_weff)

    def sy_weff(s):
        for wi in range(3):
            s.wait_ge(c_a.sem, a_w0 + (wi + 1) * 16)
            c_wf.inc(s.dma_start(out=whbm[wi], in_=weff_st[:]))
    block.sync(sy_weff)
    n_weff = c_wf.n

    if "t_weff" in tap_t:
        def sy_tw(s):
            s.wait_ge(c_a.sem, c_a.n)
            c_tap.inc(s.dma_start(out=tap_t["t_weff"], in_=weff_st[:]))
        block.sync(sy_tw)

    setup.close()

    p_weff_end = c_p.n
    a_weff_end = c_a.n
    v_weff_end = c_v.n
    gws = ExitStack()
    gw_s = gws.enter_context(nc.sbuf_tensor("gw_s", [16, C], F32))
    gwhi = gws.enter_context(nc.sbuf_tensor("gwhi", [16, C], BF16))
    gwmd = gws.enter_context(nc.sbuf_tensor("gwmd", [16, C], BF16))
    gwlo = gws.enter_context(nc.sbuf_tensor("gwlo", [16, C], BF16))
    ptg = gws.enter_context(nc.psum_tensor("ptg", [128, 384], BF16))
    c_gw = Ctr(nc, "c_gw", 16)

    def g_gw(g):
        g.wait_ge(c_p.sem, p_weff_end)
        g.wait_ge(c_a.sem, a_weff_end)
        g.wait_ge(c_v.sem, v_weff_end)
        c_gw.inc(g.dma_start(out=gw_s[:], in_=gw_in))
    block.gpsimd(g_gw)

    def dv_gw(v):
        v.wait_ge(c_gw.sem, 16)
        v.wait_ge(c_a.sem, c_a.n)
        v.drain()
        v.tensor_copy(gwhi[:], gw_s[:])
        v.drain()
        v.tensor_tensor(out=gw_s[:], in0=gw_s[:], in1=gwhi[:], op=OP.subtract)
        v.drain()
        v.tensor_copy(gwmd[:], gw_s[:])
        v.drain()
        v.tensor_tensor(out=gw_s[:], in0=gw_s[:], in1=gwmd[:], op=OP.subtract)
        v.drain()
        c_v.inc(v.tensor_copy(gwlo[:], gw_s[:]))
    block.vector(dv_gw)
    v_gw = c_v.n

    def pe_gw(t):
        t.wait_ge(c_v.sem, v_gw)
        for hl, gsrc in enumerate((gwhi, gwmd, gwlo)):
            for cc in range(NKC):
                mm = t.transpose(ptg[:, 128 * hl + 16 * cc:128 * hl + 16 * (cc + 1)],
                                 gsrc[0:16, 128 * cc:128 * (cc + 1)], id16[0:16, 0:16])
        c_p.inc(mm)
    block.tensor(pe_gw)
    p_gw = c_p.n

    def dv_gwT(v):
        v.wait_ge(c_p.sem, p_gw)
        v.drain()
        last = None
        for hl in range(3):
            last = v.tensor_copy(
                gwT[:].rearrange("p (cc th) -> p cc th", th=48)[:, :, 16 * hl:16 * (hl + 1)],
                ptg[:, 128 * hl:128 * (hl + 1)].rearrange("p (cc s) -> p cc s", s=16))
        c_v.inc(last)
    block.vector(dv_gwT)
    n_gwT = c_v.n
    gws.close()

    prev_chunk_done = dict(v=c_v.n, o0=c_o[0].n, o1=c_o[1].n, p=c_p.n, a=c_a.n,
                           shw0=0, shw1=0, ds=0, x2r=0)
    ch_marks = []

    for ch in range(NCHUNK):
        chs = ExitStack()
        sbc = lambda name, shape, dt: chs.enter_context(
            nc.sbuf_tensor(f"{name}{ch}", shape, dt, side="right"))
        psc = lambda st_, name, shape, dt: st_.enter_context(nc.psum_tensor(f"{name}{ch}", shape, dt))

        xys = ExitStack()
        xT = xys.enter_context(nc.sbuf_tensor(f"xT{ch}", [128, NKC * CHUNK], BF16))
        Y = xys.enter_context(nc.sbuf_tensor(f"Y{ch}", [128, NKC * CHUNK], BF16))

        p1s = ExitStack()
        if ch == 0:
            x_rows = x_rows0
        else:
            x_rows = p1s.enter_context(
                nc.sbuf_tensor(f"x_rows{ch}", [128, NTC * C], BF16, side="right"))
        px = [psc(p1s, f"px{i}", [128, 2048], BF16) for i in range(2)]

        pcd = prev_chunk_done

        if ch > 0:
            def g_p1(g):
                g.wait_ge(c_v.sem, pcd["v"])
                g.wait_ge(c_shw[0].sem, pcd["shw0"])
                g.wait_ge(c_shw[1].sem, pcd["shw1"])
                g.wait_ge(c_ds.sem, pcd["ds"])
                g.wait_ge(c_x2r.sem, pcd["x2r"])
                c_x.inc(g.dma_start(
                    out=_d3(x_rows, C),
                    in_=_t3(x_in[ch * CHUNK:(ch + 1) * CHUNK, :])))
            block.gpsimd(g_p1)
        n_x = c_x.n

        p_b1 = c_p.n
        v_b1 = c_v.n

        def pe_p1(t):
            tw(t)
            t.wait_ge(c_x.sem, n_x)
            t.wait_ge(c_v.sem, pcd["v"])
            t.wait_ge(c_a.sem, pcd["a"])
            for cc in range(NKC):
                if cc >= 2:
                    t.wait_ge(c_v.sem, v_b1 + cc - 1)
                for tc in range(NTC):
                    mm = t.transpose(px[cc % 2][:, 128 * tc:128 * (tc + 1)],
                                     x_rows[:, tc * C + 128 * cc: tc * C + 128 * (cc + 1)],
                                     id16[:])
                c_p.inc(mm)
        block.tensor(pe_p1)

        def dv_p1(v):
            for cc in range(NKC):
                v.wait_ge(c_p.sem, p_b1 + cc + 1)
                c_v.inc(v.tensor_copy(xT[:, cc * CHUNK:(cc + 1) * CHUNK], px[cc % 2][:]))
        block.vector(dv_p1)
        n_xT = c_v.n
        p1s.close()
        if ch == 0:
            x0s.close()

        if "t_xT" in tap_t and ch == 0:
            def sy_t1(s):
                s.wait_ge(c_v.sem, n_xT)
                c_tap.inc(s.dma_start(out=tap_t["t_xT"], in_=xT[:]))
            block.sync(sy_t1)

        att = ExitStack()
        sba = lambda name, shape, dt: att.enter_context(nc.sbuf_tensor(f"{name}{ch}", shape, dt))
        vpl = sba("vpl", [128, NTC * C], BF16)
        vsr = sba("vsr", [128, NTC * C], BF16)
        y_pair = sba("y_pair", [128, CHUNK], F32)
        s_tot = sba("s_tot", [128, CHUNK], F32)
        p_bufs2 = [[sba(f"p_buf{up}_{hh}", [128, 1536], BF16) for hh in range(2)]
                   for up in range(2)]

        p2s = ExitStack()
        wvb = p2s.enter_context(nc.sbuf_tensor(f"wvb{ch}", [128, NKC * C], BF16))
        pv = [psc(p2s, f"pv{i}", [128, 512], F32) for i in range(4)]

        def sy_p2(s):
            s.wait_ge(c_wf.sem, n_weff)
            s.wait_ge(c_v.sem, n_xT)
            s.wait_ge(c_v.sem, pcd["v"])
            s.wait_ge(c_shw[0].sem, pcd["shw0"])
            s.wait_ge(c_shw[1].sem, pcd["shw1"])
            c_wv.inc(s.dma_start(out=wvb[:], in_=wv_h))
        block.sync(sy_p2)
        n_wv = c_wv.n

        p_b2 = c_p.n
        v_b2 = c_v.n

        def pe_p2(t):
            tw(t)
            t.wait_ge(c_wv.sem, n_wv)
            t.wait_ge(c_v.sem, n_xT)
            nslot = 0
            for form in range(2):
                for tc in range(NTC):
                    for fh in range(2):
                        if nslot >= 4:
                            t.wait_ge(c_v.sem, v_b2 + nslot - 3)
                        for cc in range(NKC):
                            if form == 0:
                                st_ap = xT[:, cc * CHUNK + 128 * tc: cc * CHUNK + 128 * (tc + 1)]
                            else:
                                cls, sblk = tc // 4, tc % 4
                                b0 = cc * CHUNK + cls + 512 * sblk
                                st_ap = xT[:, b0: b0 + 509: 4]
                            mm = t.matmul(pv[nslot % 4][:], st_ap,
                                          wvb[:, cc * C + 512 * fh: cc * C + 512 * (fh + 1)],
                                          start=(cc == 0), stop=(cc == NKC - 1))
                        c_p.inc(mm)
                        nslot += 1
        block.tensor(pe_p2)

        def dv_p2(v):
            nslot = 0
            for form in range(2):
                dst = vpl if form == 0 else vsr
                for tc in range(NTC):
                    for fh in range(2):
                        v.wait_ge(c_p.sem, p_b2 + nslot + 1)
                        c_v.inc(v.tensor_copy(
                            dst[:, tc * C + 512 * fh: tc * C + 512 * (fh + 1)],
                            pv[nslot % 4][:]))
                        nslot += 1
        block.vector(dv_p2)
        n_v = c_v.n
        p2s.close()

        if ch == 0:
            for nm, buf in (("t_vpl", vpl), ("t_vsr", vsr)):
                if nm in tap_t:
                    def sy_t2(s, nm=nm, buf=buf):
                        s.wait_ge(c_v.sem, n_v)
                        c_tap.inc(s.dma_start(out=tap_t[nm], in_=buf[:]))
                    block.sync(sy_t2)

        q_bufs = [sba(f"q_buf{i}", [128, CHUNK], BF16) for i in range(2)]
        k_bufs = [sba(f"k_buf{i}", [128, CHUNK], BF16) for i in range(2)]
        wqkb2 = [sba(f"wqkb{i}", [128, 2 * NKC * 128], BF16) for i in range(2)]
        p3ps = ExitStack()
        pqk = [psc(p3ps, f"pqk{i}", [128, 512], F32) for i in range(2)]
        SAb = [psc(p3ps, f"SA{hh}", [128, 1024], F32) for hh in range(2)]
        ob = psc(p3ps, "ob", [128, 512], F32)
        sb_ = psc(p3ps, "sb", [128, 512], F32)

        qk_pe = {}
        qk_dv = {}
        wq_marks = {}

        def sy_qk(s, pair):
            if pair == 0:
                s.wait_ge(c_wf.sem, n_weff)
                s.wait_ge(c_p.sem, p_b2 + 64)
            if pair >= 2:
                s.wait_ge(c_p.sem, qk_pe[(pair - 2, 7)])
            cw = c_wq[pair % 2]
            cw.inc(s.dma_start(
                out=wqkb2[pair % 2][:, 0:NKC * 128].rearrange("p (cc f) -> p cc f", f=128),
                in_=wq_h.rearrange("p (cc f) -> p cc f", f=C)[:, :, 128 * pair:128 * (pair + 1)]))
            cw.inc(s.dma_start(
                out=wqkb2[pair % 2][:, NKC * 128:].rearrange("p (cc f) -> p cc f", f=128),
                in_=wk_h.rearrange("p (cc f) -> p cc f", f=C)[:, :, 128 * pair:128 * (pair + 1)]))
            wq_marks[pair] = (pair % 2, cw.n)

        def pe_qk_slot(t, pair, slot):
            t.wait_ge(c_wq[wq_marks[pair][0]].sem, wq_marks[pair][1])
            if (pair, slot - 2) in qk_dv:
                t.wait_ge(c_a.sem, qk_dv[(pair, slot - 2)])
            elif slot < 2 and (pair - 1, 6 + slot) in qk_dv:
                t.wait_ge(c_a.sem, qk_dv[(pair - 1, 6 + slot)])
            which, tt = slot // 4, slot % 4
            for cc in range(NKC):
                mm = t.matmul(
                    pqk[slot % 2][:],
                    wqkb2[pair % 2][:, which * NKC * 128 + 128 * cc:
                                    which * NKC * 128 + 128 * (cc + 1)],
                    xT[:, cc * CHUNK + 512 * tt: cc * CHUNK + 512 * (tt + 1)],
                    start=(cc == 0), stop=(cc == NKC - 1))
            n_ = c_p.inc(mm)
            if (pair, slot) in qk_pe:
                assert n_ == qk_pe[(pair, slot)], (pair, slot, n_, qk_pe[(pair, slot)])
            else:
                qk_pe[(pair, slot)] = n_

        def act_qk_slot(a, pair, slot):
            a.wait_ge(c_p.sem, qk_pe[(pair, slot)])
            which, tt = slot // 4, slot % 4
            if which == 0:
                op = a.activation(
                    out=q_bufs[pair % 2][:, 512 * tt:512 * (tt + 1)],
                    in_=pqk[slot % 2][:], func=AF.Identity, scale=0.125,
                    bias=bqe[:, pair:pair + 1])
            else:
                op = a.activation(
                    out=k_bufs[pair % 2][:, 512 * tt:512 * (tt + 1)],
                    in_=pqk[slot % 2][:], func=AF.Identity,
                    bias=bke[:, pair:pair + 1])
            n_ = c_a.inc(op)
            if (pair, slot) in qk_dv:
                assert n_ == qk_dv[(pair, slot)], (pair, slot, n_, qk_dv[(pair, slot)])
            else:
                qk_dv[(pair, slot)] = n_

        def sy_qk0(s):
            sy_qk(s, 0)
        block.sync(sy_qk0)

        for s_ in range(8):
            qk_dv[(0, s_)] = c_a.n + s_ + 1

        def pe_qk0(t):
            t.wait_ge(c_v.sem, n_v)
            for slot in range(8):
                pe_qk_slot(t, 0, slot)
        block.tensor(pe_qk0)

        def act_qk0(a):
            for slot in range(8):
                act_qk_slot(a, 0, slot)
        block.scalar(act_qk0)

        a_marks = {}
        v_marks = {}
        pe_marks = {}
        n_Y_v = None

        nA = c_a.n
        nV = c_v.n
        nP = c_p.n
        for pair in range(NPAIR):
            has_next = pair + 1 < NPAIR
            for u in range(7):
                for hh in range(2):
                    for h in range(2):
                        nA += 1
                        a_marks[(pair, u, hh, h)] = nA
                if has_next and u < 7:
                    nA += 1
                    qk_dv[(pair + 1, u)] = nA
            if has_next:
                nA += 1
                qk_dv[(pair + 1, 7)] = nA
            for u in range(7):
                nV += 1
                v_marks[(pair, "m", u)] = nV
                nV += 1
                v_marks[(pair, "y", u)] = nV
            nV += 2
            v_marks[(pair, "Y")] = nV
            for u in range(7):
                if u >= 1:
                    nP += 1
                    pe_marks[(pair, "os", u - 1)] = nP
                for hh in range(2):
                    for h in range(2):
                        nP += 1
                        pe_marks[(pair, u, hh, h)] = nP
                if has_next and u < 7:
                    nP += 1
                    qk_pe[(pair + 1, u)] = nP
            nP += 1
            pe_marks[(pair, "os", 6)] = nP
            if has_next:
                nP += 1
                qk_pe[(pair + 1, 7)] = nP

        for pair in range(NPAIR):
            if pair + 1 < NPAIR:
                def sy_qkn(s, pair=pair):
                    sy_qk(s, pair + 1)
                block.sync(sy_qkn)

            scheds = []
            for (br, seg) in unit_list():
                scheds.append((br, seg,
                               [unit_rects(br, seg, 2 * pair + hh) for hh in range(2)]))

            def pe_pair(t, pair=pair, scheds=scheds):
                if pair == 0:
                    t.wait_ge(c_v.sem, n_v)
                t.wait_ge(c_a.sem, qk_dv[(pair, 7)])
                for u, (br, seg, rr) in enumerate(scheds):
                    if u >= 1:
                        emit_os(t, pair, u - 1, scheds[u - 1])
                    for hh in range(2):
                        rb = 64 * hh
                        for h in range(2):
                            if h == 0:
                                key = (pair, u - 1, hh, 1) if u > 0 else (pair - 1, 6, hh, 1)
                                if key in a_marks:
                                    t.wait_ge(c_a.sem, a_marks[key])
                            else:
                                t.wait_ge(c_a.sem, a_marks[(pair, u, hh, 0)])
                            rects = [r for r in rr[hh] if r["h"] == h]
                            base_slot = 0 if h == 0 else 2
                            for k_, r in enumerate(rects):
                                slot = base_slot + k_
                                kst, kstep = r["k"]
                                qst, qstep = r["q"]
                                sa_col = 256 * (slot - base_slot) if h == 1 else 256 * slot
                                mm = t.matmul(
                                    SAb[hh][:, sa_col:sa_col + 256],
                                    k_bufs[pair % 2][rb:rb + 64, kst: kst + 127 * kstep + 1: kstep],
                                    q_bufs[pair % 2][rb:rb + 64, qst: qst + 255 * qstep + 1: qstep],
                                    start=True, stop=True, tile_position=(rb, 0))
                            assert c_p.inc(mm) == pe_marks[(pair, u, hh, h)]
                    if pair + 1 < NPAIR and u < 7:
                        pe_qk_slot(t, pair + 1, u)
                emit_os(t, pair, 6, scheds[6])
                if pair + 1 < NPAIR:
                    pe_qk_slot(t, pair + 1, 7)

            def emit_os(t, pair, u, sched):
                br, seg, rr = sched
                p_bufs = p_bufs2[(7 * pair + u) % 2]
                t.wait_ge(c_v.sem, v_marks[(pair, "m", u)])
                key = (pair, "y", u - 1) if u > 0 else (pair - 1, "y", 6)
                if key in v_marks:
                    t.wait_ge(c_v.sem, v_marks[key])
                for hh in range(2):
                    n_r = len(rr[hh])
                    tp = (0, 64 * hh)
                    for slot, r in enumerate(rr[hh]):
                        vt, srt = r["v"]
                        vb_ap = (vsr if srt else vpl)[:, vt * C + 128 * pair + 64 * hh:
                                                      vt * C + 128 * pair + 64 * (hh + 1)]
                        st = r["start"]
                        mv = p_bufs[hh][:, 256 * slot + st:256 * (slot + 1)]
                        t.matmul(ob[64 * hh:64 * hh + 64, 256 * r["h"] + st:256 * (r["h"] + 1)],
                                 vb_ap, mv, start=(slot == 0), stop=(slot == n_r - 1),
                                 tile_position=tp, skip_group_check=True)
                    for slot, r in enumerate(rr[hh]):
                        st = r["start"]
                        mv = p_bufs[hh][:, 256 * slot + st:256 * (slot + 1)]
                        mm = t.matmul(sb_[64 * hh:64 * hh + 64, 256 * r["h"] + st:256 * (r["h"] + 1)],
                                      ones[:, 0:64], mv,
                                      start=(slot == 0), stop=(slot == n_r - 1),
                                      tile_position=tp, skip_group_check=True)
                assert c_p.inc(mm) == pe_marks[(pair, "os", u)]

            def act_pair(a, pair=pair):
                for u in range(7):
                    g_ = 7 * pair + u
                    p_bufs = p_bufs2[g_ % 2]
                    pg2 = g_ - 2
                    okey = None
                    if pg2 >= 0:
                        okey = (pg2 // 7, "os", pg2 % 7)
                    for hh in range(2):
                        a.wait_ge(c_p.sem, pe_marks[(pair, u, hh, 0)])
                        if okey is not None and okey in pe_marks:
                            a.wait_ge(c_p.sem, pe_marks[okey])
                        c_a.inc(a.activation(out=p_bufs[hh][:, 0:512],
                                             in_=SAb[hh][:, 0:512], func=AF.Exp))
                        assert c_a.n == a_marks[(pair, u, hh, 0)]
                        a.wait_ge(c_p.sem, pe_marks[(pair, u, hh, 1)])
                        c_a.inc(a.activation(out=p_bufs[hh][:, 512:1536],
                                             in_=SAb[hh][:, 0:1024], func=AF.Exp))
                        assert c_a.n == a_marks[(pair, u, hh, 1)]
                    if pair + 1 < NPAIR and u < 7:
                        act_qk_slot(a, pair + 1, u)
                if pair + 1 < NPAIR:
                    act_qk_slot(a, pair + 1, 7)

            def dv_pair(v, pair=pair, scheds=scheds):
                for u, (br, seg, rr) in enumerate(scheds):
                    last = None
                    p_bufs = p_bufs2[(7 * pair + u) % 2]
                    for hh in range(2):
                        v.wait_ge(c_a.sem, a_marks[(pair, u, hh, 1)])
                        for slot, r in enumerate(rr[hh]):
                            for (mid, loc) in r["masks"]:
                                last = v.tensor_tensor(
                                    out=p_bufs[hh][:, 256 * slot + loc:256 * slot + loc + 128],
                                    in0=p_bufs[hh][:, 256 * slot + loc:256 * slot + loc + 128],
                                    in1=masks[:, 128 * mid:128 * (mid + 1)], op=OP.mult)
                    assert c_v.inc(last) == v_marks[(pair, "m", u)]
                    v.wait_ge(c_p.sem, pe_marks[(pair, "os", u)])
                    v.drain()
                    hd0 = 2 * pair
                    for hh in range(2):
                        yst, ystep = unit_ycols(br, seg, hd0 + hh)
                        rb = 64 * hh
                        ydst = y_pair[rb:rb + 64, yst: yst + 511 * ystep + 1: ystep]
                        sdst = s_tot[rb:rb + 64, yst: yst + 511 * ystep + 1: ystep]
                        if br == "b1":
                            v.tensor_copy(ydst, ob[rb:rb + 64, :])
                            last = v.tensor_copy(sdst, sb_[rb:rb + 64, :])
                        else:
                            v.tensor_tensor(out=ydst, in0=ydst, in1=ob[rb:rb + 64, :], op=OP.add)
                            last = v.tensor_tensor(out=sdst, in0=sdst,
                                                   in1=sb_[rb:rb + 64, :], op=OP.add)
                    assert c_v.inc(last) == v_marks[(pair, "y", u)]
                v.drain()
                c_v.inc(v.reciprocal(s_tot[:], s_tot[:]))
                v.drain()
                v.tensor_tensor(out=y_pair[:], in0=y_pair[:], in1=s_tot[:], op=OP.mult)
                v.drain()
                c_v.inc(v.tensor_scalar(
                    out=Y[:, pair * CHUNK:(pair + 1) * CHUNK], in0=y_pair[:],
                    scalar1=bve[:, pair:pair + 1], scalar2=None, op0=OP.add))
                assert c_v.n == v_marks[(pair, "Y")]

            block.tensor(pe_pair)
            block.scalar(act_pair)
            block.vector(dv_pair)

            if ch == 0 and pair == 0:
                nyv = c_v.n
                for nm, buf in (("t_q", q_bufs[0]), ("t_k", k_bufs[0]),
                                ("t_y", y_pair), ("t_s", s_tot)):
                    if nm in tap_t:
                        def sy_t4(s, nm=nm, buf=buf, nyv=nyv):
                            s.wait_ge(c_v.sem, nyv)
                            c_tap.inc(s.dma_start(out=tap_t[nm], in_=buf[:]))
                        block.sync(sy_t4)

        p3ps.close()
        n_Y_v = c_v.n
        n_att_v = c_v.n
        att.close()

        if "t_Y" in tap_t and ch == 0:
            def sy_t5(s):
                s.wait_ge(c_v.sem, n_Y_v)
                c_tap.inc(s.dma_start(out=tap_t["t_Y"], in_=Y[:]))
            block.sync(sy_t5)

        x2 = sbc("x2", [128, NKC * CHUNK], BF16)
        meanb = sbc("meanb", [128, CHUNK], F32)
        rstd = sbc("rstd", [128, CHUNK], F32)
        sq2 = sbc("sq2", [128, 2 * 512], BF16)
        r1s = ExitStack()
        r1b = r1s.enter_context(nc.sbuf_tensor(f"r1b{ch}", [128, NKC * CHUNK], BF16))

        p4s = ExitStack()
        owb = p4s.enter_context(nc.sbuf_tensor(f"owb{ch}", [128, NKC * C], BF16))
        po4 = [psc(p4s, f"po4{i}", [128, 512], F32) for i in range(4)]

        def sy_p4(s):
            s.wait_ge(c_g2.sem, n_cast)
            s.wait_ge(c_v.sem, n_Y_v)
            c_ow.inc(s.dma_start(out=_d3(owb, C), in_=_t3(ow_h)))
        block.sync(sy_p4)
        n_ow = c_ow.n

        p_b4 = c_p.n
        v_b4 = c_v.n

        def pe_p4(t):
            tw(t)
            t.wait_ge(c_ow.sem, n_ow)
            t.wait_ge(c_v.sem, n_Y_v)
            nslot = 0
            for fc in range(NKC):
                for tt in range(4):
                    if nslot >= 4:
                        t.wait_ge(c_v.sem, v_b4 + nslot - 3)
                    for cc in range(NKC):
                        mm = t.matmul(po4[nslot % 4][:],
                                      owb[:, cc * C + 128 * fc: cc * C + 128 * (fc + 1)],
                                      Y[:, cc * CHUNK + 512 * tt: cc * CHUNK + 512 * (tt + 1)],
                                      start=(cc == 0), stop=(cc == NKC - 1))
                    c_p.inc(mm)
                    nslot += 1
        block.tensor(pe_p4)

        def dv_p4(v):
            nslot = 0
            for fc in range(NKC):
                for tt in range(4):
                    v.wait_ge(c_p.sem, p_b4 + nslot + 1)
                    c_v.inc(v.scalar_tensor_tensor(
                        out=r1b[:, fc * CHUNK + 512 * tt: fc * CHUNK + 512 * (tt + 1)],
                        in0=po4[nslot % 4][:], scalar=obs[:, fc:fc + 1],
                        in1=xT[:, fc * CHUNK + 512 * tt: fc * CHUNK + 512 * (tt + 1)],
                        op0=OP.add, op1=OP.add))
                    nslot += 1
        block.vector(dv_p4)
        p4s.close()
        if "t_r1" in tap_t and ch == 0:
            def sy_t6(s, n=c_v.n):
                s.wait_ge(c_v.sem, n)
                c_tap.inc(s.dma_start(out=tap_t["t_r1"], in_=r1b[:]))
            block.sync(sy_t6)

        _ln_id = [0]

        def ln_emit(srcb, dst):
            lns = ExitStack()
            _ln_id[0] += 1
            pm = [psc(lns, f"pm{_ln_id[0]}_{i}", [128, 512], F32) for i in range(4)]
            v_r = c_v.n
            p_l0 = c_p.n

            def pe_ln(t):
                t.wait_ge(c_v.sem, v_r)
                for tt in range(4):
                    for cc in range(NKC):
                        mm = t.matmul(pm[tt][:], ones[:],
                                      srcb[:, cc * CHUNK + 512 * tt: cc * CHUNK + 512 * (tt + 1)],
                                      start=(cc == 0), stop=(cc == NKC - 1))
                    c_p.inc(mm)
            block.tensor(pe_ln)

            def act_ln(a):
                for tt in range(4):
                    a.wait_ge(c_p.sem, p_l0 + tt + 1)
                    c_a.inc(a.activation(out=meanb[:, 512 * tt:512 * (tt + 1)],
                                         in_=pm[tt][:], func=AF.Copy, scale=1.0 / C))
            block.scalar(act_ln)
            a_mean = c_a.n

            v_c0 = c_v.n

            def dv_lnc(v):
                v.wait_ge(c_a.sem, a_mean)
                v.drain()
                last = None
                for cc in range(NKC):
                    last = v.tensor_tensor(out=srcb[:, cc * CHUNK:(cc + 1) * CHUNK],
                                           in0=srcb[:, cc * CHUNK:(cc + 1) * CHUNK],
                                           in1=meanb[:], op=OP.subtract)
                c_v.inc(last)
            block.vector(dv_lnc)
            v_cent = c_v.n

            p_l1 = c_p.n
            a_l1 = c_a.n

            def act_lnsq(a):
                a.wait_ge(c_v.sem, v_cent)
                k = 0
                for tt in range(4):
                    for cc in range(NKC):
                        if k >= 2:
                            a.wait_ge(c_p.sem, p_l1 + k - 1)
                        sl = srcb[:, cc * CHUNK + 512 * tt: cc * CHUNK + 512 * (tt + 1)]
                        c_a.inc(a.activation(
                            out=sq2[:, 512 * (k % 2):512 * (k % 2 + 1)],
                            in_=sl, func=AF.Square))
                        k += 1
            block.scalar(act_lnsq)

            def pe_ln2(t):
                k = 0
                for tt in range(4):
                    for cc in range(NKC):
                        t.wait_ge(c_a.sem, a_l1 + k + 1)
                        mm = t.matmul(pm[tt][:], ones[:],
                                      sq2[:, 512 * (k % 2):512 * (k % 2 + 1)],
                                      start=(cc == 0), stop=(cc == NKC - 1))
                        c_p.inc(mm)
                        k += 1
            block.tensor(pe_ln2)

            def act_ln2(a):
                for tt in range(4):
                    a.wait_ge(c_p.sem, p_l1 + (tt + 1) * NKC)
                    c_a.inc(a.activation(out=rstd[:, 512 * tt:512 * (tt + 1)],
                                         in_=pm[tt][:], func=AF.Copy, scale=1.0 / C,
                                         bias=EPS))
            block.scalar(act_ln2)
            a_var = c_a.n

            def dv_lnr(v):
                v.wait_ge(c_a.sem, a_var)
                c_v.inc(v.reciprocal(rstd[:], rstd[:]))
            block.vector(dv_lnr)
            v_rcp = c_v.n

            def act_ln3(a):
                a.wait_ge(c_v.sem, v_rcp)
                c_a.inc(a.activation(out=rstd[:], in_=rstd[:], func=AF.Sqrt))
            block.scalar(act_ln3)
            a_rstd = c_a.n

            def dv_ln3(v):
                v.wait_ge(c_a.sem, a_rstd)
                v.drain()
                last = None
                for cc in range(NKC):
                    last = v.tensor_tensor(out=dst[:, cc * CHUNK:(cc + 1) * CHUNK],
                                           in0=srcb[:, cc * CHUNK:(cc + 1) * CHUNK],
                                           in1=rstd[:], op=OP.mult)
                c_v.inc(last)
            block.vector(dv_ln3)
            lns.close()

        ln_emit(r1b, x2)
        r1s.close()
        xys.close()
        n_x2 = c_v.n
        if "t_x2" in tap_t and ch == 0:
            def sy_t7(s):
                s.wait_ge(c_v.sem, n_x2)
                c_tap.inc(s.dma_start(out=tap_t["t_x2"], in_=x2[:]))
            block.sync(sy_t7)

        seq1 = sbc("seq1", [128, NTC * 16], BF16)
        seq2 = sbc("seq2", [128, NTC * 16], BF16)
        sw1 = sbc("sw1", [128, NTC], F32)
        sw2 = sbc("sw2", [128, NTC], F32)
        x2rows = sbc("x2rows", [128, NTC * C], BF16)

        p5a = ExitStack()
        px2 = [psc(p5a, f"px2_{i}", [128, 1024], BF16) for i in range(2)]
        p_b5a = c_p.n
        a_b5a = c_a.n

        def pe_x2r(t):
            t.wait_ge(c_v.sem, n_x2)
            for tc in range(NTC):
                if tc >= 2:
                    t.wait_ge(c_a.sem, a_b5a + tc - 1)
                for cc in range(NKC):
                    mm = t.transpose(px2[tc % 2][:, 128 * cc:128 * (cc + 1)],
                                     x2[:, cc * CHUNK + 128 * tc: cc * CHUNK + 128 * (tc + 1)],
                                     id16[:])
                c_p.inc(mm)
        block.tensor(pe_x2r)

        def act_x2r(a):
            for tc in range(NTC):
                a.wait_ge(c_p.sem, p_b5a + tc + 1)
                c_a.inc(a.activation(out=x2rows[:, tc * C:(tc + 1) * C],
                                     in_=px2[tc % 2][:], func=AF.Identity))
        block.scalar(act_x2r)
        n_x2r_a = c_a.n

        def sy_x2r(s):
            if ch == 0:
                s.wait_ge(c_g2.sem, n_cast)
            for tc in range(NTC):
                s.wait_ge(c_a.sem, a_b5a + tc + 1)
                c_x2r.inc(s.dma_start(out=x2r_h[ch][128 * tc:128 * (tc + 1), :],
                                      in_=x2rows[:, tc * C:(tc + 1) * C]))
        block.sync(sy_x2r)
        n_x2rw = c_x2r.n
        p5a.close()

        p5s = ExitStack()
        sc_rows = sbc("sc_rows", [128, NTC * 16], F32)
        sc48 = sbc("sc48", [128, NTC * 48], F32)
        rs2 = sbc("rs2", [128, 16], F32)
        rm1 = sbc("rm1", [128, 16], F32)
        rm2 = sbc("rm2", [128, 16], F32)
        rwt = sbc("rwt", [128, 16], F32)
        sg1 = sbc("sg1", [128, 16], F32)
        sg2 = sbc("sg2", [128, 16], F32)
        plg = psc(p5s, "plg", [128, NTC * 64], F32)

        def pe_p5(t):
            tw(t)
            t.wait_ge(c_a.sem, n_x2r_a)
            t.wait_ge(c_v.sem, n_gwT)
            for tc in range(NTC):
                for cc in range(NKC):
                    mm = t.matmul(plg[:, 64 * tc:64 * tc + 48],
                                  x2[:, cc * CHUNK + 128 * tc: cc * CHUNK + 128 * (tc + 1)],
                                  gwT[:, 48 * cc:48 * (cc + 1)],
                                  start=(cc == 0), stop=(cc == NKC - 1))
                c_p.inc(mm)
        block.tensor(pe_p5)
        p_lg = c_p.n

        def act_p5(a):
            a.wait_ge(c_p.sem, p_lg)
            c_a.inc(a.activation(
                out=sc48[:].rearrange("p (tc s) -> p tc s", s=48),
                in_=plg[:].rearrange("p (tc s) -> p tc s", s=64)[:, :, 0:48],
                func=AF.Identity))
        block.scalar(act_p5)
        a_sc = c_a.n

        def dv_p5(v):
            v.wait_ge(c_a.sem, a_sc)
            for tc in range(NTC):
                sc = sc_rows[:, 16 * tc:16 * (tc + 1)]
                v.drain()
                v.tensor_tensor(out=sc, in0=sc48[:, 48 * tc:48 * tc + 16],
                                in1=sc48[:, 48 * tc + 16:48 * tc + 32], op=OP.add)
                v.drain()
                v.tensor_tensor(out=sc, in0=sc,
                                in1=sc48[:, 48 * tc + 32:48 * tc + 48], op=OP.add)
                v.drain()
                v.reduce_max(rm1[:, tc:tc + 1], sc, axis=mybir.AxisListType.X)
                v.drain()
                v.tensor_scalar(out=seq1[:, 16 * tc:16 * (tc + 1)], in0=sc,
                                scalar1=rm1[:, tc:tc + 1],
                                scalar2=None, op0=OP.is_equal)
                v.drain()
                v.scalar_tensor_tensor(out=rs2[:], in0=seq1[:, 16 * tc:16 * (tc + 1)],
                                       scalar=-1e9, in1=sc, op0=OP.mult, op1=OP.add)
                v.drain()
                v.reduce_max(rm2[:, tc:tc + 1], rs2[:], axis=mybir.AxisListType.X)
                v.drain()
                v.tensor_scalar(out=seq2[:, 16 * tc:16 * (tc + 1)], in0=rs2[:],
                                scalar1=rm2[:, tc:tc + 1],
                                scalar2=None, op0=OP.is_equal)
            v.drain()
            c_v.inc(v.tensor_copy(rwt[:], rm2[:]))
        block.vector(dv_p5)
        v_max5 = c_v.n

        def act_p5b(a):
            a.wait_ge(c_v.sem, v_max5)
            a.activation(out=sg1[:], in_=rm1[:], func=AF.Sigmoid)
            c_a.inc(a.activation(out=sg2[:], in_=rm2[:], func=AF.Sigmoid))
        block.scalar(act_p5b)
        a_sig5 = c_a.n

        def dv_p5c(v):
            v.wait_ge(c_a.sem, a_sig5)
            v.drain()
            v.tensor_tensor(out=rwt[:], in0=sg1[:], in1=sg2[:], op=OP.add)
            v.drain()
            v.tensor_scalar_add(rwt[:], rwt[:], 1e-20)
            v.drain()
            v.reciprocal(rwt[:], rwt[:])
            v.drain()
            v.tensor_scalar_mul(rwt[:], rwt[:], 2.5)
            v.drain()
            v.tensor_tensor(out=sw1[:], in0=sg1[:], in1=rwt[:], op=OP.mult)
            last = v.tensor_tensor(out=sw2[:], in0=sg2[:], in1=rwt[:], op=OP.mult)
            c_v.inc(last)
        block.vector(dv_p5c)
        n_route = c_v.n
        p5s.close()

        Dk = [sbc(f"Dk{k}", [128, NTC], F32) for k in range(2)]
        Dki = Dki2[ch]
        wbuf = [sbc(f"wbuf{k}", [128, 2 * NTC], F32) for k in range(2)]
        cpre = sbc("cpre", [128, 16], F32)
        cpre2 = sbc("cpre2", [128, 16], F32)
        rnk = sbc("rnk", [128, 16], F32)
        tot1 = sbc("tot1", [128, 16], F32)
        p5b = ExitStack()
        pdp = [psc(p5b, f"pdp{i}", [128, 256], F32) for i in range(2)]

        p_b5b = c_p.n
        v_b5b = c_v.n

        def pe_p5b(t):
            t.wait_ge(c_v.sem, n_route)
            for k in range(2):
                if k == 1:
                    t.wait_ge(c_v.sem, v_b5b + 1)
                sq = seq1 if k == 0 else seq2
                for e in range(NE):
                    mm = t.matmul(pdp[0][:, 16 * e:16 * (e + 1)], ltx[:],
                                  sq[:, e: e + 16 * (NTC - 1) + 1: 16],
                                  start=True, stop=True)
                    mm = t.matmul(pdp[1][:, 16 * e:16 * (e + 1)], ones[:],
                                  sq[:, e: e + 16 * (NTC - 1) + 1: 16],
                                  start=True, stop=True)
                c_p.inc(mm)
        block.tensor(pe_p5b)

        def dv_p5b(v):
            for k in range(2):
                v.wait_ge(c_p.sem, p_b5b + k + 1)
                v.memset(Dk[k][:], 0.0)
                for e in range(NE):
                    pp = pdp[0][:, 16 * e:16 * (e + 1)]
                    cs = pdp[1][:, 16 * e:16 * (e + 1)]
                    v.drain()
                    v.memset(cpre[:, 0:1], 0.0)
                    v.tensor_copy(cpre[:, 1:16], cs[:, 0:15])
                    v.drain()
                    v.tensor_copy(cpre2[:, 0:1], cpre[:, 0:1])
                    v.tensor_tensor(out=cpre2[:, 1:16], in0=cpre[:, 1:16],
                                    in1=cpre[:, 0:15], op=OP.add)
                    v.drain()
                    v.tensor_copy(cpre[:, 0:2], cpre2[:, 0:2])
                    v.tensor_tensor(out=cpre[:, 2:16], in0=cpre2[:, 2:16],
                                    in1=cpre2[:, 0:14], op=OP.add)
                    v.drain()
                    v.tensor_copy(cpre2[:, 0:4], cpre[:, 0:4])
                    v.tensor_tensor(out=cpre2[:, 4:16], in0=cpre[:, 4:16],
                                    in1=cpre[:, 0:12], op=OP.add)
                    v.drain()
                    v.tensor_copy(cpre[:, 0:8], cpre2[:, 0:8])
                    v.tensor_tensor(out=cpre[:, 8:16], in0=cpre2[:, 8:16],
                                    in1=cpre2[:, 0:8], op=OP.add)
                    v.drain()
                    if k == 0:
                        v.tensor_tensor(out=tot1[:, e:e + 1], in0=cpre[:, 15:16],
                                        in1=cs[:, 15:16], op=OP.add)
                        v.drain()
                    v.tensor_tensor(out=rnk[:], in0=pp, in1=cpre[:], op=OP.add)
                    v.drain()
                    if k == 0:
                        v.tensor_scalar_add(rnk[:], rnk[:], float(ECAP * e))
                    else:
                        v.tensor_scalar(out=rnk[:], in0=rnk[:], scalar1=tot1[:, e:e + 1],
                                        scalar2=float(ECAP * e), op0=OP.add, op1=OP.add)
                    v.drain()
                    v.tensor_scalar(out=rs1[:] if False else cpre[:], in0=rnk[:],
                                    scalar1=float(ECAP * (e + 1)), scalar2=None,
                                    op0=OP.is_lt)
                    v.drain()
                    tsl = trashc[:, 16 * k:16 * (k + 1)]
                    v.tensor_tensor(out=rnk[:], in0=rnk[:], in1=tsl, op=OP.subtract)
                    v.drain()
                    v.tensor_tensor(out=rnk[:], in0=rnk[:], in1=cpre[:], op=OP.mult)
                    v.drain()
                    v.tensor_tensor(out=rnk[:], in0=rnk[:], in1=tsl, op=OP.add)
                    v.drain()
                    sq = seq1 if k == 0 else seq2
                    sel_ap = sq[:, e: e + 16 * (NTC - 1) + 1: 16]
                    v.tensor_tensor(out=rnk[:], in0=rnk[:], in1=sel_ap, op=OP.mult)
                    v.drain()
                    last = v.tensor_tensor(out=Dk[k][:], in0=Dk[k][:], in1=rnk[:], op=OP.add)
                    v.drain()
                v.tensor_copy(Dki[k][:], Dk[k][:])
                v.tensor_copy(wbuf[k][:, 0:2 * NTC:2], iotac[:])
                v.drain()
                sw = sw1 if k == 0 else sw2
                last = v.tensor_copy(wbuf[k][:, 1:2 * NTC:2], sw[:])
                c_v.inc(last)
        block.vector(dv_p5b)
        n_disp = c_v.n

        def gq_p5b(g):
            g.wait_ge(c_v.sem, n_disp)
            g.wait_ge(c_x2r.sem, n_x2rw)
            for k in range(2):
                for tc in range(NTC):
                    c_ds.inc(g.indirect_dma_start(
                        out=disp_h[ch],
                        out_offset=bass.IndirectOffsetOnAxis(ap=Dki[k][:, tc:tc + 1], axis=0),
                        in_=wbuf[k][:, 2 * tc:2 * tc + 2], in_offset=None))
        block.gpsimd(gq_p5b)
        n_ds = c_ds.n
        p5b.close()

        p6sx = ExitStack()
        wsg_sb = sbc("wsg_sb", [128, NKC * MFF], BF16)
        wsu_sb = sbc("wsu_sb", [128, NKC * MFF], BF16)
        wsd_sb = sbc("wsd_sb", [128, 4 * C], BF16)

        def sy_wsld(s):
            s.wait_ge(c_g2.sem, n_cast)
            c_ws.inc(s.dma_start(out=wsg_sb[:], in_=wsg_h))
            c_ws.inc(s.dma_start(out=wsu_sb[:], in_=wsu_h))
            c_ws.inc(s.dma_start(out=wsd_sb[:], in_=wsd_h))
        block.sync(sy_wsld)
        n_ws = c_ws.n

        sg_b = sbc("sg_b", [128, 512], F32)
        h_bf = sbc("h_bf", [128, 2 * 4 * 256], BF16)
        eo_sb = [[sbc(f"eo_sb{g2}_{i2}", [128, C], F32) for i2 in range(2)]
                 for g2 in range(2)]
        pdnh = [psc(p6sx, f"pdnh{i2}", [128, 512], F32) for i2 in range(2)]
        pga = [psc(p6sx, f"pga{i2}", [128, 512], F32) for i2 in range(2)]
        pua = [psc(p6sx, f"pua{i2}", [128, 512], F32) for i2 in range(2)]

        NTAU_S = 8
        sp0, sa0, sv0 = c_p.n, c_a.n, c_v.n
        s_pe, s_a, s_v = {}, {}, {}

        def s_order():
            seq = []
            for tau in range(NTAU_S):
                if tau >= 2:
                    seq.append(("down", tau - 2))
                seq.append(("gu", tau))
            seq.append(("down", NTAU_S - 2))
            seq.append(("down", NTAU_S - 1))
            return seq

        n = sp0
        for item in s_order():
            if item[0] == "gu":
                for j in range(4):
                    n += 1
                    s_pe[("gu", item[1], j)] = n
            else:
                for q in range(4):
                    n += 1
                    s_pe[("downq", item[1], q)] = n
                s_pe[("down", item[1])] = n
        n = sa0
        for tau in range(NTAU_S):
            for mc in range(4):
                n += 1
                s_a[("silu", tau, mc)] = n
        n = sv0
        for tau in range(NTAU_S):
            for mc in range(4):
                n += 1
                s_v[("h", tau, mc)] = n
            if tau >= 1:
                for q in range(4):
                    n += 1
                    s_v[("evacq", tau - 1, q)] = n
        for q in range(4):
            n += 1
            s_v[("evacq", NTAU_S - 1, q)] = n
        n_p6s_v = n

        shw_b = (c_shw[0].n, c_shw[1].n)

        def sy_shw(s):
            for tau in range(NTAU_S):
                s.wait_ge(c_v.sem, s_v[("evacq", tau, 3)])
                for tb in range(2):
                    tc_g = 2 * tau + tb
                    c_shw[tau % 2].inc(s.dma_start(
                        out=acc_h[ch][128 * tc_g:128 * (tc_g + 1), :],
                        in_=eo_sb[tau % 2][tb][:]))
        block.sync(sy_shw)

        def pe_p6s(t):
            tw(t)
            t.wait_ge(c_ws.sem, n_ws)
            t.wait_ge(c_v.sem, n_x2)
            fdown = [True]
            dq = [0]
            for item in s_order():
                if item[0] == "gu":
                    tau = item[1]
                    for half in range(2):
                        for w_i in range(2):
                            bank = (pga if w_i == 0 else pua)[half]
                            wb_ = wsg_sb if w_i == 0 else wsu_sb
                            if tau >= 1:
                                if w_i == 0:
                                    t.wait_ge(c_a.sem, s_a[("silu", tau - 1, 2 * half + 1)])
                                else:
                                    t.wait_ge(c_v.sem, s_v[("h", tau - 1, 2 * half + 1)])
                            for mcl in range(2):
                                mc = 2 * half + mcl
                                for cc in range(NKC):
                                    mm = t.matmul(
                                        bank[:, 256 * mcl:256 * (mcl + 1)],
                                        wb_[:, cc * MFF + 128 * mc: cc * MFF + 128 * (mc + 1)],
                                        x2[:, cc * CHUNK + 256 * tau: cc * CHUNK + 256 * (tau + 1)],
                                        start=(cc == 0), stop=(cc == NKC - 1))
                            c_p.inc(mm)
                else:
                    tau = item[1]
                    if fdown[0]:
                        t.wait_ge(c_v.sem, n_disp)
                        fdown[0] = False
                    t.wait_ge(c_v.sem, s_v[("h", tau, 3)])
                    hb = 1024 * (tau % 2)
                    for q in range(4):
                        tb, fh = q % 2, q // 2
                        if dq[0] >= 2:
                            pt_, pq_ = divmod(dq[0] - 2, 4)
                            t.wait_ge(c_v.sem, s_v[("evacq", pt_, pq_)])
                        for mc in range(4):
                            mm = t.matmul(
                                pdnh[q % 2][:],
                                h_bf[:, hb + mc * 256 + 128 * tb: hb + mc * 256 + 128 * (tb + 1)],
                                wsd_sb[:, mc * C + 512 * fh: mc * C + 512 * (fh + 1)],
                                start=(mc == 0), stop=(mc == 3))
                        n_ = c_p.inc(mm)
                        assert n_ == s_pe[("downq", tau, q)]
                        dq[0] += 1
        block.tensor(pe_p6s)

        def act_p6s(a):
            for tau in range(NTAU_S):
                for mc in range(4):
                    a.wait_ge(c_p.sem, s_pe[("gu", tau, 2 * (mc // 2))])
                    gl = tau * 4 + mc
                    if gl >= 2:
                        t2, m2 = divmod(gl - 2, 4)
                        a.wait_ge(c_v.sem, s_v[("h", t2, m2)])
                    c_a.inc(a.activation(
                        out=sg_b[:, 256 * (mc % 2):256 * (mc % 2 + 1)],
                        in_=pga[mc // 2][:, 256 * (mc % 2):256 * (mc % 2 + 1)],
                        func=AF.Silu))
        block.scalar(act_p6s)

        def s_evac(v, tau):
            if tau >= 2:
                par = tau % 2
                v.wait_ge(c_shw[par].sem, shw_b[par] + 32 * (tau // 2))
            v.drain()
            for q in range(4):
                tb, fh = q % 2, q // 2
                v.wait_ge(c_p.sem, s_pe[("downq", tau, q)])
                tc_g = 2 * tau + tb
                op = v.tensor_tensor(
                    out=eo_sb[tau % 2][tb][:, 512 * fh:512 * (fh + 1)],
                    in0=pdnh[q % 2][:],
                    in1=x2rows[:, tc_g * C + 512 * fh: tc_g * C + 512 * (fh + 1)],
                    op=OP.add)
                n_ = c_v.inc(op)
                assert n_ == s_v[("evacq", tau, q)]

        def dv_p6s(v):
            for tau in range(NTAU_S):
                if tau >= 2:
                    v.wait_ge(c_p.sem, s_pe[("down", tau - 2)])
                hb = 1024 * (tau % 2)
                for mc in range(4):
                    v.wait_ge(c_a.sem, s_a[("silu", tau, mc)])
                    v.wait_ge(c_p.sem, s_pe[("gu", tau, 2 * (mc // 2) + 1)])
                    c_v.inc(v.tensor_tensor(
                        out=h_bf[:, hb + mc * 256: hb + (mc + 1) * 256],
                        in0=sg_b[:, 256 * (mc % 2):256 * (mc % 2 + 1)],
                        in1=pua[mc // 2][:, 256 * (mc % 2):256 * (mc % 2 + 1)],
                        op=OP.mult))
                if tau >= 1:
                    s_evac(v, tau - 1)
            s_evac(v, NTAU_S - 1)
        block.vector(dv_p6s)
        assert c_v.n == n_p6s_v
        p6sx.close()

        ch_marks.append(dict(n_ds=n_ds, n_x2rw=n_x2rw, n_disp=n_disp, n_x2=n_x2))


        prev_chunk_done = dict(v=c_v.n, o0=c_o[0].n, o1=c_o[1].n, p=c_p.n, a=c_a.n,
                               shw0=c_shw[0].n, shw1=c_shw[1].n,
                               ds=c_ds.n, x2r=c_x2r.n)
        chs.close()

    pass1_v = c_v.n
    p6r = ExitStack()
    sb7 = lambda name, shape, dt: p6r.enter_context(nc.sbuf_tensor(name, shape, dt))
    ps7 = lambda name, shape, dt: p6r.enter_context(nc.psum_tensor(name, shape, dt))
    wgb = [sb7(f"wgb{i}", [128, NKC * MFF], BF16) for i in range(2)]
    wub = [sb7(f"wub{i}", [128, NKC * MFF], BF16) for i in range(2)]
    wdb = [sb7(f"wdb{i}", [128, 4 * C], BF16) for i in range(2)]
    xgT = [sb7(f"xgT{i}", [128, NKC * ECAP], BF16) for i in range(4)]
    xgr = [sb7(f"xgr{i}", [128, (ECAP // 128) * C], BF16) for i in range(4)]
    idw = [sb7(f"idw{i}", [128, (ECAP // 128) * 2], F32) for i in range(4)]
    idi = [sb7(f"idi{i}", [128, ECAP // 128], mybir.dt.int32) for i in range(4)]
    sg_b2 = sb7("sg_b2", [128, 512], F32)
    h_bf2 = sb7("h_bf2", [128, 2 * 4 * 256], BF16)
    eo_bf = [[sb7(f"eo_bf{g2}_{i2}", [128, C], BF16) for i2 in range(2)]
             for g2 in range(2)]
    pdn2 = [ps7(f"pdn2_{i}", [128, 512], F32) for i in range(2)]
    ptt = [ps7(f"ptt{i}", [128, 1024], BF16) for i in range(2)]
    pga2 = [ps7(f"pga2_{i}", [128, 512], F32) for i in range(2)]
    pua2 = [ps7(f"pua2_{i}", [128, 512], F32) for i in range(2)]

    NU = 2 * NE
    ue = lambda u: u // 2
    uc = lambda u: u % 2

    t_at_unit = {u: [] for u in range(NU)}
    t_at_unit[0] = [0, 1, 2]
    for u in range(1, NU - 2):
        t_at_unit[u].append(u + 2)

    r_pe, r_a, r_v = {}, {}, {}
    tr_seq = []
    dq_seq = []
    u_items = (("gu", 0), ("gu", 1), ("down", 0), ("down", 1))
    n = c_p.n
    for u in range(NU):
        for u_t in t_at_unit[u]:
            for j in range(4):
                n += 1
                r_pe[("tr", u_t, j)] = n
                tr_seq.append((u_t, j))
        for item in u_items:
            if item[0] == "gu":
                for j in range(4):
                    n += 1
                    r_pe[("gu", u, item[1], j)] = n
            else:
                for q in range(4):
                    n += 1
                    r_pe[("downq", u, item[1], q)] = n
                    dq_seq.append((u, item[1], q))
                r_pe[("down", u, item[1])] = n
    n = c_a.n
    for u in range(NU):
        for u_t in t_at_unit[u]:
            for j in range(4):
                n += 1
                r_a[("trv", u_t, j)] = n
        for tau in range(2):
            for mc in range(4):
                n += 1
                r_a[("silu", u, tau, mc)] = n
    n = c_v.n
    for u in range(NU):
        for tau in range(2):
            for mc in range(4):
                n += 1
                r_v[("h", u, tau, mc)] = n
            if tau >= 1:
                for q in range(4):
                    n += 1
                    r_v[("evacq", u, 0, q)] = n
        for q in range(4):
            n += 1
            r_v[("evacq", u, 1, q)] = n
    n_p6r_v = n

    gx_b2 = tuple(c_gx[i].n for i in range(4))
    gx_marks2 = {u: (u % 4, gx_b2[u % 4] + 64 * (u // 4 + 1)) for u in range(NU)}
    idc_b2 = c_idc.n
    v_idc2 = {u: idc_b2 + u + 1 for u in range(NU)}
    eo_b = (c_eo[0].n, c_eo[1].n)
    evac_consumer2 = {}
    _eo_n = [0, 0]
    for u in range(NU):
        for tau in range(2):
            g_ = 2 * u + tau
            _eo_n[g_ % 2] += 2
            evac_consumer2[g_] = (g_ % 2, eo_b[g_ % 2] + 16 * _eo_n[g_ % 2])

    sy_marks = {}
    id_marks = {}

    def sy_ffnr(s):
        s.wait_ge(c_v.sem, pass1_v)
        s.wait_ge(c_shw[0].sem, c_shw[0].n)
        s.wait_ge(c_shw[1].sem, c_shw[1].n)
        s.wait_ge(c_x2r.sem, c_x2r.n)
        s.wait_ge(c_ds.sem, c_ds.n)

        def wload(e):
            if e >= 2:
                s.wait_ge(c_p.sem, r_pe[("down", 2 * (e - 2) + 1, 1)])
            ce = c_e[e % 2]
            ce.inc(s.dma_start(out=wgb[e % 2][:], in_=wg_h[128 * e:128 * (e + 1), :]))
            ce.inc(s.dma_start(out=wub[e % 2][:], in_=wu_h[128 * e:128 * (e + 1), :]))
            ce.inc(s.dma_start(out=wdb[e % 2][:], in_=wd_h[128 * e:128 * (e + 1), :]))
            sy_marks[e] = (e % 2, ce.n)

        wload(0)
        wload(1)
        for u in range(2, NU):
            if u % 2 == 0 and ue(u) + 1 < NE:
                wload(ue(u) + 1)
    block.sync(sy_ffnr)

    def gq_p6r(g):
        g.wait_ge(c_v.sem, pass1_v)
        g.wait_ge(c_x2r.sem, c_x2r.n)
        g.wait_ge(c_ds.sem, c_ds.n)
        g.wait_ge(c_shw[0].sem, c_shw[0].n)
        g.wait_ge(c_shw[1].sem, c_shw[1].n)

        def idload(u):
            b = u % 4
            if u >= 4:
                g.wait_ge(c_v.sem, r_v[("evacq", u - 4, 1, 3)])
            c_id[b].inc(g.dma_start(
                out=idw[b][:].rearrange("p (r two) -> p r two", two=2),
                in_=_t3(disp_h[uc(u)][ECAP * ue(u):ECAP * (ue(u) + 1), :])))
            id_marks[u] = (b, c_id[b].n)

        def gathers(u):
            b = u % 4
            bb = u % 4
            g.wait_ge(c_x2r.sem, ch_marks[uc(u)]["n_x2rw"])
            g.wait_ge(c_idc.sem, v_idc2[u])
            if u >= 4:
                g.wait_ge(c_p.sem, r_pe[("tr", u - 4, 3)])
            for rc in range(ECAP // 128):
                c_gx[b].inc(g.indirect_dma_start(
                    out=xgr[bb][:, rc * C:(rc + 1) * C], out_offset=None,
                    in_=x2r_h[uc(u)],
                    in_offset=bass.IndirectOffsetOnAxis(ap=idi[bb][:, rc:rc + 1], axis=0)))
            assert gx_marks2[u] == (b, c_gx[b].n)

        def eowrites(u):
            for tau in range(2):
                g.wait_ge(c_v.sem, r_v[("evacq", u, tau, 3)])
                for tb in range(2):
                    rc = 2 * tau + tb
                    c_eo[tau % 2].inc(g.dma_start(
                        out=eo_h[uc(u)][ECAP * ue(u) + 128 * rc:
                                        ECAP * ue(u) + 128 * (rc + 1), :],
                        in_=eo_bf[tau % 2][tb][:]))

        for u in range(4):
            idload(u)
        gathers(0)
        gathers(1)
        gathers(2)
        gathers(3)
        for u in range(NU):
            eowrites(u)
            if u + 4 < NU:
                idload(u + 4)
                gathers(u + 4)
    block.gpsimd(gq_p6r)
    n_eo = (c_eo[0].n, c_eo[1].n)

    def pe_p6r(t):
        tw(t)
        t.wait_ge(c_v.sem, pass1_v)
        tr_i = [0]
        dq_i = [0]
        for u in range(NU):
            e = ue(u)
            t.wait_ge(c_e[sy_marks[e][0]].sem, sy_marks[e][1])
            for u_t in t_at_unit[u]:
                bt = u_t % 4
                t.wait_ge(c_gx[gx_marks2[u_t][0]].sem, gx_marks2[u_t][1])
                for j in range(4):
                    gi = tr_i[0]
                    if gi >= 2:
                        pu_, pj_ = tr_seq[gi - 2]
                        t.wait_ge(c_a.sem, r_a[("trv", pu_, pj_)])
                    for cc in range(NKC):
                        mm = t.transpose(
                            ptt[gi % 2][:, 128 * cc:128 * (cc + 1)],
                            xgr[bt][:, j * C + 128 * cc: j * C + 128 * (cc + 1)],
                            id16[:])
                    n_ = c_p.inc(mm)
                    assert n_ == r_pe[("tr", u_t, j)]
                    tr_i[0] += 1
            t.wait_ge(c_a.sem, r_a[("trv", u, 3)])
            for item in u_items:
                if item[0] == "gu":
                    tau = item[1]
                    prev = None
                    if (u, tau) != (0, 0):
                        prev = (u, 0) if tau == 1 else (u - 1, 1)
                    for half in range(2):
                        for w_i in range(2):
                            bank = (pga2 if w_i == 0 else pua2)[half]
                            wbuf_ = (wgb if w_i == 0 else wub)[e % 2]
                            if prev is not None:
                                if w_i == 0:
                                    t.wait_ge(c_a.sem,
                                              r_a[("silu",) + prev + (2 * half + 1,)])
                                else:
                                    t.wait_ge(c_v.sem,
                                              r_v[("h",) + prev + (2 * half + 1,)])
                            for mcl in range(2):
                                mc = 2 * half + mcl
                                for cc in range(NKC):
                                    mm = t.matmul(
                                        bank[:, 256 * mcl:256 * (mcl + 1)],
                                        wbuf_[:, cc * MFF + 128 * mc: cc * MFF + 128 * (mc + 1)],
                                        xgT[u % 4][:, cc * ECAP + 256 * tau:
                                                   cc * ECAP + 256 * (tau + 1)],
                                        start=(cc == 0), stop=(cc == NKC - 1))
                            c_p.inc(mm)
                else:
                    tau = item[1]
                    t.wait_ge(c_v.sem, r_v[("h", u, tau, 3)])
                    hb = 1024 * (tau % 2)
                    for q in range(4):
                        tb, fh = q % 2, q // 2
                        gi = dq_i[0]
                        if gi >= 2:
                            k_ = dq_seq[gi - 2]
                            t.wait_ge(c_v.sem, r_v[("evacq",) + k_])
                        for mc in range(4):
                            mm = t.matmul(
                                pdn2[gi % 2][:],
                                h_bf2[:, hb + mc * 256 + 128 * tb:
                                      hb + mc * 256 + 128 * (tb + 1)],
                                wdb[e % 2][:, mc * C + 512 * fh: mc * C + 512 * (fh + 1)],
                                start=(mc == 0), stop=(mc == 3))
                        n_ = c_p.inc(mm)
                        assert n_ == r_pe[("downq", u, tau, q)]
                        dq_i[0] += 1
    block.tensor(pe_p6r)

    def act_p6r(a):
        tr_i = [0]
        for u in range(NU):
            for u_t in t_at_unit[u]:
                bt = u_t % 4
                if u_t >= 4:
                    a.wait_ge(c_p.sem, r_pe[("gu", u_t - 4, 1, 3)])
                for j in range(4):
                    a.wait_ge(c_p.sem, r_pe[("tr", u_t, j)])
                    n_ = c_a.inc(a.activation(
                        out=xgT[bt][:].rearrange("p (cc j2) -> p cc j2",
                                                 j2=ECAP)[:, :, 128 * j:128 * (j + 1)],
                        in_=ptt[tr_i[0] % 2][:], func=AF.Identity))
                    assert n_ == r_a[("trv", u_t, j)]
                    tr_i[0] += 1
            for tau in range(2):
                for mc in range(4):
                    a.wait_ge(c_p.sem, r_pe[("gu", u, tau, 2 * (mc // 2))])
                    gl = (2 * u + tau) * 4 + mc
                    if gl >= 2:
                        u2, r2 = divmod(gl - 2, 8)
                        tau2, mc2 = divmod(r2, 4)
                        a.wait_ge(c_v.sem, r_v[("h", u2, tau2, mc2)])
                    c_a.inc(a.activation(
                        out=sg_b2[:, 256 * (mc % 2):256 * (mc % 2 + 1)],
                        in_=pga2[mc // 2][:, 256 * (mc % 2):256 * (mc % 2 + 1)],
                        func=AF.Silu))
    block.scalar(act_p6r)

    def emit_cast2(v, u):
        b = u % 4
        v.wait_ge(c_id[id_marks[u][0]].sem, id_marks[u][1])
        cp = v.tensor_copy(idi[b][:], idw[b][:, 0:2 * (ECAP // 128):2])
        cp.then_inc(c_idc.sem, 1)
        c_idc.n += 1
        assert c_idc.n == v_idc2[u]

    def r_evac(v, u, tau):
        g_ = 2 * u + tau
        if g_ >= 2:
            par, cnt = evac_consumer2[g_ - 2]
            v.wait_ge(c_eo[par].sem, cnt)
        v.drain()
        for q in range(4):
            tb, fh = q % 2, q // 2
            v.wait_ge(c_p.sem, r_pe[("downq", u, tau, q)])
            rc = 2 * tau + tb
            op = v.tensor_scalar(
                out=eo_bf[tau % 2][tb][:, 512 * fh:512 * (fh + 1)],
                in0=pdn2[q % 2][:],
                scalar1=idw[u % 4][:, 2 * rc + 1:2 * rc + 2], scalar2=None,
                op0=OP.mult)
            n_ = c_v.inc(op)
            assert n_ == r_v[("evacq", u, tau, q)]

    def dv_p6r(v):
        emit_cast2(v, 0)
        emit_cast2(v, 1)
        emit_cast2(v, 2)
        emit_cast2(v, 3)
        for u in range(NU):
            for tau in range(2):
                gt_ = 2 * u + tau
                if gt_ >= 2:
                    u2, tau2 = divmod(gt_ - 2, 2)
                    v.wait_ge(c_p.sem, r_pe[("down", u2, tau2)])
                hb = 1024 * (tau % 2)
                for mc in range(4):
                    v.wait_ge(c_a.sem, r_a[("silu", u, tau, mc)])
                    v.wait_ge(c_p.sem, r_pe[("gu", u, tau, 2 * (mc // 2) + 1)])
                    c_v.inc(v.tensor_tensor(
                        out=h_bf2[:, hb + mc * 256: hb + (mc + 1) * 256],
                        in0=sg_b2[:, 256 * (mc % 2):256 * (mc % 2 + 1)],
                        in1=pua2[mc // 2][:, 256 * (mc % 2):256 * (mc % 2 + 1)],
                        op=OP.mult))
                if tau >= 1:
                    r_evac(v, u, 0)
            r_evac(v, u, 1)
            if u + 4 < NU:
                emit_cast2(v, u + 4)
    block.vector(dv_p6r)
    assert c_v.n == n_p6r_v, (c_v.n, n_p6r_v)
    p6r.close()

    p7s = ExitStack()
    accr = [p7s.enter_context(nc.sbuf_tensor(f"accr_{i}", [128, C], F32))
            for i in range(2)]
    outr = [p7s.enter_context(nc.sbuf_tensor(f"outr_{i}", [128, C], F32))
            for i in range(2)]
    bnst = p7s.enter_context(nc.sbuf_tensor("bnst", [128, 12], F32))
    bnag = p7s.enter_context(nc.sbuf_tensor("bnag", [128, 2], F32))
    rstd2 = p7s.enter_context(nc.sbuf_tensor("rstd2", [128, 1], F32))
    egb = [[p7s.enter_context(nc.sbuf_tensor(f"egb_{k}_{i}", [128, C], BF16))
            for i in range(2)] for k in range(2)]

    NT2 = NCHUNK * NTC
    v_b7 = c_v.n
    a_b7 = c_a.n
    o_b7 = [c_o[0].n, c_o[1].n]
    ac_b = [c_ac[0].n, c_ac[1].n]
    eg_b = [c_eg[0].n, c_eg[1].n]
    pe_last = r_pe[("down", NU - 1, 1)]

    def sy_p7(s):
        def store(ti):
            b = ti % 2
            s.wait_ge(c_v.sem, v_b7 + 2 * ti + 2)
            c_o[b].inc(s.dma_start(
                out=out_hbm[128 * ti: 128 * (ti + 1), :], in_=outr[b][:]))
        s.wait_ge(c_shw[0].sem, c_shw[0].n)
        s.wait_ge(c_shw[1].sem, c_shw[1].n)
        for cc_ in (c_e[0], c_e[1], c_gx[0], c_gx[1], c_gx[2], c_gx[3],
                    c_id[0], c_id[1], c_id[2], c_id[3], c_eo[0], c_eo[1]):
            s.wait_ge(cc_.sem, cc_.n)
        s.wait_ge(c_p.sem, pe_last)
        s.wait_ge(c_v.sem, n_p6r_v)
        for ti in range(NT2):
            b = ti % 2
            if ti >= 2:
                store(ti - 2)
            ch2, tc = divmod(ti, NTC)
            c_ac[b].inc(s.dma_start(out=accr[b][:],
                                    in_=acc_h[ch2][128 * tc:128 * (tc + 1), :]))
        store(NT2 - 2)
        store(NT2 - 1)
    block.sync(sy_p7)

    def gq_p7(g):
        g.wait_ge(c_eo[0].sem, n_eo[0])
        g.wait_ge(c_eo[1].sem, n_eo[1])
        for cc_ in (c_e[0], c_e[1], c_id[0], c_id[1], c_id[2], c_id[3]):
            g.wait_ge(cc_.sem, cc_.n)
        g.wait_ge(c_p.sem, pe_last)
        for ti in range(NT2):
            b = ti % 2
            ch2, tc = divmod(ti, NTC)
            if ti >= 2:
                g.wait_ge(c_v.sem, v_b7 + 2 * (ti - 2) + 1)
            for k in range(2):
                c_eg[b].inc(g.indirect_dma_start(
                    out=egb[k][b][:], out_offset=None, in_=eo_h[ch2],
                    in_offset=bass.IndirectOffsetOnAxis(ap=Dki2[ch2][k][:, tc:tc + 1],
                                                        axis=0)))
    block.gpsimd(gq_p7)

    def dv_p7(v):
        for ti in range(NT2):
            b = ti % 2
            v.wait_ge(c_ac[b].sem, ac_b[b] + 16 * (ti // 2 + 1))
            v.wait_ge(c_eg[b].sem, eg_b[b] + 32 * (ti // 2 + 1))
            v.drain()
            v.tensor_tensor(out=accr[b][:], in0=accr[b][:], in1=egb[0][b][:],
                            op=OP.add)
            v.drain()
            v.tensor_tensor(out=accr[b][:], in0=accr[b][:], in1=egb[1][b][:],
                            op=OP.add)
            v.drain()
            v.bn_stats(bnst[:, 0:6], accr[b][:, 0:512])
            v.bn_stats(bnst[:, 6:12], accr[b][:, 512:1024])
            v.drain()
            v.bn_aggr(bnag[:], bnst[:])
            v.drain()
            v.tensor_scalar_add(rstd2[:], bnag[:, 1:2], EPS)
            v.drain()
            c_v.inc(v.reciprocal(rstd2[:], rstd2[:]))
            v.wait_ge(c_a.sem, a_b7 + ti + 1)
            if ti >= 2:
                v.wait_ge(c_o[b].sem, o_b7[b] + 16 * (ti // 2))
            v.drain()
            c_v.inc(v.tensor_scalar(
                out=outr[b][:], in0=accr[b][:], scalar1=bnag[:, 0:1],
                scalar2=rstd2[:], op0=OP.subtract, op1=OP.mult))
    block.vector(dv_p7)

    def act_p7(a):
        for ti in range(NT2):
            a.wait_ge(c_v.sem, v_b7 + 2 * ti + 1)
            c_a.inc(a.activation(out=rstd2[:], in_=rstd2[:], func=AF.Sqrt))
    block.scalar(act_p7)
    p7s.close()


    def sy_final(s):
        s.wait_ge(c_o[0].sem, c_o[0].n)
        s.wait_ge(c_o[1].sem, c_o[1].n)
        if c_tap.n:
            s.wait_ge(c_tap.sem, c_tap.n)
    block.sync(sy_final)

    blk_cm.__exit__(None, None, None)
    stk.close()
    return nc


def _host_consts():
    i = np.arange(128)[:, None]
    j = np.arange(128)[None, :]
    m_list = [(j >= i)]
    for dp in (0, 1):
        for t_ in (0, 1):
            m_list.append((j + 128 * t_) >= (2 * i + dp))
    masks = np.concatenate([m.astype(np.float32) for m in m_list], axis=1)
    sel = np.zeros((16, 16 * 128), np.float32)
    for e in range(16):
        sel[e, 128 * e:128 * (e + 1)] = 1.0
    ltx = (np.arange(128)[:, None] < np.arange(128)[None, :]).astype(np.float32)
    iota = (np.arange(NTC)[None, :] * 128 + np.arange(128)[:, None]).astype(np.float32)
    dinit = np.zeros((NE * ECAP + 128, 2), np.float32)
    dinit[:, 0] = CHUNK
    return dict(
        id32=np.eye(128, dtype=np.float32),
        id16=np.eye(128, dtype=np.float32).astype(ml_dtypes.bfloat16),
        ones=np.ones((128, 128), dtype=np.float32).astype(ml_dtypes.bfloat16),
        masks=masks.astype(ml_dtypes.bfloat16),
        sel=sel.astype(ml_dtypes.bfloat16),
        ltx=ltx.astype(ml_dtypes.bfloat16),
        iota=iota,
        dinit=dinit,
        zrow=np.zeros((1, C), ml_dtypes.bfloat16),
        zrows=np.zeros((128, C), ml_dtypes.bfloat16),
        trashc=np.broadcast_to(
            NE * ECAP + np.arange(2 * NTC, dtype=np.float32)[None, :],
            (128, 2 * NTC)).copy(),
    )


def kernel(L0_attn_x, L0_attn_c_attn_w, L0_attn_c_attn_b, L0_attn_q_w, L0_attn_q_b,
           L0_attn_k_w, L0_attn_k_b, L0_attn_v_w, L0_attn_v_b, L0_attn_o_w, L0_attn_o_b,
           L0_ffn_gate_w, L0_ffn_w_gate, L0_ffn_w_up, L0_ffn_w_down,
           L0_ffn_ws_gate, L0_ffn_ws_up, L0_ffn_ws_down, _taps=(), _cores=None):
    nc = build(taps=_taps)
    xf = np.ascontiguousarray(np.asarray(L0_attn_x).reshape(B * T, C))
    consts = _host_consts()
    shared = dict(
        caw=np.ascontiguousarray(L0_attn_c_attn_w), cab=np.ascontiguousarray(L0_attn_c_attn_b),
        qw=np.ascontiguousarray(L0_attn_q_w), qb=np.ascontiguousarray(L0_attn_q_b),
        kw=np.ascontiguousarray(L0_attn_k_w), kb=np.ascontiguousarray(L0_attn_k_b),
        vw=np.ascontiguousarray(L0_attn_v_w), vb=np.ascontiguousarray(L0_attn_v_b),
        ow=np.ascontiguousarray(L0_attn_o_w), ob=np.ascontiguousarray(L0_attn_o_b),
        gw=np.ascontiguousarray(L0_ffn_gate_w), wg=np.ascontiguousarray(L0_ffn_w_gate),
        wu=np.ascontiguousarray(L0_ffn_w_up), wd=np.ascontiguousarray(L0_ffn_w_down),
        wsg=np.ascontiguousarray(L0_ffn_ws_gate), wsu=np.ascontiguousarray(L0_ffn_ws_up),
        wsd=np.ascontiguousarray(L0_ffn_ws_down), **consts)
    cores = list(range(NCORES)) if _cores is None else _cores
    in_maps = []
    for ci in range(len(cores)):
        m = dict(shared)
        m["x"] = np.ascontiguousarray(xf[ci * TPC:(ci + 1) * TPC])
        in_maps.append(m)
    res = run_bass_kernel_spmd(nc, in_maps, core_ids=cores)
    if _taps:
        return res
    out = np.concatenate([r["out"] for r in res.results], axis=0)
    return np.ascontiguousarray(out.reshape(B, T, C).astype(np.float32))


# version 30
# speedup vs baseline: 1.0595x; 1.0444x over previous
import numpy as np
import ml_dtypes
from contextlib import ExitStack

import concourse.bass as bass
import concourse.mybir as mybir
from concourse.bass_utils import run_bass_kernel_spmd

F32 = mybir.dt.float32
BF16 = mybir.dt.bfloat16
AF = mybir.ActivationFunctionType
OP = mybir.AluOpType

B, T, C, H, D = 2, 16384, 1024, 16, 64
NCORES = 8
TPC = (B * T) // NCORES
CHUNK = 2048
NCHUNK = TPC // CHUNK
NPAIR = H // 2
NE = 16
MFF = 512
EPS = 1e-5
NKC = C // 128
ECAP = 512
NTC = CHUNK // 128


class Ctr:
    def __init__(self, nc, name, step=1):
        self.sem = nc.semaphore(name).__enter__()
        self.step = step
        self.n = 0

    def inc(self, instr):
        instr.then_inc(self.sem, self.step)
        self.n += self.step
        return self.n


def unit_list():
    return [("b1", s) for s in range(4)] + [("b2", s) for s in range(2)] + [("b3", 0)]


def unit_rects(branch, seg, head):
    rects = []
    if branch in ("b1", "b3"):
        if branch == "b1":
            kstart = lambda kc: (512 * seg + 128 * kc, 1)
            qstart = lambda h: (512 * seg + 256 * h, 1)
            vtile = lambda kc: (4 * seg + kc, False)
        else:
            c = head % 4
            kstart = lambda kc: (c + 512 * kc, 4)
            qstart = lambda h: (c + 1024 * h, 4)
            vtile = lambda kc: (4 * c + kc, True)
        for h in (0, 1):
            for kc in range(2 * (h + 1)):
                o = 128 * kc - 256 * h
                masks = [(0, o)] if 0 <= o < 256 else []
                rects.append(dict(h=h, k=kstart(kc), q=qstart(h), v=vtile(kc),
                                  start=max(0, o), masks=masks))
    else:
        p = head % 2
        for h in (0, 1):
            for h2 in range(h + 1):
                for dp in (0, 1):
                    c = p + 2 * dp
                    masks = [(1 + 2 * dp + t, 128 * t) for t in (0, 1)] if h == h2 else []
                    rects.append(dict(h=h, k=(c + 1024 * seg + 512 * h2, 4),
                                      q=(1024 * seg + p + 512 * h, 2),
                                      v=(4 * c + 2 * seg + h2, True),
                                      start=0, masks=masks))
    return rects


def unit_ycols(branch, seg, head):
    if branch == "b1":
        return (512 * seg, 1)
    if branch == "b2":
        return (1024 * seg + head % 2, 2)
    return (head % 4, 4)


def _t3(src2d, p=128):
    return src2d.rearrange("(rc p) c -> p rc c", p=p)


def _d3(dstbuf, ncols):
    return dstbuf[:].rearrange("p (rc c) -> p rc c", c=ncols)


def build(taps=()):
    taps = set(taps)
    nc = bass.Bass("TRN2", target_bir_lowering=False, debug=False)

    x_in = nc.dram_tensor("x", [TPC, C], F32, kind="ExternalInput").ap()
    out_hbm = nc.dram_tensor("out", [TPC, C], F32, kind="ExternalOutput").ap()
    caw = nc.dram_tensor("caw", [C, 3 * C], F32, kind="ExternalInput").ap()
    cab = nc.dram_tensor("cab", [3 * C], F32, kind="ExternalInput").ap()
    qw = nc.dram_tensor("qw", [C, C], F32, kind="ExternalInput").ap()
    qb = nc.dram_tensor("qb", [C], F32, kind="ExternalInput").ap()
    kw = nc.dram_tensor("kw", [C, C], F32, kind="ExternalInput").ap()
    kb = nc.dram_tensor("kb", [C], F32, kind="ExternalInput").ap()
    vw = nc.dram_tensor("vw", [C, C], F32, kind="ExternalInput").ap()
    vb = nc.dram_tensor("vb", [C], F32, kind="ExternalInput").ap()
    ow = nc.dram_tensor("ow", [C, C], F32, kind="ExternalInput").ap()
    ob = nc.dram_tensor("ob", [C], F32, kind="ExternalInput").ap()
    gw_in = nc.dram_tensor("gw", [NE, C], F32, kind="ExternalInput").ap()
    wg_in = nc.dram_tensor("wg", [NE, C, MFF], F32, kind="ExternalInput").ap()
    wu_in = nc.dram_tensor("wu", [NE, C, MFF], F32, kind="ExternalInput").ap()
    wd_in = nc.dram_tensor("wd", [NE, MFF, C], F32, kind="ExternalInput").ap()
    wsg_in = nc.dram_tensor("wsg", [C, MFF], F32, kind="ExternalInput").ap()
    wsu_in = nc.dram_tensor("wsu", [C, MFF], F32, kind="ExternalInput").ap()
    wsd_in = nc.dram_tensor("wsd", [MFF, C], F32, kind="ExternalInput").ap()
    id32_in = nc.dram_tensor("id32", [128, 128], F32, kind="ExternalInput").ap()
    id16_in = nc.dram_tensor("id16", [128, 128], BF16, kind="ExternalInput").ap()
    ones_in = nc.dram_tensor("ones", [128, 128], BF16, kind="ExternalInput").ap()
    masks_in = nc.dram_tensor("masks", [128, 5 * 128], BF16, kind="ExternalInput").ap()
    sel_in = nc.dram_tensor("sel", [16, NE * 128], BF16, kind="ExternalInput").ap()

    wq_h = nc.dram_tensor("wq_h", [128, NKC * C], BF16, kind="Internal").ap()
    wk_h = nc.dram_tensor("wk_h", [128, NKC * C], BF16, kind="Internal").ap()
    wv_h = nc.dram_tensor("wv_h", [128, NKC * C], BF16, kind="Internal").ap()
    ow_h = nc.dram_tensor("ow_h", [C, C], BF16, kind="Internal").ap()
    wg_h = nc.dram_tensor("wg_h", [NE * 128, NKC * MFF], BF16, kind="Internal").ap()
    wu_h = nc.dram_tensor("wu_h", [NE * 128, NKC * MFF], BF16, kind="Internal").ap()
    wd_h = nc.dram_tensor("wd_h", [NE * 128, 4 * C], BF16, kind="Internal").ap()
    wsg_h = nc.dram_tensor("wsg_h", [128, NKC * MFF], BF16, kind="Internal").ap()
    wsu_h = nc.dram_tensor("wsu_h", [128, NKC * MFF], BF16, kind="Internal").ap()
    wsd_h = nc.dram_tensor("wsd_h", [128, 4 * C], BF16, kind="Internal").ap()
    x2r_h = [nc.dram_tensor(f"x2r_h{i}", [CHUNK + 1, C], BF16, kind="Internal").ap()
             for i in range(NCHUNK)]
    disp_h = [nc.dram_tensor(f"disp_h{i}", [NE * ECAP + 128, 2], F32, kind="Internal").ap()
              for i in range(NCHUNK)]
    acc_h = [nc.dram_tensor(f"acc_h{i}", [CHUNK + 1, C], F32, kind="Internal").ap()
             for i in range(NCHUNK)]
    eo_h = [nc.dram_tensor(f"eo_h{i}", [NE * ECAP + 128, C], BF16, kind="Internal").ap()
            for i in range(NCHUNK)]
    ltx_in = nc.dram_tensor("ltx", [128, 128], BF16, kind="ExternalInput").ap()
    iota_in = nc.dram_tensor("iota", [128, NTC], F32, kind="ExternalInput").ap()
    dinit_in = nc.dram_tensor("dinit", [NE * ECAP + 128, 2], F32, kind="ExternalInput").ap()
    zrow_in = nc.dram_tensor("zrow", [1, C], BF16, kind="ExternalInput").ap()
    zrows_in = nc.dram_tensor("zrows", [128, C], BF16, kind="ExternalInput").ap()
    trashc_in = nc.dram_tensor("trashc", [128, 2 * NTC], F32, kind="ExternalInput").ap()

    tap_t = {}
    for name, shape, dt in [
        ("t_xT", [128, NKC * CHUNK], BF16), ("t_q", [128, CHUNK], BF16),
        ("t_k", [128, CHUNK], BF16), ("t_vpl", [128, NTC * C], BF16),
        ("t_vsr", [128, NTC * C], BF16), ("t_y", [128, CHUNK], F32),
        ("t_s", [128, CHUNK], F32), ("t_Y", [128, NKC * CHUNK], BF16),
        ("t_r1", [128, NKC * CHUNK], BF16), ("t_x2", [128, NKC * CHUNK], BF16),
        ("t_route", [NE, CHUNK], BF16), ("t_acc", [128, NKC * CHUNK], F32),
        ("t_weff", [128, NKC * C], BF16),
    ]:
        if name in taps:
            tap_t[name] = nc.dram_tensor(name, shape, dt, kind="ExternalOutput").ap()

    stk = ExitStack()
    sb = lambda name, shape, dt: stk.enter_context(nc.sbuf_tensor(name, shape, dt))

    id32 = sb("id32c", [128, 128], F32)
    id16 = sb("id16c", [128, 128], BF16)
    ones = sb("onesc", [128, 128], BF16)
    masks = sb("masksc", [128, 5 * 128], BF16)
    bqe = sb("bqe", [128, NKC], F32)
    bke = sb("bke", [128, NKC], F32)
    bve = sb("bve", [128, NKC], F32)
    obs = sb("obs", [128, NKC], F32)
    gwT = sb("gwT", [128, 3 * NKC * NE], BF16)
    sel = sb("selc", [16, NE * 128], BF16)
    ltx = sb("ltxc", [128, 128], BF16)
    iotac = sb("iotac", [128, NTC], F32)
    trashc = sb("trashc_s", [128, 2 * NTC], F32)
    Dki2 = [[sb(f"Dki_{i}_{k}", [128, NTC], mybir.dt.int32) for k in range(2)]
            for i in range(NCHUNK)]

    c_g = Ctr(nc, "c_g", 16)
    c_ld1 = Ctr(nc, "c_ld1", 16)
    c_g2 = Ctr(nc, "c_g2", 16)
    c_x = Ctr(nc, "c_x", 16)
    c_wf = Ctr(nc, "c_wf", 16)
    c_wv = Ctr(nc, "c_wv", 16)
    c_wq = [Ctr(nc, "c_wq0", 16), Ctr(nc, "c_wq1", 16)]
    c_ow = Ctr(nc, "c_ow", 16)
    c_e = [Ctr(nc, "c_e0", 16), Ctr(nc, "c_e1", 16)]
    c_o = [Ctr(nc, "c_o0", 16), Ctr(nc, "c_o1", 16)]
    c_tap = Ctr(nc, "c_tap", 16)
    c_x2r = Ctr(nc, "c_x2r", 16)
    c_ds = Ctr(nc, "c_ds", 16)
    c_gx = [Ctr(nc, f"c_gx{i}", 16) for i in range(4)]
    c_tr = [Ctr(nc, "c_tr0", 16), Ctr(nc, "c_tr1", 16)]
    c_id = [Ctr(nc, f"c_id{i}", 16) for i in range(4)]
    c_ws = Ctr(nc, "c_ws", 16)
    c_eo = [Ctr(nc, "c_eo0", 16), Ctr(nc, "c_eo1", 16)]
    c_eg = [Ctr(nc, "c_eg0", 16), Ctr(nc, "c_eg1", 16)]
    c_idc = Ctr(nc, "c_idc", 1)
    c_shw = [Ctr(nc, "c_shw0", 16), Ctr(nc, "c_shw1", 16)]
    c_ac = [Ctr(nc, "c_ac0", 16), Ctr(nc, "c_ac1", 16)]
    c_p = Ctr(nc, "c_p", 1)
    c_a = Ctr(nc, "c_a", 1)
    c_v = Ctr(nc, "c_v", 1)
    c_q = Ctr(nc, "c_q", 1)

    blk_cm = nc.Block()
    block = blk_cm.__enter__()

    def tw(eng):
        if c_tap.n:
            eng.wait_ge(c_tap.sem, c_tap.n)

    x0s = ExitStack()
    x_rows0 = x0s.enter_context(nc.sbuf_tensor("x_rows0", [128, NTC * C], BF16, side="right"))
    setup = ExitStack()
    sb2 = lambda name, shape, dt: setup.enter_context(nc.sbuf_tensor(name, shape, dt))
    ps2 = lambda name, shape, dt: setup.enter_context(nc.psum_tensor(name, shape, dt))
    caw_s = sb2("caw_s", [128, NKC * 3 * C], BF16)
    qw_s = sb2("qw_s", [128, NKC * C], BF16)
    kw_s = sb2("kw_s", [128, NKC * C], BF16)
    vw_s = sb2("vw_s", [128, NKC * C], BF16)
    cab_s = sb2("cab_s", [128, 24], BF16)
    qb_s = sb2("qb_s", [128, NKC], F32)
    kb_s = sb2("kb_s", [128, NKC], F32)
    vb_s = sb2("vb_s", [128, NKC], F32)
    cawT = sb2("cawT", [128, 24 * C], BF16)
    weff_st = sb2("weff_st", [128, NKC * C], BF16)
    pt = [ps2(f"pt{i}", [128, 1024], BF16) for i in range(2)]
    pw = [ps2(f"pw{i}", [128, 512], F32) for i in range(4)]
    pb8 = ps2("pb8", [128, NKC], F32)

    def g_setup(g):
        c_ld1.inc(g.dma_start(out=id16[:], in_=id16_in))
        c_ld1.inc(g.dma_start(out=_d3(caw_s, 3 * C), in_=_t3(caw)))
        c_g.inc(g.dma_start(out=_d3(qw_s, C), in_=_t3(qw)))
        c_g.inc(g.dma_start(out=_d3(kw_s, C), in_=_t3(kw)))
        c_g.inc(g.dma_start(out=_d3(vw_s, C), in_=_t3(vw)))
        c_g.inc(g.dma_start(out=cab_s[:], in_=cab.rearrange("(j p) -> p j", p=128), allow_slow_non_contiguous=True))
        c_g.inc(g.dma_start(out=qb_s[:], in_=qb.rearrange("(j p) -> p j", p=128), allow_slow_non_contiguous=True))
        c_g.inc(g.dma_start(out=kb_s[:], in_=kb.rearrange("(j p) -> p j", p=128), allow_slow_non_contiguous=True))
        c_g.inc(g.dma_start(out=vb_s[:], in_=vb.rearrange("(j p) -> p j", p=128), allow_slow_non_contiguous=True))
        c_g.inc(g.dma_start(out=obs[:], in_=ob.rearrange("(j p) -> p j", p=128), allow_slow_non_contiguous=True))
        c_g.inc(g.dma_start(out=id32[:], in_=id32_in))
        c_g.inc(g.dma_start(out=ones[:], in_=ones_in))
        c_g.inc(g.dma_start(out=sel[:], in_=sel_in))
        c_g.inc(g.dma_start(out=ltx[:], in_=ltx_in))
        c_g.inc(g.dma_start(out=iotac[:], in_=iota_in))
        c_g.inc(g.dma_start(out=masks[:], in_=masks_in))
        c_g.inc(g.dma_start(out=trashc[:], in_=trashc_in))
        c_x.inc(g.dma_start(
            out=_d3(x_rows0, C),
            in_=_t3(x_in[0:CHUNK, :])))
        c_g2.inc(g.dma_start(out=ow_h, in_=ow))
        wgf = wg_in.rearrange("e c m -> (e c) m")
        wuf = wu_in.rearrange("e c m -> (e c) m")
        wdf = wd_in.rearrange("e m c -> (e m) c")
        for e_ in range(NE):
            c_g2.inc(g.dma_start(
                out=wg_h[128 * e_:128 * (e_ + 1), :].rearrange("p (rc c) -> p rc c", c=MFF),
                in_=wgf[e_ * C:(e_ + 1) * C, :].rearrange("(rc p) c -> p rc c", p=128)))
            c_g2.inc(g.dma_start(
                out=wu_h[128 * e_:128 * (e_ + 1), :].rearrange("p (rc c) -> p rc c", c=MFF),
                in_=wuf[e_ * C:(e_ + 1) * C, :].rearrange("(rc p) c -> p rc c", p=128)))
            c_g2.inc(g.dma_start(
                out=wd_h[128 * e_:128 * (e_ + 1), :].rearrange("p (mc c) -> p mc c", c=C),
                in_=wdf[e_ * MFF:(e_ + 1) * MFF, :].rearrange("(mc p) c -> p mc c", p=128)))
        c_g2.inc(g.dma_start(
            out=wsg_h.rearrange("p (rc c) -> p rc c", c=MFF),
            in_=wsg_in.rearrange("(rc p) c -> p rc c", p=128)))
        c_g2.inc(g.dma_start(
            out=wsu_h.rearrange("p (rc c) -> p rc c", c=MFF),
            in_=wsu_in.rearrange("(rc p) c -> p rc c", p=128)))
        c_g2.inc(g.dma_start(
            out=wsd_h.rearrange("p (mc c) -> p mc c", c=C),
            in_=wsd_in.rearrange("(mc p) c -> p mc c", p=128)))
        for i in range(NCHUNK):
            c_g2.inc(g.dma_start(out=eo_h[i][NE * ECAP:NE * ECAP + 128, :], in_=zrows_in))
            c_g2.inc(g.dma_start(out=disp_h[i], in_=dinit_in))
            c_g2.inc(g.dma_start(out=x2r_h[i][CHUNK:CHUNK + 1, :], in_=zrow_in))
    block.gpsimd(g_setup)
    n_ld0, n_cast = 15 * 16, 58 * 16

    v_t0 = c_v.n
    p_t0 = c_p.n

    def pe_cawT(t):
        t.wait_ge(c_ld1.sem, 32)
        for mc in range(24):
            if mc >= 2:
                t.wait_ge(c_v.sem, v_t0 + mc - 1)
            for cc in range(NKC):
                mm = t.transpose(pt[mc % 2][:, 128 * cc:128 * (cc + 1)],
                                 caw_s[:, cc * 3 * C + 128 * mc: cc * 3 * C + 128 * (mc + 1)],
                                 id16[:])
            c_p.inc(mm)
    block.tensor(pe_cawT)

    def dv_cawT(v):
        for mc in range(24):
            v.wait_ge(c_p.sem, p_t0 + mc + 1)
            c_v.inc(v.tensor_copy(cawT[:, mc * C:(mc + 1) * C], pt[mc % 2][:]))
    block.vector(dv_cawT)

    wsrc = {0: qw_s, 1: kw_s, 2: vw_s}
    whbm = {0: wq_h, 1: wk_h, 2: wv_h}
    bsrc = {0: qb_s, 1: kb_s, 2: vb_s}
    bdst = {0: bqe, 1: bke, 2: bve}
    a_w0 = c_a.n
    v_w0 = c_v.n
    s_w0 = c_wf.n
    p_w0 = c_p.n

    def pe_weff(t):
        t.wait_ge(c_g.sem, n_ld0)
        t.wait_ge(c_v.sem, v_t0 + 24)
        nslot = 0
        for wi in range(3):
            if wi > 0:
                t.wait_ge(c_v.sem, v_w0 + (2 if wi == 1 else 3))
                t.wait_ge(c_wf.sem, s_w0 + 16 * wi)
            for cc in range(NKC):
                for fh in range(2):
                    if nslot >= 4:
                        t.wait_ge(c_a.sem, a_w0 + nslot - 3)
                    for mc in range(NKC):
                        mcg = wi * NKC + mc
                        mm = t.matmul(pw[nslot % 4][:],
                                      cawT[:, mcg * C + 128 * cc: mcg * C + 128 * (cc + 1)],
                                      wsrc[wi][:, mc * C + 512 * fh: mc * C + 512 * (fh + 1)],
                                      start=(mc == 0), stop=(mc == NKC - 1))
                    c_p.inc(mm)
                    nslot += 1
            for fc in range(NKC):
                for mc in range(NKC):
                    mcg = wi * NKC + mc
                    mm = t.matmul(pb8[:, fc:fc + 1],
                                  wsrc[wi][:, mc * C + 128 * fc: mc * C + 128 * (fc + 1)],
                                  cab_s[:, mcg:mcg + 1],
                                  start=(mc == 0), stop=(mc == NKC - 1))
            c_p.inc(mm)
    block.tensor(pe_weff)

    def act_weff(a):
        nslot = 0
        for wi in range(3):
            for cc in range(NKC):
                for fh in range(2):
                    nslot += 1
                    a.wait_ge(c_p.sem, p_w0 + nslot + wi)
                    c_a.inc(a.activation(
                        out=weff_st[:, cc * C + 512 * fh: cc * C + 512 * (fh + 1)],
                        in_=pw[(nslot - 1) % 4][:], func=AF.Copy))
    block.scalar(act_weff)

    def dv_weff(v):
        for wi in range(3):
            v.wait_ge(c_p.sem, p_w0 + (wi + 1) * 17)
            if wi == 0:
                c_v.inc(v.tensor_scalar_mul(qb_s[:], qb_s[:], 0.125))
                v.drain()
                c_v.inc(v.scalar_tensor_tensor(
                    out=bqe[:], in0=pb8[:], scalar=0.125, in1=qb_s[:],
                    op0=OP.mult, op1=OP.add))
            else:
                c_v.inc(v.tensor_tensor(out=bdst[wi][:], in0=pb8[:], in1=bsrc[wi][:],
                                        op=OP.add))
    block.vector(dv_weff)

    def sy_weff(s):
        for wi in range(3):
            s.wait_ge(c_a.sem, a_w0 + (wi + 1) * 16)
            c_wf.inc(s.dma_start(out=whbm[wi], in_=weff_st[:]))
    block.sync(sy_weff)
    n_weff = c_wf.n

    if "t_weff" in tap_t:
        def sy_tw(s):
            s.wait_ge(c_a.sem, c_a.n)
            c_tap.inc(s.dma_start(out=tap_t["t_weff"], in_=weff_st[:]))
        block.sync(sy_tw)

    setup.close()

    p_weff_end = c_p.n
    a_weff_end = c_a.n
    v_weff_end = c_v.n
    gws = ExitStack()
    gw_s = gws.enter_context(nc.sbuf_tensor("gw_s", [16, C], F32))
    gwhi = gws.enter_context(nc.sbuf_tensor("gwhi", [16, C], BF16))
    gwmd = gws.enter_context(nc.sbuf_tensor("gwmd", [16, C], BF16))
    gwlo = gws.enter_context(nc.sbuf_tensor("gwlo", [16, C], BF16))
    ptg = gws.enter_context(nc.psum_tensor("ptg", [128, 384], BF16))
    c_gw = Ctr(nc, "c_gw", 16)

    def sy_gw(s):
        s.wait_ge(c_p.sem, p_weff_end)
        s.wait_ge(c_a.sem, a_weff_end)
        s.wait_ge(c_v.sem, v_weff_end)
        c_gw.inc(s.dma_start(out=gw_s[:], in_=gw_in))
    block.sync(sy_gw)

    def dv_gw(v):
        v.wait_ge(c_gw.sem, 16)
        v.wait_ge(c_a.sem, c_a.n)
        v.drain()
        v.tensor_copy(gwhi[:], gw_s[:])
        v.drain()
        v.tensor_tensor(out=gw_s[:], in0=gw_s[:], in1=gwhi[:], op=OP.subtract)
        v.drain()
        v.tensor_copy(gwmd[:], gw_s[:])
        v.drain()
        v.tensor_tensor(out=gw_s[:], in0=gw_s[:], in1=gwmd[:], op=OP.subtract)
        v.drain()
        c_v.inc(v.tensor_copy(gwlo[:], gw_s[:]))
    block.vector(dv_gw)
    v_gw = c_v.n

    def pe_gw(t):
        t.wait_ge(c_v.sem, v_gw)
        for hl, gsrc in enumerate((gwhi, gwmd, gwlo)):
            for cc in range(NKC):
                mm = t.transpose(ptg[:, 128 * hl + 16 * cc:128 * hl + 16 * (cc + 1)],
                                 gsrc[0:16, 128 * cc:128 * (cc + 1)], id16[0:16, 0:16])
        c_p.inc(mm)
    block.tensor(pe_gw)
    p_gw = c_p.n

    def dv_gwT(v):
        v.wait_ge(c_p.sem, p_gw)
        v.drain()
        last = None
        for hl in range(3):
            last = v.tensor_copy(
                gwT[:].rearrange("p (cc th) -> p cc th", th=48)[:, :, 16 * hl:16 * (hl + 1)],
                ptg[:, 128 * hl:128 * (hl + 1)].rearrange("p (cc s) -> p cc s", s=16))
        c_v.inc(last)
    block.vector(dv_gwT)
    n_gwT = c_v.n
    gws.close()

    prev_chunk_done = dict(v=c_v.n, o0=c_o[0].n, o1=c_o[1].n, p=c_p.n, a=c_a.n,
                           shw0=0, shw1=0, ds=0, x2r=0)
    ch_marks = []

    for ch in range(NCHUNK):
        chs = ExitStack()
        sbc = lambda name, shape, dt: chs.enter_context(
            nc.sbuf_tensor(f"{name}{ch}", shape, dt, side="right"))
        psc = lambda st_, name, shape, dt: st_.enter_context(nc.psum_tensor(f"{name}{ch}", shape, dt))

        xys = ExitStack()
        xT = xys.enter_context(nc.sbuf_tensor(f"xT{ch}", [128, NKC * CHUNK], BF16))
        Y = xys.enter_context(nc.sbuf_tensor(f"Y{ch}", [128, NKC * CHUNK], BF16))

        p1s = ExitStack()
        if ch == 0:
            x_rows = x_rows0
        else:
            x_rows = p1s.enter_context(
                nc.sbuf_tensor(f"x_rows{ch}", [128, NTC * C], BF16, side="right"))
        px = [psc(p1s, f"px{i}", [128, 2048], BF16) for i in range(2)]

        pcd = prev_chunk_done

        if ch > 0:
            def g_p1(g):
                g.wait_ge(c_v.sem, pcd["v"])
                g.wait_ge(c_shw[0].sem, pcd["shw0"])
                g.wait_ge(c_shw[1].sem, pcd["shw1"])
                g.wait_ge(c_ds.sem, pcd["ds"])
                g.wait_ge(c_x2r.sem, pcd["x2r"])
                c_x.inc(g.dma_start(
                    out=_d3(x_rows, C),
                    in_=_t3(x_in[ch * CHUNK:(ch + 1) * CHUNK, :])))
            block.gpsimd(g_p1)
        n_x = c_x.n

        p_b1 = c_p.n
        v_b1 = c_v.n

        def pe_p1(t):
            tw(t)
            t.wait_ge(c_x.sem, n_x)
            t.wait_ge(c_v.sem, pcd["v"])
            t.wait_ge(c_a.sem, pcd["a"])
            for cc in range(NKC):
                if cc >= 2:
                    t.wait_ge(c_v.sem, v_b1 + cc - 1)
                for tc in range(NTC):
                    mm = t.transpose(px[cc % 2][:, 128 * tc:128 * (tc + 1)],
                                     x_rows[:, tc * C + 128 * cc: tc * C + 128 * (cc + 1)],
                                     id16[:])
                c_p.inc(mm)
        block.tensor(pe_p1)

        def dv_p1(v):
            for cc in range(NKC):
                v.wait_ge(c_p.sem, p_b1 + cc + 1)
                c_v.inc(v.tensor_copy(xT[:, cc * CHUNK:(cc + 1) * CHUNK], px[cc % 2][:]))
        block.vector(dv_p1)
        n_xT = c_v.n
        p1s.close()
        if ch == 0:
            x0s.close()

        if "t_xT" in tap_t and ch == 0:
            def sy_t1(s):
                s.wait_ge(c_v.sem, n_xT)
                c_tap.inc(s.dma_start(out=tap_t["t_xT"], in_=xT[:]))
            block.sync(sy_t1)

        att = ExitStack()
        sba = lambda name, shape, dt: att.enter_context(nc.sbuf_tensor(f"{name}{ch}", shape, dt))
        vpl = sba("vpl", [128, NTC * C], BF16)
        vsr = sba("vsr", [128, NTC * C], BF16)
        y_pair = sba("y_pair", [128, CHUNK], F32)
        s_tot = sba("s_tot", [128, CHUNK], F32)
        p_bufs2 = [[sba(f"p_buf{up}_{hh}", [128, 1536], BF16) for hh in range(2)]
                   for up in range(2)]

        p2s = ExitStack()
        wvb = p2s.enter_context(nc.sbuf_tensor(f"wvb{ch}", [128, NKC * C], BF16))
        pv = [psc(p2s, f"pv{i}", [128, 512], F32) for i in range(4)]

        def sy_p2(s):
            s.wait_ge(c_wf.sem, n_weff)
            s.wait_ge(c_v.sem, n_xT)
            s.wait_ge(c_v.sem, pcd["v"])
            s.wait_ge(c_shw[0].sem, pcd["shw0"])
            s.wait_ge(c_shw[1].sem, pcd["shw1"])
            c_wv.inc(s.dma_start(out=wvb[:], in_=wv_h))
        block.sync(sy_p2)
        n_wv = c_wv.n

        p_b2 = c_p.n
        v_b2 = c_v.n

        def pe_p2(t):
            tw(t)
            t.wait_ge(c_wv.sem, n_wv)
            t.wait_ge(c_v.sem, n_xT)
            nslot = 0
            for form in range(2):
                for tc in range(NTC):
                    for fh in range(2):
                        if nslot >= 4:
                            t.wait_ge(c_v.sem, v_b2 + nslot - 3)
                        for cc in range(NKC):
                            if form == 0:
                                st_ap = xT[:, cc * CHUNK + 128 * tc: cc * CHUNK + 128 * (tc + 1)]
                            else:
                                cls, sblk = tc // 4, tc % 4
                                b0 = cc * CHUNK + cls + 512 * sblk
                                st_ap = xT[:, b0: b0 + 509: 4]
                            mm = t.matmul(pv[nslot % 4][:], st_ap,
                                          wvb[:, cc * C + 512 * fh: cc * C + 512 * (fh + 1)],
                                          start=(cc == 0), stop=(cc == NKC - 1))
                        c_p.inc(mm)
                        nslot += 1
        block.tensor(pe_p2)

        def dv_p2(v):
            nslot = 0
            for form in range(2):
                dst = vpl if form == 0 else vsr
                for tc in range(NTC):
                    for fh in range(2):
                        v.wait_ge(c_p.sem, p_b2 + nslot + 1)
                        c_v.inc(v.tensor_copy(
                            dst[:, tc * C + 512 * fh: tc * C + 512 * (fh + 1)],
                            pv[nslot % 4][:]))
                        nslot += 1
        block.vector(dv_p2)
        n_v = c_v.n
        p2s.close()

        if ch == 0:
            for nm, buf in (("t_vpl", vpl), ("t_vsr", vsr)):
                if nm in tap_t:
                    def sy_t2(s, nm=nm, buf=buf):
                        s.wait_ge(c_v.sem, n_v)
                        c_tap.inc(s.dma_start(out=tap_t[nm], in_=buf[:]))
                    block.sync(sy_t2)

        q_bufs = [sba(f"q_buf{i}", [128, CHUNK], BF16) for i in range(2)]
        k_bufs = [sba(f"k_buf{i}", [128, CHUNK], BF16) for i in range(2)]
        wqkb2 = [sba(f"wqkb{i}", [128, 2 * NKC * 128], BF16) for i in range(2)]
        p3ps = ExitStack()
        pqk = [psc(p3ps, f"pqk{i}", [128, 512], F32) for i in range(2)]
        SAb = [psc(p3ps, f"SA{hh}", [128, 1024], F32) for hh in range(2)]
        ob = psc(p3ps, "ob", [128, 512], F32)
        sb_ = psc(p3ps, "sb", [128, 512], F32)

        qk_pe = {}
        qk_dv = {}
        wq_marks = {}

        def sy_qk(s, pair):
            if pair == 0:
                s.wait_ge(c_wf.sem, n_weff)
                s.wait_ge(c_p.sem, p_b2 + 64)
            if pair >= 2:
                s.wait_ge(c_p.sem, qk_pe[(pair - 2, 7)])
            cw = c_wq[pair % 2]
            cw.inc(s.dma_start(
                out=wqkb2[pair % 2][:, 0:NKC * 128].rearrange("p (cc f) -> p cc f", f=128),
                in_=wq_h.rearrange("p (cc f) -> p cc f", f=C)[:, :, 128 * pair:128 * (pair + 1)]))
            cw.inc(s.dma_start(
                out=wqkb2[pair % 2][:, NKC * 128:].rearrange("p (cc f) -> p cc f", f=128),
                in_=wk_h.rearrange("p (cc f) -> p cc f", f=C)[:, :, 128 * pair:128 * (pair + 1)]))
            wq_marks[pair] = (pair % 2, cw.n)

        def pe_qk_slot(t, pair, slot):
            t.wait_ge(c_wq[wq_marks[pair][0]].sem, wq_marks[pair][1])
            if (pair, slot - 2) in qk_dv:
                t.wait_ge(c_a.sem, qk_dv[(pair, slot - 2)])
            elif slot < 2 and (pair - 1, 6 + slot) in qk_dv:
                t.wait_ge(c_a.sem, qk_dv[(pair - 1, 6 + slot)])
            which, tt = slot // 4, slot % 4
            for cc in range(NKC):
                mm = t.matmul(
                    pqk[slot % 2][:],
                    wqkb2[pair % 2][:, which * NKC * 128 + 128 * cc:
                                    which * NKC * 128 + 128 * (cc + 1)],
                    xT[:, cc * CHUNK + 512 * tt: cc * CHUNK + 512 * (tt + 1)],
                    start=(cc == 0), stop=(cc == NKC - 1))
            n_ = c_p.inc(mm)
            if (pair, slot) in qk_pe:
                assert n_ == qk_pe[(pair, slot)], (pair, slot, n_, qk_pe[(pair, slot)])
            else:
                qk_pe[(pair, slot)] = n_

        def act_qk_slot(a, pair, slot):
            a.wait_ge(c_p.sem, qk_pe[(pair, slot)])
            which, tt = slot // 4, slot % 4
            if which == 0:
                op = a.activation(
                    out=q_bufs[pair % 2][:, 512 * tt:512 * (tt + 1)],
                    in_=pqk[slot % 2][:], func=AF.Identity, scale=0.125,
                    bias=bqe[:, pair:pair + 1])
            else:
                op = a.activation(
                    out=k_bufs[pair % 2][:, 512 * tt:512 * (tt + 1)],
                    in_=pqk[slot % 2][:], func=AF.Identity,
                    bias=bke[:, pair:pair + 1])
            n_ = c_a.inc(op)
            if (pair, slot) in qk_dv:
                assert n_ == qk_dv[(pair, slot)], (pair, slot, n_, qk_dv[(pair, slot)])
            else:
                qk_dv[(pair, slot)] = n_

        def sy_qk0(s):
            sy_qk(s, 0)
        block.sync(sy_qk0)

        for s_ in range(8):
            qk_dv[(0, s_)] = c_a.n + s_ + 1

        def pe_qk0(t):
            t.wait_ge(c_v.sem, n_v)
            for slot in range(8):
                pe_qk_slot(t, 0, slot)
        block.tensor(pe_qk0)

        def act_qk0(a):
            for slot in range(8):
                act_qk_slot(a, 0, slot)
        block.scalar(act_qk0)

        a_marks = {}
        v_marks = {}
        pe_marks = {}
        n_Y_v = None

        nA = c_a.n
        nV = c_v.n
        nP = c_p.n
        for pair in range(NPAIR):
            has_next = pair + 1 < NPAIR
            for u in range(7):
                for hh in range(2):
                    for h in range(2):
                        nA += 1
                        a_marks[(pair, u, hh, h)] = nA
                if has_next and u < 7:
                    nA += 1
                    qk_dv[(pair + 1, u)] = nA
            if has_next:
                nA += 1
                qk_dv[(pair + 1, 7)] = nA
            for u in range(7):
                nV += 1
                v_marks[(pair, "m", u)] = nV
                nV += 1
                v_marks[(pair, "y", u)] = nV
            nV += 2
            v_marks[(pair, "Y")] = nV
            for u in range(7):
                if u >= 1:
                    nP += 1
                    pe_marks[(pair, "os", u - 1)] = nP
                for hh in range(2):
                    for h in range(2):
                        nP += 1
                        pe_marks[(pair, u, hh, h)] = nP
                if has_next and u < 7:
                    nP += 1
                    qk_pe[(pair + 1, u)] = nP
            nP += 1
            pe_marks[(pair, "os", 6)] = nP
            if has_next:
                nP += 1
                qk_pe[(pair + 1, 7)] = nP

        for pair in range(NPAIR):
            if pair + 1 < NPAIR:
                def sy_qkn(s, pair=pair):
                    sy_qk(s, pair + 1)
                block.sync(sy_qkn)

            scheds = []
            for (br, seg) in unit_list():
                scheds.append((br, seg,
                               [unit_rects(br, seg, 2 * pair + hh) for hh in range(2)]))

            def pe_pair(t, pair=pair, scheds=scheds):
                if pair == 0:
                    t.wait_ge(c_v.sem, n_v)
                t.wait_ge(c_a.sem, qk_dv[(pair, 7)])
                for u, (br, seg, rr) in enumerate(scheds):
                    if u >= 1:
                        emit_os(t, pair, u - 1, scheds[u - 1])
                    for hh in range(2):
                        rb = 64 * hh
                        for h in range(2):
                            if h == 0:
                                key = (pair, u - 1, hh, 1) if u > 0 else (pair - 1, 6, hh, 1)
                                if key in a_marks:
                                    t.wait_ge(c_a.sem, a_marks[key])
                            else:
                                t.wait_ge(c_a.sem, a_marks[(pair, u, hh, 0)])
                            rects = [r for r in rr[hh] if r["h"] == h]
                            base_slot = 0 if h == 0 else 2
                            for k_, r in enumerate(rects):
                                slot = base_slot + k_
                                kst, kstep = r["k"]
                                qst, qstep = r["q"]
                                sa_col = 256 * (slot - base_slot) if h == 1 else 256 * slot
                                mm = t.matmul(
                                    SAb[hh][:, sa_col:sa_col + 256],
                                    k_bufs[pair % 2][rb:rb + 64, kst: kst + 127 * kstep + 1: kstep],
                                    q_bufs[pair % 2][rb:rb + 64, qst: qst + 255 * qstep + 1: qstep],
                                    start=True, stop=True, tile_position=(rb, 0))
                            assert c_p.inc(mm) == pe_marks[(pair, u, hh, h)]
                    if pair + 1 < NPAIR and u < 7:
                        pe_qk_slot(t, pair + 1, u)
                emit_os(t, pair, 6, scheds[6])
                if pair + 1 < NPAIR:
                    pe_qk_slot(t, pair + 1, 7)

            def emit_os(t, pair, u, sched):
                br, seg, rr = sched
                p_bufs = p_bufs2[(7 * pair + u) % 2]
                t.wait_ge(c_v.sem, v_marks[(pair, "m", u)])
                key = (pair, "y", u - 1) if u > 0 else (pair - 1, "y", 6)
                if key in v_marks:
                    t.wait_ge(c_v.sem, v_marks[key])
                for hh in range(2):
                    n_r = len(rr[hh])
                    tp = (0, 64 * hh)
                    for slot, r in enumerate(rr[hh]):
                        vt, srt = r["v"]
                        vb_ap = (vsr if srt else vpl)[:, vt * C + 128 * pair + 64 * hh:
                                                      vt * C + 128 * pair + 64 * (hh + 1)]
                        st = r["start"]
                        mv = p_bufs[hh][:, 256 * slot + st:256 * (slot + 1)]
                        t.matmul(ob[64 * hh:64 * hh + 64, 256 * r["h"] + st:256 * (r["h"] + 1)],
                                 vb_ap, mv, start=(slot == 0), stop=(slot == n_r - 1),
                                 tile_position=tp, skip_group_check=True)
                    for slot, r in enumerate(rr[hh]):
                        st = r["start"]
                        mv = p_bufs[hh][:, 256 * slot + st:256 * (slot + 1)]
                        mm = t.matmul(sb_[64 * hh:64 * hh + 64, 256 * r["h"] + st:256 * (r["h"] + 1)],
                                      ones[:, 0:64], mv,
                                      start=(slot == 0), stop=(slot == n_r - 1),
                                      tile_position=tp, skip_group_check=True)
                assert c_p.inc(mm) == pe_marks[(pair, "os", u)]

            def act_pair(a, pair=pair):
                for u in range(7):
                    g_ = 7 * pair + u
                    p_bufs = p_bufs2[g_ % 2]
                    pg2 = g_ - 2
                    okey = None
                    if pg2 >= 0:
                        okey = (pg2 // 7, "os", pg2 % 7)
                    for hh in range(2):
                        a.wait_ge(c_p.sem, pe_marks[(pair, u, hh, 0)])
                        if okey is not None and okey in pe_marks:
                            a.wait_ge(c_p.sem, pe_marks[okey])
                        c_a.inc(a.activation(out=p_bufs[hh][:, 0:512],
                                             in_=SAb[hh][:, 0:512], func=AF.Exp))
                        assert c_a.n == a_marks[(pair, u, hh, 0)]
                        a.wait_ge(c_p.sem, pe_marks[(pair, u, hh, 1)])
                        c_a.inc(a.activation(out=p_bufs[hh][:, 512:1536],
                                             in_=SAb[hh][:, 0:1024], func=AF.Exp))
                        assert c_a.n == a_marks[(pair, u, hh, 1)]
                    if pair + 1 < NPAIR and u < 7:
                        act_qk_slot(a, pair + 1, u)
                if pair + 1 < NPAIR:
                    act_qk_slot(a, pair + 1, 7)

            def dv_pair(v, pair=pair, scheds=scheds):
                for u, (br, seg, rr) in enumerate(scheds):
                    last = None
                    p_bufs = p_bufs2[(7 * pair + u) % 2]
                    for hh in range(2):
                        v.wait_ge(c_a.sem, a_marks[(pair, u, hh, 1)])
                        for slot, r in enumerate(rr[hh]):
                            for (mid, loc) in r["masks"]:
                                last = v.tensor_tensor(
                                    out=p_bufs[hh][:, 256 * slot + loc:256 * slot + loc + 128],
                                    in0=p_bufs[hh][:, 256 * slot + loc:256 * slot + loc + 128],
                                    in1=masks[:, 128 * mid:128 * (mid + 1)], op=OP.mult)
                    assert c_v.inc(last) == v_marks[(pair, "m", u)]
                    v.wait_ge(c_p.sem, pe_marks[(pair, "os", u)])
                    v.drain()
                    hd0 = 2 * pair
                    for hh in range(2):
                        yst, ystep = unit_ycols(br, seg, hd0 + hh)
                        rb = 64 * hh
                        ydst = y_pair[rb:rb + 64, yst: yst + 511 * ystep + 1: ystep]
                        sdst = s_tot[rb:rb + 64, yst: yst + 511 * ystep + 1: ystep]
                        if br == "b1":
                            v.tensor_copy(ydst, ob[rb:rb + 64, :])
                            last = v.tensor_copy(sdst, sb_[rb:rb + 64, :])
                        else:
                            v.tensor_tensor(out=ydst, in0=ydst, in1=ob[rb:rb + 64, :], op=OP.add)
                            last = v.tensor_tensor(out=sdst, in0=sdst,
                                                   in1=sb_[rb:rb + 64, :], op=OP.add)
                    assert c_v.inc(last) == v_marks[(pair, "y", u)]
                v.drain()
                c_v.inc(v.reciprocal(s_tot[:], s_tot[:]))
                v.drain()
                v.tensor_tensor(out=y_pair[:], in0=y_pair[:], in1=s_tot[:], op=OP.mult)
                v.drain()
                c_v.inc(v.tensor_scalar(
                    out=Y[:, pair * CHUNK:(pair + 1) * CHUNK], in0=y_pair[:],
                    scalar1=bve[:, pair:pair + 1], scalar2=None, op0=OP.add))
                assert c_v.n == v_marks[(pair, "Y")]

            block.tensor(pe_pair)
            block.scalar(act_pair)
            block.vector(dv_pair)

            if ch == 0 and pair == 0:
                nyv = c_v.n
                for nm, buf in (("t_q", q_bufs[0]), ("t_k", k_bufs[0]),
                                ("t_y", y_pair), ("t_s", s_tot)):
                    if nm in tap_t:
                        def sy_t4(s, nm=nm, buf=buf, nyv=nyv):
                            s.wait_ge(c_v.sem, nyv)
                            c_tap.inc(s.dma_start(out=tap_t[nm], in_=buf[:]))
                        block.sync(sy_t4)

        p3ps.close()
        n_Y_v = c_v.n
        n_att_v = c_v.n
        att.close()

        if "t_Y" in tap_t and ch == 0:
            def sy_t5(s):
                s.wait_ge(c_v.sem, n_Y_v)
                c_tap.inc(s.dma_start(out=tap_t["t_Y"], in_=Y[:]))
            block.sync(sy_t5)

        x2 = sbc("x2", [128, NKC * CHUNK], BF16)
        meanb = sbc("meanb", [128, CHUNK], F32)
        rstd = sbc("rstd", [128, CHUNK], F32)
        sq2 = sbc("sq2", [128, 2 * 512], BF16)
        r1s = ExitStack()
        r1b = r1s.enter_context(nc.sbuf_tensor(f"r1b{ch}", [128, NKC * CHUNK], BF16))

        p4s = ExitStack()
        owb = p4s.enter_context(nc.sbuf_tensor(f"owb{ch}", [128, NKC * C], BF16))
        po4 = [psc(p4s, f"po4{i}", [128, 512], F32) for i in range(4)]

        def sy_p4(s):
            s.wait_ge(c_g2.sem, n_cast)
            s.wait_ge(c_v.sem, n_Y_v)
            c_ow.inc(s.dma_start(out=_d3(owb, C), in_=_t3(ow_h)))
        block.sync(sy_p4)
        n_ow = c_ow.n

        p_b4 = c_p.n
        v_b4 = c_v.n

        def pe_p4(t):
            tw(t)
            t.wait_ge(c_ow.sem, n_ow)
            t.wait_ge(c_v.sem, n_Y_v)
            nslot = 0
            for fc in range(NKC):
                for tt in range(4):
                    if nslot >= 4:
                        t.wait_ge(c_v.sem, v_b4 + nslot - 3)
                    for cc in range(NKC):
                        mm = t.matmul(po4[nslot % 4][:],
                                      owb[:, cc * C + 128 * fc: cc * C + 128 * (fc + 1)],
                                      Y[:, cc * CHUNK + 512 * tt: cc * CHUNK + 512 * (tt + 1)],
                                      start=(cc == 0), stop=(cc == NKC - 1))
                    c_p.inc(mm)
                    nslot += 1
        block.tensor(pe_p4)

        def dv_p4(v):
            nslot = 0
            for fc in range(NKC):
                for tt in range(4):
                    v.wait_ge(c_p.sem, p_b4 + nslot + 1)
                    c_v.inc(v.scalar_tensor_tensor(
                        out=r1b[:, fc * CHUNK + 512 * tt: fc * CHUNK + 512 * (tt + 1)],
                        in0=po4[nslot % 4][:], scalar=obs[:, fc:fc + 1],
                        in1=xT[:, fc * CHUNK + 512 * tt: fc * CHUNK + 512 * (tt + 1)],
                        op0=OP.add, op1=OP.add))
                    nslot += 1
        block.vector(dv_p4)
        p4s.close()
        if "t_r1" in tap_t and ch == 0:
            def sy_t6(s, n=c_v.n):
                s.wait_ge(c_v.sem, n)
                c_tap.inc(s.dma_start(out=tap_t["t_r1"], in_=r1b[:]))
            block.sync(sy_t6)

        _ln_id = [0]

        def ln_emit(srcb, dst):
            lns = ExitStack()
            _ln_id[0] += 1
            pm = [psc(lns, f"pm{_ln_id[0]}_{i}", [128, 512], F32) for i in range(4)]
            v_r = c_v.n
            p_l0 = c_p.n

            def pe_ln(t):
                t.wait_ge(c_v.sem, v_r)
                for tt in range(4):
                    for cc in range(NKC):
                        mm = t.matmul(pm[tt][:], ones[:],
                                      srcb[:, cc * CHUNK + 512 * tt: cc * CHUNK + 512 * (tt + 1)],
                                      start=(cc == 0), stop=(cc == NKC - 1))
                    c_p.inc(mm)
            block.tensor(pe_ln)

            def act_ln(a):
                for tt in range(4):
                    a.wait_ge(c_p.sem, p_l0 + tt + 1)
                    c_a.inc(a.activation(out=meanb[:, 512 * tt:512 * (tt + 1)],
                                         in_=pm[tt][:], func=AF.Copy, scale=1.0 / C))
            block.scalar(act_ln)
            a_mean = c_a.n

            v_c0 = c_v.n

            def dv_lnc(v):
                v.wait_ge(c_a.sem, a_mean)
                v.drain()
                last = None
                for cc in range(NKC):
                    last = v.tensor_tensor(out=srcb[:, cc * CHUNK:(cc + 1) * CHUNK],
                                           in0=srcb[:, cc * CHUNK:(cc + 1) * CHUNK],
                                           in1=meanb[:], op=OP.subtract)
                c_v.inc(last)
            block.vector(dv_lnc)
            v_cent = c_v.n

            p_l1 = c_p.n
            a_l1 = c_a.n

            def act_lnsq(a):
                a.wait_ge(c_v.sem, v_cent)
                k = 0
                for tt in range(4):
                    for cc in range(NKC):
                        if k >= 2:
                            a.wait_ge(c_p.sem, p_l1 + k - 1)
                        sl = srcb[:, cc * CHUNK + 512 * tt: cc * CHUNK + 512 * (tt + 1)]
                        c_a.inc(a.activation(
                            out=sq2[:, 512 * (k % 2):512 * (k % 2 + 1)],
                            in_=sl, func=AF.Square))
                        k += 1
            block.scalar(act_lnsq)

            def pe_ln2(t):
                k = 0
                for tt in range(4):
                    for cc in range(NKC):
                        t.wait_ge(c_a.sem, a_l1 + k + 1)
                        mm = t.matmul(pm[tt][:], ones[:],
                                      sq2[:, 512 * (k % 2):512 * (k % 2 + 1)],
                                      start=(cc == 0), stop=(cc == NKC - 1))
                        c_p.inc(mm)
                        k += 1
            block.tensor(pe_ln2)

            def act_ln2(a):
                for tt in range(4):
                    a.wait_ge(c_p.sem, p_l1 + (tt + 1) * NKC)
                    c_a.inc(a.activation(out=rstd[:, 512 * tt:512 * (tt + 1)],
                                         in_=pm[tt][:], func=AF.Copy, scale=1.0 / C,
                                         bias=EPS))
            block.scalar(act_ln2)
            a_var = c_a.n

            def dv_lnr(v):
                v.wait_ge(c_a.sem, a_var)
                c_v.inc(v.reciprocal(rstd[:], rstd[:]))
            block.vector(dv_lnr)
            v_rcp = c_v.n

            def act_ln3(a):
                a.wait_ge(c_v.sem, v_rcp)
                c_a.inc(a.activation(out=rstd[:], in_=rstd[:], func=AF.Sqrt))
            block.scalar(act_ln3)
            a_rstd = c_a.n

            def dv_ln3(v):
                v.wait_ge(c_a.sem, a_rstd)
                v.drain()
                last = None
                for cc in range(NKC):
                    last = v.tensor_tensor(out=dst[:, cc * CHUNK:(cc + 1) * CHUNK],
                                           in0=srcb[:, cc * CHUNK:(cc + 1) * CHUNK],
                                           in1=rstd[:], op=OP.mult)
                c_v.inc(last)
            block.vector(dv_ln3)
            lns.close()

        ln_emit(r1b, x2)
        r1s.close()
        xys.close()
        n_x2 = c_v.n
        if "t_x2" in tap_t and ch == 0:
            def sy_t7(s):
                s.wait_ge(c_v.sem, n_x2)
                c_tap.inc(s.dma_start(out=tap_t["t_x2"], in_=x2[:]))
            block.sync(sy_t7)

        seq1 = sbc("seq1", [128, NTC * 16], BF16)
        seq2 = sbc("seq2", [128, NTC * 16], BF16)
        sw1 = sbc("sw1", [128, NTC], F32)
        sw2 = sbc("sw2", [128, NTC], F32)
        x2rows = sbc("x2rows", [128, NTC * C], BF16)

        p5a = ExitStack()
        px2 = [psc(p5a, f"px2_{i}", [128, 1024], BF16) for i in range(2)]
        p_b5a = c_p.n
        a_b5a = c_a.n

        def pe_x2r(t):
            t.wait_ge(c_v.sem, n_x2)
            for tc in range(NTC):
                if tc >= 2:
                    t.wait_ge(c_a.sem, a_b5a + tc - 1)
                for cc in range(NKC):
                    mm = t.transpose(px2[tc % 2][:, 128 * cc:128 * (cc + 1)],
                                     x2[:, cc * CHUNK + 128 * tc: cc * CHUNK + 128 * (tc + 1)],
                                     id16[:])
                c_p.inc(mm)
        block.tensor(pe_x2r)

        def act_x2r(a):
            for tc in range(NTC):
                a.wait_ge(c_p.sem, p_b5a + tc + 1)
                c_a.inc(a.activation(out=x2rows[:, tc * C:(tc + 1) * C],
                                     in_=px2[tc % 2][:], func=AF.Identity))
        block.scalar(act_x2r)
        n_x2r_a = c_a.n

        def sy_x2r(s):
            if ch == 0:
                s.wait_ge(c_g2.sem, n_cast)
            for tc in range(NTC):
                s.wait_ge(c_a.sem, a_b5a + tc + 1)
                c_x2r.inc(s.dma_start(out=x2r_h[ch][128 * tc:128 * (tc + 1), :],
                                      in_=x2rows[:, tc * C:(tc + 1) * C]))
        block.sync(sy_x2r)
        n_x2rw = c_x2r.n
        p5a.close()

        p5s = ExitStack()
        sc_rows = sbc("sc_rows", [128, NTC * 16], F32)
        sc48 = sbc("sc48", [128, NTC * 48], F32)
        rs2 = sbc("rs2", [128, 16], F32)
        rm1 = sbc("rm1", [128, 16], F32)
        rm2 = sbc("rm2", [128, 16], F32)
        rwt = sbc("rwt", [128, 16], F32)
        sg1 = sbc("sg1", [128, 16], F32)
        sg2 = sbc("sg2", [128, 16], F32)
        plg = psc(p5s, "plg", [128, NTC * 64], F32)

        def pe_p5(t):
            tw(t)
            t.wait_ge(c_a.sem, n_x2r_a)
            t.wait_ge(c_v.sem, n_gwT)
            for tc in range(NTC):
                for cc in range(NKC):
                    mm = t.matmul(plg[:, 64 * tc:64 * tc + 48],
                                  x2[:, cc * CHUNK + 128 * tc: cc * CHUNK + 128 * (tc + 1)],
                                  gwT[:, 48 * cc:48 * (cc + 1)],
                                  start=(cc == 0), stop=(cc == NKC - 1))
                c_p.inc(mm)
        block.tensor(pe_p5)
        p_lg = c_p.n

        def act_p5(a):
            a.wait_ge(c_p.sem, p_lg)
            c_a.inc(a.activation(
                out=sc48[:].rearrange("p (tc s) -> p tc s", s=48),
                in_=plg[:].rearrange("p (tc s) -> p tc s", s=64)[:, :, 0:48],
                func=AF.Identity))
        block.scalar(act_p5)
        a_sc = c_a.n

        def dv_p5(v):
            v.wait_ge(c_a.sem, a_sc)
            for tc in range(NTC):
                sc = sc_rows[:, 16 * tc:16 * (tc + 1)]
                v.drain()
                v.tensor_tensor(out=sc, in0=sc48[:, 48 * tc:48 * tc + 16],
                                in1=sc48[:, 48 * tc + 16:48 * tc + 32], op=OP.add)
                v.drain()
                v.tensor_tensor(out=sc, in0=sc,
                                in1=sc48[:, 48 * tc + 32:48 * tc + 48], op=OP.add)
                v.drain()
                v.reduce_max(rm1[:, tc:tc + 1], sc, axis=mybir.AxisListType.X)
                v.drain()
                v.tensor_scalar(out=seq1[:, 16 * tc:16 * (tc + 1)], in0=sc,
                                scalar1=rm1[:, tc:tc + 1],
                                scalar2=None, op0=OP.is_equal)
                v.drain()
                v.scalar_tensor_tensor(out=rs2[:], in0=seq1[:, 16 * tc:16 * (tc + 1)],
                                       scalar=-1e9, in1=sc, op0=OP.mult, op1=OP.add)
                v.drain()
                v.reduce_max(rm2[:, tc:tc + 1], rs2[:], axis=mybir.AxisListType.X)
                v.drain()
                v.tensor_scalar(out=seq2[:, 16 * tc:16 * (tc + 1)], in0=rs2[:],
                                scalar1=rm2[:, tc:tc + 1],
                                scalar2=None, op0=OP.is_equal)
            v.drain()
            c_v.inc(v.tensor_copy(rwt[:], rm2[:]))
        block.vector(dv_p5)
        v_max5 = c_v.n

        def act_p5b(a):
            a.wait_ge(c_v.sem, v_max5)
            a.activation(out=sg1[:], in_=rm1[:], func=AF.Sigmoid)
            c_a.inc(a.activation(out=sg2[:], in_=rm2[:], func=AF.Sigmoid))
        block.scalar(act_p5b)
        a_sig5 = c_a.n

        def dv_p5c(v):
            v.wait_ge(c_a.sem, a_sig5)
            v.drain()
            v.tensor_tensor(out=rwt[:], in0=sg1[:], in1=sg2[:], op=OP.add)
            v.drain()
            v.tensor_scalar_add(rwt[:], rwt[:], 1e-20)
            v.drain()
            v.reciprocal(rwt[:], rwt[:])
            v.drain()
            v.tensor_scalar_mul(rwt[:], rwt[:], 2.5)
            v.drain()
            v.tensor_tensor(out=sw1[:], in0=sg1[:], in1=rwt[:], op=OP.mult)
            last = v.tensor_tensor(out=sw2[:], in0=sg2[:], in1=rwt[:], op=OP.mult)
            c_v.inc(last)
        block.vector(dv_p5c)
        n_route = c_v.n
        p5s.close()

        Dk = [sbc(f"Dk{k}", [128, NTC], F32) for k in range(2)]
        Dki = Dki2[ch]
        wbuf = [sbc(f"wbuf{k}", [128, 2 * NTC], F32) for k in range(2)]
        cpre = sbc("cpre", [128, 16], F32)
        cpre2 = sbc("cpre2", [128, 16], F32)
        rnk = sbc("rnk", [128, 16], F32)
        tot1 = sbc("tot1", [128, 16], F32)
        p5b = ExitStack()
        pdp = [psc(p5b, f"pdp{i}", [128, 256], F32) for i in range(2)]

        p_b5b = c_p.n
        v_b5b = c_v.n

        def pe_p5b(t):
            t.wait_ge(c_v.sem, n_route)
            for k in range(2):
                if k == 1:
                    t.wait_ge(c_v.sem, v_b5b + 1)
                sq = seq1 if k == 0 else seq2
                for e in range(NE):
                    mm = t.matmul(pdp[0][:, 16 * e:16 * (e + 1)], ltx[:],
                                  sq[:, e: e + 16 * (NTC - 1) + 1: 16],
                                  start=True, stop=True)
                    mm = t.matmul(pdp[1][:, 16 * e:16 * (e + 1)], ones[:],
                                  sq[:, e: e + 16 * (NTC - 1) + 1: 16],
                                  start=True, stop=True)
                c_p.inc(mm)
        block.tensor(pe_p5b)

        def dv_p5b(v):
            for k in range(2):
                v.wait_ge(c_p.sem, p_b5b + k + 1)
                v.memset(Dk[k][:], 0.0)
                for e in range(NE):
                    pp = pdp[0][:, 16 * e:16 * (e + 1)]
                    cs = pdp[1][:, 16 * e:16 * (e + 1)]
                    v.drain()
                    v.memset(cpre[:, 0:1], 0.0)
                    v.tensor_copy(cpre[:, 1:16], cs[:, 0:15])
                    v.drain()
                    v.tensor_copy(cpre2[:, 0:1], cpre[:, 0:1])
                    v.tensor_tensor(out=cpre2[:, 1:16], in0=cpre[:, 1:16],
                                    in1=cpre[:, 0:15], op=OP.add)
                    v.drain()
                    v.tensor_copy(cpre[:, 0:2], cpre2[:, 0:2])
                    v.tensor_tensor(out=cpre[:, 2:16], in0=cpre2[:, 2:16],
                                    in1=cpre2[:, 0:14], op=OP.add)
                    v.drain()
                    v.tensor_copy(cpre2[:, 0:4], cpre[:, 0:4])
                    v.tensor_tensor(out=cpre2[:, 4:16], in0=cpre[:, 4:16],
                                    in1=cpre[:, 0:12], op=OP.add)
                    v.drain()
                    v.tensor_copy(cpre[:, 0:8], cpre2[:, 0:8])
                    v.tensor_tensor(out=cpre[:, 8:16], in0=cpre2[:, 8:16],
                                    in1=cpre2[:, 0:8], op=OP.add)
                    v.drain()
                    if k == 0:
                        v.tensor_tensor(out=tot1[:, e:e + 1], in0=cpre[:, 15:16],
                                        in1=cs[:, 15:16], op=OP.add)
                        v.drain()
                    v.tensor_tensor(out=rnk[:], in0=pp, in1=cpre[:], op=OP.add)
                    v.drain()
                    if k == 0:
                        v.tensor_scalar_add(rnk[:], rnk[:], float(ECAP * e))
                    else:
                        v.tensor_scalar(out=rnk[:], in0=rnk[:], scalar1=tot1[:, e:e + 1],
                                        scalar2=float(ECAP * e), op0=OP.add, op1=OP.add)
                    v.drain()
                    v.tensor_scalar(out=rs1[:] if False else cpre[:], in0=rnk[:],
                                    scalar1=float(ECAP * (e + 1)), scalar2=None,
                                    op0=OP.is_lt)
                    v.drain()
                    tsl = trashc[:, 16 * k:16 * (k + 1)]
                    v.tensor_tensor(out=rnk[:], in0=rnk[:], in1=tsl, op=OP.subtract)
                    v.drain()
                    v.tensor_tensor(out=rnk[:], in0=rnk[:], in1=cpre[:], op=OP.mult)
                    v.drain()
                    v.tensor_tensor(out=rnk[:], in0=rnk[:], in1=tsl, op=OP.add)
                    v.drain()
                    sq = seq1 if k == 0 else seq2
                    sel_ap = sq[:, e: e + 16 * (NTC - 1) + 1: 16]
                    v.tensor_tensor(out=rnk[:], in0=rnk[:], in1=sel_ap, op=OP.mult)
                    v.drain()
                    last = v.tensor_tensor(out=Dk[k][:], in0=Dk[k][:], in1=rnk[:], op=OP.add)
                    v.drain()
                v.tensor_copy(Dki[k][:], Dk[k][:])
                v.tensor_copy(wbuf[k][:, 0:2 * NTC:2], iotac[:])
                v.drain()
                sw = sw1 if k == 0 else sw2
                last = v.tensor_copy(wbuf[k][:, 1:2 * NTC:2], sw[:])
                c_v.inc(last)
        block.vector(dv_p5b)
        n_disp = c_v.n

        def gq_p5b(g):
            g.wait_ge(c_v.sem, n_disp)
            g.wait_ge(c_x2r.sem, n_x2rw)
            for k in range(2):
                for tc in range(NTC):
                    c_ds.inc(g.indirect_dma_start(
                        out=disp_h[ch],
                        out_offset=bass.IndirectOffsetOnAxis(ap=Dki[k][:, tc:tc + 1], axis=0),
                        in_=wbuf[k][:, 2 * tc:2 * tc + 2], in_offset=None))
        block.gpsimd(gq_p5b)
        n_ds = c_ds.n
        p5b.close()

        p6sx = ExitStack()
        wsg_sb = sbc("wsg_sb", [128, NKC * MFF], BF16)
        wsu_sb = sbc("wsu_sb", [128, NKC * MFF], BF16)
        wsd_sb = sbc("wsd_sb", [128, 4 * C], BF16)

        def sy_wsld(s):
            s.wait_ge(c_g2.sem, n_cast)
            c_ws.inc(s.dma_start(out=wsg_sb[:], in_=wsg_h))
            c_ws.inc(s.dma_start(out=wsu_sb[:], in_=wsu_h))
            c_ws.inc(s.dma_start(out=wsd_sb[:], in_=wsd_h))
        block.sync(sy_wsld)
        n_ws = c_ws.n

        sg_b = sbc("sg_b", [128, 512], F32)
        h_bf = sbc("h_bf", [128, 2 * 4 * 256], BF16)
        eo_sb = [[sbc(f"eo_sb{g2}_{i2}", [128, C], F32) for i2 in range(2)]
                 for g2 in range(2)]
        pdnh = [psc(p6sx, f"pdnh{i2}", [128, 512], F32) for i2 in range(2)]
        pga = [psc(p6sx, f"pga{i2}", [128, 512], F32) for i2 in range(2)]
        pua = [psc(p6sx, f"pua{i2}", [128, 512], F32) for i2 in range(2)]

        NTAU_S = 8
        sp0, sa0, sv0 = c_p.n, c_a.n, c_v.n
        s_pe, s_a, s_v = {}, {}, {}

        def s_order():
            seq = []
            for tau in range(NTAU_S):
                if tau >= 2:
                    seq.append(("down", tau - 2))
                seq.append(("gu", tau))
            seq.append(("down", NTAU_S - 2))
            seq.append(("down", NTAU_S - 1))
            return seq

        n = sp0
        for item in s_order():
            if item[0] == "gu":
                for j in range(4):
                    n += 1
                    s_pe[("gu", item[1], j)] = n
            else:
                for q in range(4):
                    n += 1
                    s_pe[("downq", item[1], q)] = n
                s_pe[("down", item[1])] = n
        n = sa0
        for tau in range(NTAU_S):
            for mc in range(4):
                n += 1
                s_a[("silu", tau, mc)] = n
        n = sv0
        for tau in range(NTAU_S):
            for mc in range(4):
                n += 1
                s_v[("h", tau, mc)] = n
            if tau >= 1:
                for q in range(4):
                    n += 1
                    s_v[("evacq", tau - 1, q)] = n
        for q in range(4):
            n += 1
            s_v[("evacq", NTAU_S - 1, q)] = n
        n_p6s_v = n

        shw_b = (c_shw[0].n, c_shw[1].n)

        def sy_shw(s):
            for tau in range(NTAU_S):
                s.wait_ge(c_v.sem, s_v[("evacq", tau, 3)])
                for tb in range(2):
                    tc_g = 2 * tau + tb
                    c_shw[tau % 2].inc(s.dma_start(
                        out=acc_h[ch][128 * tc_g:128 * (tc_g + 1), :],
                        in_=eo_sb[tau % 2][tb][:]))
        block.sync(sy_shw)

        def pe_p6s(t):
            tw(t)
            t.wait_ge(c_ws.sem, n_ws)
            t.wait_ge(c_v.sem, n_x2)
            fdown = [True]
            dq = [0]
            for item in s_order():
                if item[0] == "gu":
                    tau = item[1]
                    for half in range(2):
                        for w_i in range(2):
                            bank = (pga if w_i == 0 else pua)[half]
                            wb_ = wsg_sb if w_i == 0 else wsu_sb
                            if tau >= 1:
                                if w_i == 0:
                                    t.wait_ge(c_a.sem, s_a[("silu", tau - 1, 2 * half + 1)])
                                else:
                                    t.wait_ge(c_v.sem, s_v[("h", tau - 1, 2 * half + 1)])
                            for mcl in range(2):
                                mc = 2 * half + mcl
                                for cc in range(NKC):
                                    mm = t.matmul(
                                        bank[:, 256 * mcl:256 * (mcl + 1)],
                                        wb_[:, cc * MFF + 128 * mc: cc * MFF + 128 * (mc + 1)],
                                        x2[:, cc * CHUNK + 256 * tau: cc * CHUNK + 256 * (tau + 1)],
                                        start=(cc == 0), stop=(cc == NKC - 1))
                            c_p.inc(mm)
                else:
                    tau = item[1]
                    if fdown[0]:
                        t.wait_ge(c_v.sem, n_disp)
                        fdown[0] = False
                    t.wait_ge(c_v.sem, s_v[("h", tau, 3)])
                    hb = 1024 * (tau % 2)
                    for q in range(4):
                        tb, fh = q % 2, q // 2
                        if dq[0] >= 2:
                            pt_, pq_ = divmod(dq[0] - 2, 4)
                            t.wait_ge(c_v.sem, s_v[("evacq", pt_, pq_)])
                        for mc in range(4):
                            mm = t.matmul(
                                pdnh[q % 2][:],
                                h_bf[:, hb + mc * 256 + 128 * tb: hb + mc * 256 + 128 * (tb + 1)],
                                wsd_sb[:, mc * C + 512 * fh: mc * C + 512 * (fh + 1)],
                                start=(mc == 0), stop=(mc == 3))
                        n_ = c_p.inc(mm)
                        assert n_ == s_pe[("downq", tau, q)]
                        dq[0] += 1
        block.tensor(pe_p6s)

        def act_p6s(a):
            for tau in range(NTAU_S):
                for mc in range(4):
                    a.wait_ge(c_p.sem, s_pe[("gu", tau, 2 * (mc // 2))])
                    gl = tau * 4 + mc
                    if gl >= 2:
                        t2, m2 = divmod(gl - 2, 4)
                        a.wait_ge(c_v.sem, s_v[("h", t2, m2)])
                    c_a.inc(a.activation(
                        out=sg_b[:, 256 * (mc % 2):256 * (mc % 2 + 1)],
                        in_=pga[mc // 2][:, 256 * (mc % 2):256 * (mc % 2 + 1)],
                        func=AF.Silu))
        block.scalar(act_p6s)

        def s_evac(v, tau):
            if tau >= 2:
                par = tau % 2
                v.wait_ge(c_shw[par].sem, shw_b[par] + 32 * (tau // 2))
            v.drain()
            for q in range(4):
                tb, fh = q % 2, q // 2
                v.wait_ge(c_p.sem, s_pe[("downq", tau, q)])
                tc_g = 2 * tau + tb
                op = v.tensor_tensor(
                    out=eo_sb[tau % 2][tb][:, 512 * fh:512 * (fh + 1)],
                    in0=pdnh[q % 2][:],
                    in1=x2rows[:, tc_g * C + 512 * fh: tc_g * C + 512 * (fh + 1)],
                    op=OP.add)
                n_ = c_v.inc(op)
                assert n_ == s_v[("evacq", tau, q)]

        def dv_p6s(v):
            for tau in range(NTAU_S):
                if tau >= 2:
                    v.wait_ge(c_p.sem, s_pe[("down", tau - 2)])
                hb = 1024 * (tau % 2)
                for mc in range(4):
                    v.wait_ge(c_a.sem, s_a[("silu", tau, mc)])
                    v.wait_ge(c_p.sem, s_pe[("gu", tau, 2 * (mc // 2) + 1)])
                    c_v.inc(v.tensor_tensor(
                        out=h_bf[:, hb + mc * 256: hb + (mc + 1) * 256],
                        in0=sg_b[:, 256 * (mc % 2):256 * (mc % 2 + 1)],
                        in1=pua[mc // 2][:, 256 * (mc % 2):256 * (mc % 2 + 1)],
                        op=OP.mult))
                if tau >= 1:
                    s_evac(v, tau - 1)
            s_evac(v, NTAU_S - 1)
        block.vector(dv_p6s)
        assert c_v.n == n_p6s_v
        p6sx.close()

        ch_marks.append(dict(n_ds=n_ds, n_x2rw=n_x2rw, n_disp=n_disp, n_x2=n_x2))


        prev_chunk_done = dict(v=c_v.n, o0=c_o[0].n, o1=c_o[1].n, p=c_p.n, a=c_a.n,
                               shw0=c_shw[0].n, shw1=c_shw[1].n,
                               ds=c_ds.n, x2r=c_x2r.n)
        chs.close()

    pass1_v = c_v.n
    p6r = ExitStack()
    sb7 = lambda name, shape, dt: p6r.enter_context(nc.sbuf_tensor(name, shape, dt))
    ps7 = lambda name, shape, dt: p6r.enter_context(nc.psum_tensor(name, shape, dt))
    wgb = [sb7(f"wgb{i}", [128, NKC * MFF], BF16) for i in range(2)]
    wub = [sb7(f"wub{i}", [128, NKC * MFF], BF16) for i in range(2)]
    wdb = [sb7(f"wdb{i}", [128, 4 * C], BF16) for i in range(2)]
    xgT = [sb7(f"xgT{i}", [128, NKC * ECAP], BF16) for i in range(4)]
    xgr = [sb7(f"xgr{i}", [128, (ECAP // 128) * C], BF16) for i in range(4)]
    idw = [sb7(f"idw{i}", [128, (ECAP // 128) * 2], F32) for i in range(4)]
    idi = [sb7(f"idi{i}", [128, ECAP // 128], mybir.dt.int32) for i in range(4)]
    sg_b2 = sb7("sg_b2", [128, 512], F32)
    h_bf2 = sb7("h_bf2", [128, 2 * 4 * 256], BF16)
    eo_bf = [[sb7(f"eo_bf{g2}_{i2}", [128, C], BF16) for i2 in range(2)]
             for g2 in range(2)]
    pdn2 = [ps7(f"pdn2_{i}", [128, 512], F32) for i in range(2)]
    ptt = [ps7(f"ptt{i}", [128, 1024], BF16) for i in range(2)]
    pga2 = [ps7(f"pga2_{i}", [128, 512], F32) for i in range(2)]
    pua2 = [ps7(f"pua2_{i}", [128, 512], F32) for i in range(2)]

    NU = 2 * NE
    ue = lambda u: u // 2
    uc = lambda u: u % 2

    t_at_unit = {u: [] for u in range(NU)}
    t_at_unit[0] = [0, 1, 2]
    for u in range(1, NU - 2):
        t_at_unit[u].append(u + 2)

    r_pe, r_a, r_v = {}, {}, {}
    tr_seq = []
    dq_seq = []
    u_items = (("gu", 0), ("gu", 1), ("down", 0), ("down", 1))
    n = c_p.n
    for u in range(NU):
        for u_t in t_at_unit[u]:
            for j in range(4):
                n += 1
                r_pe[("tr", u_t, j)] = n
                tr_seq.append((u_t, j))
        for item in u_items:
            if item[0] == "gu":
                for j in range(4):
                    n += 1
                    r_pe[("gu", u, item[1], j)] = n
            else:
                for q in range(4):
                    n += 1
                    r_pe[("downq", u, item[1], q)] = n
                    dq_seq.append((u, item[1], q))
                r_pe[("down", u, item[1])] = n
    n = c_a.n
    for u in range(NU):
        for u_t in t_at_unit[u]:
            for j in range(4):
                n += 1
                r_a[("trv", u_t, j)] = n
        for tau in range(2):
            for mc in range(4):
                n += 1
                r_a[("silu", u, tau, mc)] = n
    n = c_v.n
    for u in range(NU):
        for tau in range(2):
            for mc in range(4):
                n += 1
                r_v[("h", u, tau, mc)] = n
            if tau >= 1:
                for q in range(4):
                    n += 1
                    r_v[("evacq", u, 0, q)] = n
        for q in range(4):
            n += 1
            r_v[("evacq", u, 1, q)] = n
    n_p6r_v = n

    gx_b2 = tuple(c_gx[i].n for i in range(4))
    gx_marks2 = {u: (u % 4, gx_b2[u % 4] + 64 * (u // 4 + 1)) for u in range(NU)}
    idc_b2 = c_idc.n
    v_idc2 = {u: idc_b2 + u + 1 for u in range(NU)}
    eo_b = (c_eo[0].n, c_eo[1].n)
    evac_consumer2 = {}
    _eo_n = [0, 0]
    for u in range(NU):
        for tau in range(2):
            g_ = 2 * u + tau
            _eo_n[g_ % 2] += 2
            evac_consumer2[g_] = (g_ % 2, eo_b[g_ % 2] + 16 * _eo_n[g_ % 2])

    sy_marks = {}
    id_marks = {}

    def sy_ffnr(s):
        s.wait_ge(c_v.sem, pass1_v)
        s.wait_ge(c_shw[0].sem, c_shw[0].n)
        s.wait_ge(c_shw[1].sem, c_shw[1].n)
        s.wait_ge(c_x2r.sem, c_x2r.n)
        s.wait_ge(c_ds.sem, c_ds.n)

        def wload(e):
            if e >= 2:
                s.wait_ge(c_p.sem, r_pe[("down", 2 * (e - 2) + 1, 1)])
            ce = c_e[e % 2]
            ce.inc(s.dma_start(out=wgb[e % 2][:], in_=wg_h[128 * e:128 * (e + 1), :]))
            ce.inc(s.dma_start(out=wub[e % 2][:], in_=wu_h[128 * e:128 * (e + 1), :]))
            ce.inc(s.dma_start(out=wdb[e % 2][:], in_=wd_h[128 * e:128 * (e + 1), :]))
            sy_marks[e] = (e % 2, ce.n)

        wload(0)
        wload(1)
        for u in range(2, NU):
            if u % 2 == 0 and ue(u) + 1 < NE:
                wload(ue(u) + 1)
    block.sync(sy_ffnr)

    def gq_p6r(g):
        g.wait_ge(c_v.sem, pass1_v)
        g.wait_ge(c_x2r.sem, c_x2r.n)
        g.wait_ge(c_ds.sem, c_ds.n)
        g.wait_ge(c_shw[0].sem, c_shw[0].n)
        g.wait_ge(c_shw[1].sem, c_shw[1].n)

        def idload(u):
            b = u % 4
            if u >= 4:
                g.wait_ge(c_v.sem, r_v[("evacq", u - 4, 1, 3)])
            c_id[b].inc(g.dma_start(
                out=idw[b][:].rearrange("p (r two) -> p r two", two=2),
                in_=_t3(disp_h[uc(u)][ECAP * ue(u):ECAP * (ue(u) + 1), :])))
            id_marks[u] = (b, c_id[b].n)

        def gathers(u):
            b = u % 4
            bb = u % 4
            g.wait_ge(c_x2r.sem, ch_marks[uc(u)]["n_x2rw"])
            g.wait_ge(c_idc.sem, v_idc2[u])
            if u >= 4:
                g.wait_ge(c_p.sem, r_pe[("tr", u - 4, 3)])
            for rc in range(ECAP // 128):
                c_gx[b].inc(g.indirect_dma_start(
                    out=xgr[bb][:, rc * C:(rc + 1) * C], out_offset=None,
                    in_=x2r_h[uc(u)],
                    in_offset=bass.IndirectOffsetOnAxis(ap=idi[bb][:, rc:rc + 1], axis=0)))
            assert gx_marks2[u] == (b, c_gx[b].n)

        def eowrites(u):
            for tau in range(2):
                g.wait_ge(c_v.sem, r_v[("evacq", u, tau, 3)])
                for tb in range(2):
                    rc = 2 * tau + tb
                    c_eo[tau % 2].inc(g.dma_start(
                        out=eo_h[uc(u)][ECAP * ue(u) + 128 * rc:
                                        ECAP * ue(u) + 128 * (rc + 1), :],
                        in_=eo_bf[tau % 2][tb][:]))

        for u in range(4):
            idload(u)
        gathers(0)
        gathers(1)
        gathers(2)
        gathers(3)
        for u in range(NU):
            eowrites(u)
            if u + 4 < NU:
                idload(u + 4)
                gathers(u + 4)
    block.gpsimd(gq_p6r)
    n_eo = (c_eo[0].n, c_eo[1].n)

    def pe_p6r(t):
        tw(t)
        t.wait_ge(c_v.sem, pass1_v)
        tr_i = [0]
        dq_i = [0]
        for u in range(NU):
            e = ue(u)
            t.wait_ge(c_e[sy_marks[e][0]].sem, sy_marks[e][1])
            for u_t in t_at_unit[u]:
                bt = u_t % 4
                t.wait_ge(c_gx[gx_marks2[u_t][0]].sem, gx_marks2[u_t][1])
                for j in range(4):
                    gi = tr_i[0]
                    if gi >= 2:
                        pu_, pj_ = tr_seq[gi - 2]
                        t.wait_ge(c_a.sem, r_a[("trv", pu_, pj_)])
                    for cc in range(NKC):
                        mm = t.transpose(
                            ptt[gi % 2][:, 128 * cc:128 * (cc + 1)],
                            xgr[bt][:, j * C + 128 * cc: j * C + 128 * (cc + 1)],
                            id16[:])
                    n_ = c_p.inc(mm)
                    assert n_ == r_pe[("tr", u_t, j)]
                    tr_i[0] += 1
            t.wait_ge(c_a.sem, r_a[("trv", u, 3)])
            for item in u_items:
                if item[0] == "gu":
                    tau = item[1]
                    prev = None
                    if (u, tau) != (0, 0):
                        prev = (u, 0) if tau == 1 else (u - 1, 1)
                    for half in range(2):
                        for w_i in range(2):
                            bank = (pga2 if w_i == 0 else pua2)[half]
                            wbuf_ = (wgb if w_i == 0 else wub)[e % 2]
                            if prev is not None:
                                if w_i == 0:
                                    t.wait_ge(c_a.sem,
                                              r_a[("silu",) + prev + (2 * half + 1,)])
                                else:
                                    t.wait_ge(c_v.sem,
                                              r_v[("h",) + prev + (2 * half + 1,)])
                            for mcl in range(2):
                                mc = 2 * half + mcl
                                for cc in range(NKC):
                                    mm = t.matmul(
                                        bank[:, 256 * mcl:256 * (mcl + 1)],
                                        wbuf_[:, cc * MFF + 128 * mc: cc * MFF + 128 * (mc + 1)],
                                        xgT[u % 4][:, cc * ECAP + 256 * tau:
                                                   cc * ECAP + 256 * (tau + 1)],
                                        start=(cc == 0), stop=(cc == NKC - 1))
                            c_p.inc(mm)
                else:
                    tau = item[1]
                    t.wait_ge(c_v.sem, r_v[("h", u, tau, 3)])
                    hb = 1024 * (tau % 2)
                    for q in range(4):
                        tb, fh = q % 2, q // 2
                        gi = dq_i[0]
                        if gi >= 2:
                            k_ = dq_seq[gi - 2]
                            t.wait_ge(c_v.sem, r_v[("evacq",) + k_])
                        for mc in range(4):
                            mm = t.matmul(
                                pdn2[gi % 2][:],
                                h_bf2[:, hb + mc * 256 + 128 * tb:
                                      hb + mc * 256 + 128 * (tb + 1)],
                                wdb[e % 2][:, mc * C + 512 * fh: mc * C + 512 * (fh + 1)],
                                start=(mc == 0), stop=(mc == 3))
                        n_ = c_p.inc(mm)
                        assert n_ == r_pe[("downq", u, tau, q)]
                        dq_i[0] += 1
    block.tensor(pe_p6r)

    def act_p6r(a):
        tr_i = [0]
        for u in range(NU):
            for u_t in t_at_unit[u]:
                bt = u_t % 4
                if u_t >= 4:
                    a.wait_ge(c_p.sem, r_pe[("gu", u_t - 4, 1, 3)])
                for j in range(4):
                    a.wait_ge(c_p.sem, r_pe[("tr", u_t, j)])
                    n_ = c_a.inc(a.activation(
                        out=xgT[bt][:].rearrange("p (cc j2) -> p cc j2",
                                                 j2=ECAP)[:, :, 128 * j:128 * (j + 1)],
                        in_=ptt[tr_i[0] % 2][:], func=AF.Identity))
                    assert n_ == r_a[("trv", u_t, j)]
                    tr_i[0] += 1
            for tau in range(2):
                for mc in range(4):
                    a.wait_ge(c_p.sem, r_pe[("gu", u, tau, 2 * (mc // 2))])
                    gl = (2 * u + tau) * 4 + mc
                    if gl >= 2:
                        u2, r2 = divmod(gl - 2, 8)
                        tau2, mc2 = divmod(r2, 4)
                        a.wait_ge(c_v.sem, r_v[("h", u2, tau2, mc2)])
                    c_a.inc(a.activation(
                        out=sg_b2[:, 256 * (mc % 2):256 * (mc % 2 + 1)],
                        in_=pga2[mc // 2][:, 256 * (mc % 2):256 * (mc % 2 + 1)],
                        func=AF.Silu))
    block.scalar(act_p6r)

    def emit_cast2(v, u):
        b = u % 4
        v.wait_ge(c_id[id_marks[u][0]].sem, id_marks[u][1])
        cp = v.tensor_copy(idi[b][:], idw[b][:, 0:2 * (ECAP // 128):2])
        cp.then_inc(c_idc.sem, 1)
        c_idc.n += 1
        assert c_idc.n == v_idc2[u]

    def r_evac(v, u, tau):
        g_ = 2 * u + tau
        if g_ >= 2:
            par, cnt = evac_consumer2[g_ - 2]
            v.wait_ge(c_eo[par].sem, cnt)
        v.drain()
        for q in range(4):
            tb, fh = q % 2, q // 2
            v.wait_ge(c_p.sem, r_pe[("downq", u, tau, q)])
            rc = 2 * tau + tb
            op = v.tensor_scalar(
                out=eo_bf[tau % 2][tb][:, 512 * fh:512 * (fh + 1)],
                in0=pdn2[q % 2][:],
                scalar1=idw[u % 4][:, 2 * rc + 1:2 * rc + 2], scalar2=None,
                op0=OP.mult)
            n_ = c_v.inc(op)
            assert n_ == r_v[("evacq", u, tau, q)]

    def dv_p6r(v):
        emit_cast2(v, 0)
        emit_cast2(v, 1)
        emit_cast2(v, 2)
        emit_cast2(v, 3)
        for u in range(NU):
            for tau in range(2):
                gt_ = 2 * u + tau
                if gt_ >= 2:
                    u2, tau2 = divmod(gt_ - 2, 2)
                    v.wait_ge(c_p.sem, r_pe[("down", u2, tau2)])
                hb = 1024 * (tau % 2)
                for mc in range(4):
                    v.wait_ge(c_a.sem, r_a[("silu", u, tau, mc)])
                    v.wait_ge(c_p.sem, r_pe[("gu", u, tau, 2 * (mc // 2) + 1)])
                    c_v.inc(v.tensor_tensor(
                        out=h_bf2[:, hb + mc * 256: hb + (mc + 1) * 256],
                        in0=sg_b2[:, 256 * (mc % 2):256 * (mc % 2 + 1)],
                        in1=pua2[mc // 2][:, 256 * (mc % 2):256 * (mc % 2 + 1)],
                        op=OP.mult))
                if tau >= 1:
                    r_evac(v, u, 0)
            r_evac(v, u, 1)
            if u + 4 < NU:
                emit_cast2(v, u + 4)
    block.vector(dv_p6r)
    assert c_v.n == n_p6r_v, (c_v.n, n_p6r_v)
    p6r.close()

    p7s = ExitStack()
    accr = [p7s.enter_context(nc.sbuf_tensor(f"accr_{i}", [128, C], F32))
            for i in range(2)]
    outr = [p7s.enter_context(nc.sbuf_tensor(f"outr_{i}", [128, C], F32))
            for i in range(2)]
    bnst = p7s.enter_context(nc.sbuf_tensor("bnst", [128, 12], F32))
    bnag = p7s.enter_context(nc.sbuf_tensor("bnag", [128, 2], F32))
    rstd2 = p7s.enter_context(nc.sbuf_tensor("rstd2", [128, 1], F32))
    egb = [[p7s.enter_context(nc.sbuf_tensor(f"egb_{k}_{i}", [128, C], BF16))
            for i in range(2)] for k in range(2)]

    NT2 = NCHUNK * NTC
    v_b7 = c_v.n
    a_b7 = c_a.n
    o_b7 = [c_o[0].n, c_o[1].n]
    ac_b = [c_ac[0].n, c_ac[1].n]
    eg_b = [c_eg[0].n, c_eg[1].n]
    pe_last = r_pe[("down", NU - 1, 1)]

    def sy_p7(s):
        def store(ti):
            b = ti % 2
            s.wait_ge(c_v.sem, v_b7 + 2 * ti + 2)
            c_o[b].inc(s.dma_start(
                out=out_hbm[128 * ti: 128 * (ti + 1), :], in_=outr[b][:]))
        s.wait_ge(c_shw[0].sem, c_shw[0].n)
        s.wait_ge(c_shw[1].sem, c_shw[1].n)
        for cc_ in (c_e[0], c_e[1], c_gx[0], c_gx[1], c_gx[2], c_gx[3],
                    c_id[0], c_id[1], c_id[2], c_id[3], c_eo[0], c_eo[1]):
            s.wait_ge(cc_.sem, cc_.n)
        s.wait_ge(c_p.sem, pe_last)
        s.wait_ge(c_v.sem, n_p6r_v)
        for ti in range(NT2):
            b = ti % 2
            if ti >= 2:
                store(ti - 2)
            ch2, tc = divmod(ti, NTC)
            c_ac[b].inc(s.dma_start(out=accr[b][:],
                                    in_=acc_h[ch2][128 * tc:128 * (tc + 1), :]))
        store(NT2 - 2)
        store(NT2 - 1)
    block.sync(sy_p7)

    def gq_p7(g):
        g.wait_ge(c_eo[0].sem, n_eo[0])
        g.wait_ge(c_eo[1].sem, n_eo[1])
        for cc_ in (c_e[0], c_e[1], c_id[0], c_id[1], c_id[2], c_id[3]):
            g.wait_ge(cc_.sem, cc_.n)
        g.wait_ge(c_p.sem, pe_last)
        for ti in range(NT2):
            b = ti % 2
            ch2, tc = divmod(ti, NTC)
            if ti >= 2:
                g.wait_ge(c_v.sem, v_b7 + 2 * (ti - 2) + 1)
            for k in range(2):
                c_eg[b].inc(g.indirect_dma_start(
                    out=egb[k][b][:], out_offset=None, in_=eo_h[ch2],
                    in_offset=bass.IndirectOffsetOnAxis(ap=Dki2[ch2][k][:, tc:tc + 1],
                                                        axis=0)))
    block.gpsimd(gq_p7)

    def dv_p7(v):
        for ti in range(NT2):
            b = ti % 2
            v.wait_ge(c_ac[b].sem, ac_b[b] + 16 * (ti // 2 + 1))
            v.wait_ge(c_eg[b].sem, eg_b[b] + 32 * (ti // 2 + 1))
            v.drain()
            v.tensor_tensor(out=accr[b][:], in0=accr[b][:], in1=egb[0][b][:],
                            op=OP.add)
            v.drain()
            v.tensor_tensor(out=accr[b][:], in0=accr[b][:], in1=egb[1][b][:],
                            op=OP.add)
            v.drain()
            v.bn_stats(bnst[:, 0:6], accr[b][:, 0:512])
            v.bn_stats(bnst[:, 6:12], accr[b][:, 512:1024])
            v.drain()
            v.bn_aggr(bnag[:], bnst[:])
            v.drain()
            v.tensor_scalar_add(rstd2[:], bnag[:, 1:2], EPS)
            v.drain()
            c_v.inc(v.reciprocal(rstd2[:], rstd2[:]))
            v.wait_ge(c_a.sem, a_b7 + ti + 1)
            if ti >= 2:
                v.wait_ge(c_o[b].sem, o_b7[b] + 16 * (ti // 2))
            v.drain()
            c_v.inc(v.tensor_scalar(
                out=outr[b][:], in0=accr[b][:], scalar1=bnag[:, 0:1],
                scalar2=rstd2[:], op0=OP.subtract, op1=OP.mult))
    block.vector(dv_p7)

    def act_p7(a):
        for ti in range(NT2):
            a.wait_ge(c_v.sem, v_b7 + 2 * ti + 1)
            c_a.inc(a.activation(out=rstd2[:], in_=rstd2[:], func=AF.Sqrt))
    block.scalar(act_p7)
    p7s.close()


    def sy_final(s):
        s.wait_ge(c_o[0].sem, c_o[0].n)
        s.wait_ge(c_o[1].sem, c_o[1].n)
        if c_tap.n:
            s.wait_ge(c_tap.sem, c_tap.n)
    block.sync(sy_final)

    blk_cm.__exit__(None, None, None)
    stk.close()
    return nc


def _host_consts():
    i = np.arange(128)[:, None]
    j = np.arange(128)[None, :]
    m_list = [(j >= i)]
    for dp in (0, 1):
        for t_ in (0, 1):
            m_list.append((j + 128 * t_) >= (2 * i + dp))
    masks = np.concatenate([m.astype(np.float32) for m in m_list], axis=1)
    sel = np.zeros((16, 16 * 128), np.float32)
    for e in range(16):
        sel[e, 128 * e:128 * (e + 1)] = 1.0
    ltx = (np.arange(128)[:, None] < np.arange(128)[None, :]).astype(np.float32)
    iota = (np.arange(NTC)[None, :] * 128 + np.arange(128)[:, None]).astype(np.float32)
    dinit = np.zeros((NE * ECAP + 128, 2), np.float32)
    dinit[:, 0] = CHUNK
    return dict(
        id32=np.eye(128, dtype=np.float32),
        id16=np.eye(128, dtype=np.float32).astype(ml_dtypes.bfloat16),
        ones=np.ones((128, 128), dtype=np.float32).astype(ml_dtypes.bfloat16),
        masks=masks.astype(ml_dtypes.bfloat16),
        sel=sel.astype(ml_dtypes.bfloat16),
        ltx=ltx.astype(ml_dtypes.bfloat16),
        iota=iota,
        dinit=dinit,
        zrow=np.zeros((1, C), ml_dtypes.bfloat16),
        zrows=np.zeros((128, C), ml_dtypes.bfloat16),
        trashc=np.broadcast_to(
            NE * ECAP + np.arange(2 * NTC, dtype=np.float32)[None, :],
            (128, 2 * NTC)).copy(),
    )


def kernel(L0_attn_x, L0_attn_c_attn_w, L0_attn_c_attn_b, L0_attn_q_w, L0_attn_q_b,
           L0_attn_k_w, L0_attn_k_b, L0_attn_v_w, L0_attn_v_b, L0_attn_o_w, L0_attn_o_b,
           L0_ffn_gate_w, L0_ffn_w_gate, L0_ffn_w_up, L0_ffn_w_down,
           L0_ffn_ws_gate, L0_ffn_ws_up, L0_ffn_ws_down, _taps=(), _cores=None):
    nc = build(taps=_taps)
    xf = np.ascontiguousarray(np.asarray(L0_attn_x).reshape(B * T, C))
    consts = _host_consts()
    shared = dict(
        caw=np.ascontiguousarray(L0_attn_c_attn_w), cab=np.ascontiguousarray(L0_attn_c_attn_b),
        qw=np.ascontiguousarray(L0_attn_q_w), qb=np.ascontiguousarray(L0_attn_q_b),
        kw=np.ascontiguousarray(L0_attn_k_w), kb=np.ascontiguousarray(L0_attn_k_b),
        vw=np.ascontiguousarray(L0_attn_v_w), vb=np.ascontiguousarray(L0_attn_v_b),
        ow=np.ascontiguousarray(L0_attn_o_w), ob=np.ascontiguousarray(L0_attn_o_b),
        gw=np.ascontiguousarray(L0_ffn_gate_w), wg=np.ascontiguousarray(L0_ffn_w_gate),
        wu=np.ascontiguousarray(L0_ffn_w_up), wd=np.ascontiguousarray(L0_ffn_w_down),
        wsg=np.ascontiguousarray(L0_ffn_ws_gate), wsu=np.ascontiguousarray(L0_ffn_ws_up),
        wsd=np.ascontiguousarray(L0_ffn_ws_down), **consts)
    cores = list(range(NCORES)) if _cores is None else _cores
    in_maps = []
    for ci in range(len(cores)):
        m = dict(shared)
        m["x"] = np.ascontiguousarray(xf[ci * TPC:(ci + 1) * TPC])
        in_maps.append(m)
    res = run_bass_kernel_spmd(nc, in_maps, core_ids=cores)
    if _taps:
        return res
    out = np.concatenate([r["out"] for r in res.results], axis=0)
    return np.ascontiguousarray(out.reshape(B, T, C).astype(np.float32))


# version 31
# speedup vs baseline: 1.0813x; 1.0206x over previous
import numpy as np
import ml_dtypes
from contextlib import ExitStack

import concourse.bass as bass
import concourse.mybir as mybir
from concourse.bass_utils import run_bass_kernel_spmd

F32 = mybir.dt.float32
BF16 = mybir.dt.bfloat16
AF = mybir.ActivationFunctionType
OP = mybir.AluOpType

B, T, C, H, D = 2, 16384, 1024, 16, 64
NCORES = 8
TPC = (B * T) // NCORES
CHUNK = 2048
NCHUNK = TPC // CHUNK
NPAIR = H // 2
NE = 16
MFF = 512
EPS = 1e-5
NKC = C // 128
ECAP = 512
NTC = CHUNK // 128


class Ctr:
    def __init__(self, nc, name, step=1):
        self.sem = nc.semaphore(name).__enter__()
        self.step = step
        self.n = 0

    def inc(self, instr):
        instr.then_inc(self.sem, self.step)
        self.n += self.step
        return self.n


def unit_list():
    return [("b1", s) for s in range(4)] + [("b2", s) for s in range(2)] + [("b3", 0)]


def unit_rects(branch, seg, head):
    rects = []
    if branch in ("b1", "b3"):
        if branch == "b1":
            kstart = lambda kc: (512 * seg + 128 * kc, 1)
            qstart = lambda h: (512 * seg + 256 * h, 1)
            vtile = lambda kc: (4 * seg + kc, False)
        else:
            c = head % 4
            kstart = lambda kc: (c + 512 * kc, 4)
            qstart = lambda h: (c + 1024 * h, 4)
            vtile = lambda kc: (4 * c + kc, True)
        for h in (0, 1):
            for kc in range(2 * (h + 1)):
                o = 128 * kc - 256 * h
                masks = [(0, o)] if 0 <= o < 256 else []
                rects.append(dict(h=h, k=kstart(kc), q=qstart(h), v=vtile(kc),
                                  start=max(0, o), masks=masks))
    else:
        p = head % 2
        for h in (0, 1):
            for h2 in range(h + 1):
                for dp in (0, 1):
                    c = p + 2 * dp
                    masks = [(1 + 2 * dp + t, 128 * t) for t in (0, 1)] if h == h2 else []
                    rects.append(dict(h=h, k=(c + 1024 * seg + 512 * h2, 4),
                                      q=(1024 * seg + p + 512 * h, 2),
                                      v=(4 * c + 2 * seg + h2, True),
                                      start=0, masks=masks))
    return rects


def unit_ycols(branch, seg, head):
    if branch == "b1":
        return (512 * seg, 1)
    if branch == "b2":
        return (1024 * seg + head % 2, 2)
    return (head % 4, 4)


def _t3(src2d, p=128):
    return src2d.rearrange("(rc p) c -> p rc c", p=p)


def _d3(dstbuf, ncols):
    return dstbuf[:].rearrange("p (rc c) -> p rc c", c=ncols)


def build(taps=()):
    taps = set(taps)
    nc = bass.Bass("TRN2", target_bir_lowering=False, debug=False)

    x_in = nc.dram_tensor("x", [TPC, C], F32, kind="ExternalInput").ap()
    out_hbm = nc.dram_tensor("out", [TPC, C], F32, kind="ExternalOutput").ap()
    caw = nc.dram_tensor("caw", [C, 3 * C], F32, kind="ExternalInput").ap()
    cab = nc.dram_tensor("cab", [3 * C], F32, kind="ExternalInput").ap()
    qw = nc.dram_tensor("qw", [C, C], F32, kind="ExternalInput").ap()
    qb = nc.dram_tensor("qb", [C], F32, kind="ExternalInput").ap()
    kw = nc.dram_tensor("kw", [C, C], F32, kind="ExternalInput").ap()
    kb = nc.dram_tensor("kb", [C], F32, kind="ExternalInput").ap()
    vw = nc.dram_tensor("vw", [C, C], F32, kind="ExternalInput").ap()
    vb = nc.dram_tensor("vb", [C], F32, kind="ExternalInput").ap()
    ow = nc.dram_tensor("ow", [C, C], F32, kind="ExternalInput").ap()
    ob = nc.dram_tensor("ob", [C], F32, kind="ExternalInput").ap()
    gw_in = nc.dram_tensor("gw", [NE, C], F32, kind="ExternalInput").ap()
    wg_in = nc.dram_tensor("wg", [NE, C, MFF], F32, kind="ExternalInput").ap()
    wu_in = nc.dram_tensor("wu", [NE, C, MFF], F32, kind="ExternalInput").ap()
    wd_in = nc.dram_tensor("wd", [NE, MFF, C], F32, kind="ExternalInput").ap()
    wsg_in = nc.dram_tensor("wsg", [C, MFF], F32, kind="ExternalInput").ap()
    wsu_in = nc.dram_tensor("wsu", [C, MFF], F32, kind="ExternalInput").ap()
    wsd_in = nc.dram_tensor("wsd", [MFF, C], F32, kind="ExternalInput").ap()
    id32_in = nc.dram_tensor("id32", [128, 128], F32, kind="ExternalInput").ap()
    id16_in = nc.dram_tensor("id16", [128, 128], BF16, kind="ExternalInput").ap()
    ones_in = nc.dram_tensor("ones", [128, 128], BF16, kind="ExternalInput").ap()
    masks_in = nc.dram_tensor("masks", [128, 5 * 128], BF16, kind="ExternalInput").ap()
    sel_in = nc.dram_tensor("sel", [16, NE * 128], BF16, kind="ExternalInput").ap()

    wq_h = nc.dram_tensor("wq_h", [128, NKC * C], BF16, kind="Internal").ap()
    wk_h = nc.dram_tensor("wk_h", [128, NKC * C], BF16, kind="Internal").ap()
    wv_h = nc.dram_tensor("wv_h", [128, NKC * C], BF16, kind="Internal").ap()
    ow_h = nc.dram_tensor("ow_h", [C, C], BF16, kind="Internal").ap()
    wg_h = nc.dram_tensor("wg_h", [NE * 128, NKC * MFF], BF16, kind="Internal").ap()
    wu_h = nc.dram_tensor("wu_h", [NE * 128, NKC * MFF], BF16, kind="Internal").ap()
    wd_h = nc.dram_tensor("wd_h", [NE * 128, 4 * C], BF16, kind="Internal").ap()
    wsg_h = nc.dram_tensor("wsg_h", [128, NKC * MFF], BF16, kind="Internal").ap()
    wsu_h = nc.dram_tensor("wsu_h", [128, NKC * MFF], BF16, kind="Internal").ap()
    wsd_h = nc.dram_tensor("wsd_h", [128, 4 * C], BF16, kind="Internal").ap()
    x2r_h = [nc.dram_tensor(f"x2r_h{i}", [CHUNK + 1, C], BF16, kind="Internal").ap()
             for i in range(NCHUNK)]
    disp_h = [nc.dram_tensor(f"disp_h{i}", [NE * ECAP + 128, 2], F32, kind="Internal").ap()
              for i in range(NCHUNK)]
    acc_h = [nc.dram_tensor(f"acc_h{i}", [CHUNK + 1, C], F32, kind="Internal").ap()
             for i in range(NCHUNK)]
    eo_h = [nc.dram_tensor(f"eo_h{i}", [NE * ECAP + 128, C], BF16, kind="Internal").ap()
            for i in range(NCHUNK)]
    ltx_in = nc.dram_tensor("ltx", [128, 128], BF16, kind="ExternalInput").ap()
    iota_in = nc.dram_tensor("iota", [128, NTC], F32, kind="ExternalInput").ap()
    dinit_in = nc.dram_tensor("dinit", [NE * ECAP + 128, 2], F32, kind="ExternalInput").ap()
    zrow_in = nc.dram_tensor("zrow", [1, C], BF16, kind="ExternalInput").ap()
    zrows_in = nc.dram_tensor("zrows", [128, C], BF16, kind="ExternalInput").ap()
    trashc_in = nc.dram_tensor("trashc", [128, 2 * NTC], F32, kind="ExternalInput").ap()

    tap_t = {}
    for name, shape, dt in [
        ("t_xT", [128, NKC * CHUNK], BF16), ("t_q", [128, CHUNK], BF16),
        ("t_k", [128, CHUNK], BF16), ("t_vpl", [128, NTC * C], BF16),
        ("t_vsr", [128, NTC * C], BF16), ("t_y", [128, CHUNK], F32),
        ("t_s", [128, CHUNK], F32), ("t_Y", [128, NKC * CHUNK], BF16),
        ("t_r1", [128, NKC * CHUNK], BF16), ("t_x2", [128, NKC * CHUNK], BF16),
        ("t_route", [NE, CHUNK], BF16), ("t_acc", [128, NKC * CHUNK], F32),
        ("t_weff", [128, NKC * C], BF16),
    ]:
        if name in taps:
            tap_t[name] = nc.dram_tensor(name, shape, dt, kind="ExternalOutput").ap()

    stk = ExitStack()
    sb = lambda name, shape, dt: stk.enter_context(nc.sbuf_tensor(name, shape, dt))

    id32 = sb("id32c", [128, 128], F32)
    id16 = sb("id16c", [128, 128], BF16)
    ones = sb("onesc", [128, 128], BF16)
    masks = sb("masksc", [128, 5 * 128], BF16)
    bqe = sb("bqe", [128, NKC], F32)
    bke = sb("bke", [128, NKC], F32)
    bve = sb("bve", [128, NKC], F32)
    obs = sb("obs", [128, NKC], F32)
    gwT = sb("gwT", [128, 3 * NKC * NE], BF16)
    sel = sb("selc", [16, NE * 128], BF16)
    ltx = sb("ltxc", [128, 128], BF16)
    iotac = sb("iotac", [128, NTC], F32)
    trashc = sb("trashc_s", [128, 2 * NTC], F32)
    Dki2 = [[sb(f"Dki_{i}_{k}", [128, NTC], mybir.dt.int32) for k in range(2)]
            for i in range(NCHUNK)]

    c_g = Ctr(nc, "c_g", 16)
    c_ld1 = Ctr(nc, "c_ld1", 16)
    c_g2 = Ctr(nc, "c_g2", 16)
    c_x = Ctr(nc, "c_x", 16)
    c_wf = Ctr(nc, "c_wf", 16)
    c_wv = Ctr(nc, "c_wv", 16)
    c_wq = [Ctr(nc, "c_wq0", 16), Ctr(nc, "c_wq1", 16)]
    c_ow = Ctr(nc, "c_ow", 16)
    c_e = [Ctr(nc, "c_e0", 16), Ctr(nc, "c_e1", 16)]
    c_o = [Ctr(nc, "c_o0", 16), Ctr(nc, "c_o1", 16)]
    c_tap = Ctr(nc, "c_tap", 16)
    c_x2r = Ctr(nc, "c_x2r", 16)
    c_ds = Ctr(nc, "c_ds", 16)
    c_gx = [Ctr(nc, f"c_gx{i}", 16) for i in range(4)]
    c_tr = [Ctr(nc, "c_tr0", 16), Ctr(nc, "c_tr1", 16)]
    c_id = [Ctr(nc, f"c_id{i}", 16) for i in range(4)]
    c_ws = Ctr(nc, "c_ws", 16)
    c_eo = [Ctr(nc, "c_eo0", 16), Ctr(nc, "c_eo1", 16)]
    c_eg = [Ctr(nc, "c_eg0", 16), Ctr(nc, "c_eg1", 16)]
    c_idc = Ctr(nc, "c_idc", 1)
    c_shw = [Ctr(nc, "c_shw0", 16), Ctr(nc, "c_shw1", 16)]
    c_ac = [Ctr(nc, "c_ac0", 16), Ctr(nc, "c_ac1", 16)]
    c_p = Ctr(nc, "c_p", 1)
    c_a = Ctr(nc, "c_a", 1)
    c_v = Ctr(nc, "c_v", 1)
    c_q = Ctr(nc, "c_q", 1)

    blk_cm = nc.Block()
    block = blk_cm.__enter__()

    def tw(eng):
        if c_tap.n:
            eng.wait_ge(c_tap.sem, c_tap.n)

    x0s = ExitStack()
    x_rows0 = x0s.enter_context(nc.sbuf_tensor("x_rows0", [128, NTC * C], BF16, side="right"))
    setup = ExitStack()
    sb2 = lambda name, shape, dt: setup.enter_context(nc.sbuf_tensor(name, shape, dt))
    ps2 = lambda name, shape, dt: setup.enter_context(nc.psum_tensor(name, shape, dt))
    caw_s = sb2("caw_s", [128, NKC * 3 * C], BF16)
    qw_s = sb2("qw_s", [128, NKC * C], BF16)
    kw_s = sb2("kw_s", [128, NKC * C], BF16)
    vw_s = sb2("vw_s", [128, NKC * C], BF16)
    cab_s = sb2("cab_s", [128, 24], BF16)
    qb_s = sb2("qb_s", [128, NKC], F32)
    kb_s = sb2("kb_s", [128, NKC], F32)
    vb_s = sb2("vb_s", [128, NKC], F32)
    cawT = sb2("cawT", [128, 24 * C], BF16)
    weff_st = sb2("weff_st", [128, NKC * C], BF16)
    pt = [ps2(f"pt{i}", [128, 1024], BF16) for i in range(2)]
    pw = [ps2(f"pw{i}", [128, 512], F32) for i in range(4)]
    pb8 = ps2("pb8", [128, NKC], F32)

    def g_setup(g):
        c_ld1.inc(g.dma_start(out=id16[:], in_=id16_in))
        c_ld1.inc(g.dma_start(out=_d3(caw_s, 3 * C), in_=_t3(caw)))
        c_g.inc(g.dma_start(out=_d3(qw_s, C), in_=_t3(qw)))
        c_g.inc(g.dma_start(out=_d3(kw_s, C), in_=_t3(kw)))
        c_g.inc(g.dma_start(out=_d3(vw_s, C), in_=_t3(vw)))
        c_g.inc(g.dma_start(out=cab_s[:], in_=cab.rearrange("(j p) -> p j", p=128), allow_slow_non_contiguous=True))
        c_g.inc(g.dma_start(out=qb_s[:], in_=qb.rearrange("(j p) -> p j", p=128), allow_slow_non_contiguous=True))
        c_g.inc(g.dma_start(out=kb_s[:], in_=kb.rearrange("(j p) -> p j", p=128), allow_slow_non_contiguous=True))
        c_g.inc(g.dma_start(out=vb_s[:], in_=vb.rearrange("(j p) -> p j", p=128), allow_slow_non_contiguous=True))
        c_g.inc(g.dma_start(out=obs[:], in_=ob.rearrange("(j p) -> p j", p=128), allow_slow_non_contiguous=True))
        c_g.inc(g.dma_start(out=id32[:], in_=id32_in))
        c_g.inc(g.dma_start(out=ones[:], in_=ones_in))
        c_g.inc(g.dma_start(out=sel[:], in_=sel_in))
        c_g.inc(g.dma_start(out=ltx[:], in_=ltx_in))
        c_g.inc(g.dma_start(out=iotac[:], in_=iota_in))
        c_g.inc(g.dma_start(out=masks[:], in_=masks_in))
        c_g.inc(g.dma_start(out=trashc[:], in_=trashc_in))
        c_x.inc(g.dma_start(
            out=_d3(x_rows0, C),
            in_=_t3(x_in[0:CHUNK, :])))
        c_g2.inc(g.dma_start(out=ow_h, in_=ow))
        wgf = wg_in.rearrange("e c m -> (e c) m")
        wuf = wu_in.rearrange("e c m -> (e c) m")
        wdf = wd_in.rearrange("e m c -> (e m) c")
        for e_ in range(NE):
            c_g2.inc(g.dma_start(
                out=wg_h[128 * e_:128 * (e_ + 1), :].rearrange("p (rc c) -> p rc c", c=MFF),
                in_=wgf[e_ * C:(e_ + 1) * C, :].rearrange("(rc p) c -> p rc c", p=128)))
            c_g2.inc(g.dma_start(
                out=wu_h[128 * e_:128 * (e_ + 1), :].rearrange("p (rc c) -> p rc c", c=MFF),
                in_=wuf[e_ * C:(e_ + 1) * C, :].rearrange("(rc p) c -> p rc c", p=128)))
            c_g2.inc(g.dma_start(
                out=wd_h[128 * e_:128 * (e_ + 1), :].rearrange("p (mc c) -> p mc c", c=C),
                in_=wdf[e_ * MFF:(e_ + 1) * MFF, :].rearrange("(mc p) c -> p mc c", p=128)))
        c_g2.inc(g.dma_start(
            out=wsg_h.rearrange("p (rc c) -> p rc c", c=MFF),
            in_=wsg_in.rearrange("(rc p) c -> p rc c", p=128)))
        c_g2.inc(g.dma_start(
            out=wsu_h.rearrange("p (rc c) -> p rc c", c=MFF),
            in_=wsu_in.rearrange("(rc p) c -> p rc c", p=128)))
        c_g2.inc(g.dma_start(
            out=wsd_h.rearrange("p (mc c) -> p mc c", c=C),
            in_=wsd_in.rearrange("(mc p) c -> p mc c", p=128)))
        for i in range(NCHUNK):
            c_g2.inc(g.dma_start(out=eo_h[i][NE * ECAP:NE * ECAP + 128, :], in_=zrows_in))
            c_g2.inc(g.dma_start(out=disp_h[i], in_=dinit_in))
            c_g2.inc(g.dma_start(out=x2r_h[i][CHUNK:CHUNK + 1, :], in_=zrow_in))
    block.gpsimd(g_setup)
    n_ld0, n_cast = 15 * 16, 58 * 16

    v_t0 = c_v.n
    p_t0 = c_p.n

    def pe_cawT(t):
        t.wait_ge(c_ld1.sem, 32)
        for mc in range(24):
            if mc >= 2:
                t.wait_ge(c_v.sem, v_t0 + mc - 1)
            for cc in range(NKC):
                mm = t.transpose(pt[mc % 2][:, 128 * cc:128 * (cc + 1)],
                                 caw_s[:, cc * 3 * C + 128 * mc: cc * 3 * C + 128 * (mc + 1)],
                                 id16[:])
            c_p.inc(mm)
    block.tensor(pe_cawT)

    def dv_cawT(v):
        for mc in range(24):
            v.wait_ge(c_p.sem, p_t0 + mc + 1)
            c_v.inc(v.tensor_copy(cawT[:, mc * C:(mc + 1) * C], pt[mc % 2][:]))
    block.vector(dv_cawT)

    wsrc = {0: qw_s, 1: kw_s, 2: vw_s}
    whbm = {0: wq_h, 1: wk_h, 2: wv_h}
    bsrc = {0: qb_s, 1: kb_s, 2: vb_s}
    bdst = {0: bqe, 1: bke, 2: bve}
    a_w0 = c_a.n
    v_w0 = c_v.n
    s_w0 = c_wf.n
    p_w0 = c_p.n

    def pe_weff(t):
        t.wait_ge(c_g.sem, n_ld0)
        t.wait_ge(c_v.sem, v_t0 + 24)
        nslot = 0
        for wi in range(3):
            if wi > 0:
                t.wait_ge(c_v.sem, v_w0 + (2 if wi == 1 else 3))
                t.wait_ge(c_wf.sem, s_w0 + 16 * wi)
            for cc in range(NKC):
                for fh in range(2):
                    if nslot >= 4:
                        t.wait_ge(c_a.sem, a_w0 + nslot - 3)
                    for mc in range(NKC):
                        mcg = wi * NKC + mc
                        mm = t.matmul(pw[nslot % 4][:],
                                      cawT[:, mcg * C + 128 * cc: mcg * C + 128 * (cc + 1)],
                                      wsrc[wi][:, mc * C + 512 * fh: mc * C + 512 * (fh + 1)],
                                      start=(mc == 0), stop=(mc == NKC - 1))
                    c_p.inc(mm)
                    nslot += 1
            for fc in range(NKC):
                for mc in range(NKC):
                    mcg = wi * NKC + mc
                    mm = t.matmul(pb8[:, fc:fc + 1],
                                  wsrc[wi][:, mc * C + 128 * fc: mc * C + 128 * (fc + 1)],
                                  cab_s[:, mcg:mcg + 1],
                                  start=(mc == 0), stop=(mc == NKC - 1))
            c_p.inc(mm)
    block.tensor(pe_weff)

    def act_weff(a):
        nslot = 0
        for wi in range(3):
            for cc in range(NKC):
                for fh in range(2):
                    nslot += 1
                    a.wait_ge(c_p.sem, p_w0 + nslot + wi)
                    c_a.inc(a.activation(
                        out=weff_st[:, cc * C + 512 * fh: cc * C + 512 * (fh + 1)],
                        in_=pw[(nslot - 1) % 4][:], func=AF.Copy))
    block.scalar(act_weff)

    def dv_weff(v):
        for wi in range(3):
            v.wait_ge(c_p.sem, p_w0 + (wi + 1) * 17)
            if wi == 0:
                c_v.inc(v.tensor_scalar_mul(qb_s[:], qb_s[:], 0.125))
                v.drain()
                c_v.inc(v.scalar_tensor_tensor(
                    out=bqe[:], in0=pb8[:], scalar=0.125, in1=qb_s[:],
                    op0=OP.mult, op1=OP.add))
            else:
                c_v.inc(v.tensor_tensor(out=bdst[wi][:], in0=pb8[:], in1=bsrc[wi][:],
                                        op=OP.add))
    block.vector(dv_weff)

    def sy_weff(s):
        for wi in range(3):
            s.wait_ge(c_a.sem, a_w0 + (wi + 1) * 16)
            c_wf.inc(s.dma_start(out=whbm[wi], in_=weff_st[:]))
    block.sync(sy_weff)
    n_weff = c_wf.n

    if "t_weff" in tap_t:
        def sy_tw(s):
            s.wait_ge(c_a.sem, c_a.n)
            c_tap.inc(s.dma_start(out=tap_t["t_weff"], in_=weff_st[:]))
        block.sync(sy_tw)

    setup.close()

    p_weff_end = c_p.n
    a_weff_end = c_a.n
    v_weff_end = c_v.n
    gws = ExitStack()
    gw_s = gws.enter_context(nc.sbuf_tensor("gw_s", [16, C], F32))
    gwhi = gws.enter_context(nc.sbuf_tensor("gwhi", [16, C], BF16))
    gwmd = gws.enter_context(nc.sbuf_tensor("gwmd", [16, C], BF16))
    gwlo = gws.enter_context(nc.sbuf_tensor("gwlo", [16, C], BF16))
    ptg = gws.enter_context(nc.psum_tensor("ptg", [128, 384], BF16))
    c_gw = Ctr(nc, "c_gw", 16)

    def sy_gw(s):
        s.wait_ge(c_p.sem, p_weff_end)
        s.wait_ge(c_a.sem, a_weff_end)
        s.wait_ge(c_v.sem, v_weff_end)
        c_gw.inc(s.dma_start(out=gw_s[:], in_=gw_in))
    block.sync(sy_gw)

    def dv_gw(v):
        v.wait_ge(c_gw.sem, 16)
        v.wait_ge(c_a.sem, c_a.n)
        v.drain()
        v.tensor_copy(gwhi[:], gw_s[:])
        v.drain()
        v.tensor_tensor(out=gw_s[:], in0=gw_s[:], in1=gwhi[:], op=OP.subtract)
        v.drain()
        v.tensor_copy(gwmd[:], gw_s[:])
        v.drain()
        v.tensor_tensor(out=gw_s[:], in0=gw_s[:], in1=gwmd[:], op=OP.subtract)
        v.drain()
        c_v.inc(v.tensor_copy(gwlo[:], gw_s[:]))
    block.vector(dv_gw)
    v_gw = c_v.n

    def pe_gw(t):
        t.wait_ge(c_v.sem, v_gw)
        for hl, gsrc in enumerate((gwhi, gwmd, gwlo)):
            for cc in range(NKC):
                mm = t.transpose(ptg[:, 128 * hl + 16 * cc:128 * hl + 16 * (cc + 1)],
                                 gsrc[0:16, 128 * cc:128 * (cc + 1)], id16[0:16, 0:16])
        c_p.inc(mm)
    block.tensor(pe_gw)
    p_gw = c_p.n

    def dv_gwT(v):
        v.wait_ge(c_p.sem, p_gw)
        v.drain()
        last = None
        for hl in range(3):
            last = v.tensor_copy(
                gwT[:].rearrange("p (cc th) -> p cc th", th=48)[:, :, 16 * hl:16 * (hl + 1)],
                ptg[:, 128 * hl:128 * (hl + 1)].rearrange("p (cc s) -> p cc s", s=16))
        c_v.inc(last)
    block.vector(dv_gwT)
    n_gwT = c_v.n
    gws.close()

    prev_chunk_done = dict(v=c_v.n, o0=c_o[0].n, o1=c_o[1].n, p=c_p.n, a=c_a.n,
                           shw0=0, shw1=0, ds=0, x2r=0)
    ch_marks = []

    for ch in range(NCHUNK):
        chs = ExitStack()
        sbc = lambda name, shape, dt: chs.enter_context(
            nc.sbuf_tensor(f"{name}{ch}", shape, dt, side="right"))
        psc = lambda st_, name, shape, dt: st_.enter_context(nc.psum_tensor(f"{name}{ch}", shape, dt))

        xys = ExitStack()
        xT = xys.enter_context(nc.sbuf_tensor(f"xT{ch}", [128, NKC * CHUNK], BF16))
        Y = xys.enter_context(nc.sbuf_tensor(f"Y{ch}", [128, NKC * CHUNK], BF16))

        p1s = ExitStack()
        if ch == 0:
            x_rows = x_rows0
        else:
            x_rows = p1s.enter_context(
                nc.sbuf_tensor(f"x_rows{ch}", [128, NTC * C], BF16, side="right"))
        px = [psc(p1s, f"px{i}", [128, 2048], BF16) for i in range(2)]

        pcd = prev_chunk_done

        if ch > 0:
            def g_p1(g):
                g.wait_ge(c_v.sem, pcd["v"])
                g.wait_ge(c_shw[0].sem, pcd["shw0"])
                g.wait_ge(c_shw[1].sem, pcd["shw1"])
                g.wait_ge(c_ds.sem, pcd["ds"])
                g.wait_ge(c_x2r.sem, pcd["x2r"])
                c_x.inc(g.dma_start(
                    out=_d3(x_rows, C),
                    in_=_t3(x_in[ch * CHUNK:(ch + 1) * CHUNK, :])))
            block.gpsimd(g_p1)
        n_x = c_x.n

        p_b1 = c_p.n
        v_b1 = c_v.n

        def pe_p1(t):
            tw(t)
            t.wait_ge(c_x.sem, n_x)
            t.wait_ge(c_v.sem, pcd["v"])
            t.wait_ge(c_a.sem, pcd["a"])
            for cc in range(NKC):
                if cc >= 2:
                    t.wait_ge(c_v.sem, v_b1 + cc - 1)
                for tc in range(NTC):
                    mm = t.transpose(px[cc % 2][:, 128 * tc:128 * (tc + 1)],
                                     x_rows[:, tc * C + 128 * cc: tc * C + 128 * (cc + 1)],
                                     id16[:])
                c_p.inc(mm)
        block.tensor(pe_p1)

        def dv_p1(v):
            for cc in range(NKC):
                v.wait_ge(c_p.sem, p_b1 + cc + 1)
                c_v.inc(v.tensor_copy(xT[:, cc * CHUNK:(cc + 1) * CHUNK], px[cc % 2][:]))
        block.vector(dv_p1)
        n_xT = c_v.n
        p1s.close()
        if ch == 0:
            x0s.close()

        if "t_xT" in tap_t and ch == 0:
            def sy_t1(s):
                s.wait_ge(c_v.sem, n_xT)
                c_tap.inc(s.dma_start(out=tap_t["t_xT"], in_=xT[:]))
            block.sync(sy_t1)

        att = ExitStack()
        sba = lambda name, shape, dt: att.enter_context(nc.sbuf_tensor(f"{name}{ch}", shape, dt))
        vpl = sba("vpl", [128, NTC * C], BF16)
        vsr = sba("vsr", [128, NTC * C], BF16)
        y_pair = sba("y_pair", [128, CHUNK], F32)
        s_tot = sba("s_tot", [128, CHUNK], F32)
        p_bufs2 = [[sba(f"p_buf{up}_{hh}", [128, 1536], BF16) for hh in range(2)]
                   for up in range(2)]

        p2s = ExitStack()
        wvb = p2s.enter_context(nc.sbuf_tensor(f"wvb{ch}", [128, NKC * C], BF16))
        pv = [psc(p2s, f"pv{i}", [128, 512], F32) for i in range(4)]

        def sy_p2(s):
            s.wait_ge(c_wf.sem, n_weff)
            s.wait_ge(c_v.sem, n_xT)
            s.wait_ge(c_v.sem, pcd["v"])
            s.wait_ge(c_shw[0].sem, pcd["shw0"])
            s.wait_ge(c_shw[1].sem, pcd["shw1"])
            c_wv.inc(s.dma_start(out=wvb[:], in_=wv_h))
        block.sync(sy_p2)
        n_wv = c_wv.n

        p_b2 = c_p.n
        v_b2 = c_v.n

        def pe_p2(t):
            tw(t)
            t.wait_ge(c_wv.sem, n_wv)
            t.wait_ge(c_v.sem, n_xT)
            nslot = 0
            for form in range(2):
                for tc in range(NTC):
                    for fh in range(2):
                        if nslot >= 4:
                            t.wait_ge(c_v.sem, v_b2 + nslot - 3)
                        for cc in range(NKC):
                            if form == 0:
                                st_ap = xT[:, cc * CHUNK + 128 * tc: cc * CHUNK + 128 * (tc + 1)]
                            else:
                                cls, sblk = tc // 4, tc % 4
                                b0 = cc * CHUNK + cls + 512 * sblk
                                st_ap = xT[:, b0: b0 + 509: 4]
                            mm = t.matmul(pv[nslot % 4][:], st_ap,
                                          wvb[:, cc * C + 512 * fh: cc * C + 512 * (fh + 1)],
                                          start=(cc == 0), stop=(cc == NKC - 1))
                        c_p.inc(mm)
                        nslot += 1
        block.tensor(pe_p2)

        def dv_p2(v):
            nslot = 0
            for form in range(2):
                dst = vpl if form == 0 else vsr
                for tc in range(NTC):
                    for fh in range(2):
                        v.wait_ge(c_p.sem, p_b2 + nslot + 1)
                        c_v.inc(v.tensor_copy(
                            dst[:, tc * C + 512 * fh: tc * C + 512 * (fh + 1)],
                            pv[nslot % 4][:]))
                        nslot += 1
        block.vector(dv_p2)
        n_v = c_v.n
        p2s.close()

        if ch == 0:
            for nm, buf in (("t_vpl", vpl), ("t_vsr", vsr)):
                if nm in tap_t:
                    def sy_t2(s, nm=nm, buf=buf):
                        s.wait_ge(c_v.sem, n_v)
                        c_tap.inc(s.dma_start(out=tap_t[nm], in_=buf[:]))
                    block.sync(sy_t2)

        q_bufs = [sba(f"q_buf{i}", [128, CHUNK], BF16) for i in range(2)]
        k_bufs = [sba(f"k_buf{i}", [128, CHUNK], BF16) for i in range(2)]
        wqkb2 = [sba(f"wqkb{i}", [128, 2 * NKC * 128], BF16) for i in range(2)]
        p3ps = ExitStack()
        pqk = [psc(p3ps, f"pqk{i}", [128, 512], F32) for i in range(2)]
        SAb = [psc(p3ps, f"SA{hh}", [128, 1024], F32) for hh in range(2)]
        ob = psc(p3ps, "ob", [128, 512], F32)
        sb_ = psc(p3ps, "sb", [128, 512], F32)

        qk_pe = {}
        qk_dv = {}
        wq_marks = {}

        def sy_qk(s, pair):
            if pair == 0:
                s.wait_ge(c_wf.sem, n_weff)
                s.wait_ge(c_p.sem, p_b2 + 64)
            if pair >= 2:
                s.wait_ge(c_p.sem, qk_pe[(pair - 2, 7)])
            cw = c_wq[pair % 2]
            cw.inc(s.dma_start(
                out=wqkb2[pair % 2][:, 0:NKC * 128].rearrange("p (cc f) -> p cc f", f=128),
                in_=wq_h.rearrange("p (cc f) -> p cc f", f=C)[:, :, 128 * pair:128 * (pair + 1)]))
            cw.inc(s.dma_start(
                out=wqkb2[pair % 2][:, NKC * 128:].rearrange("p (cc f) -> p cc f", f=128),
                in_=wk_h.rearrange("p (cc f) -> p cc f", f=C)[:, :, 128 * pair:128 * (pair + 1)]))
            wq_marks[pair] = (pair % 2, cw.n)

        def pe_qk_slot(t, pair, slot):
            t.wait_ge(c_wq[wq_marks[pair][0]].sem, wq_marks[pair][1])
            if (pair, slot - 2) in qk_dv:
                t.wait_ge(c_a.sem, qk_dv[(pair, slot - 2)])
            elif slot < 2 and (pair - 1, 6 + slot) in qk_dv:
                t.wait_ge(c_a.sem, qk_dv[(pair - 1, 6 + slot)])
            which, tt = slot // 4, slot % 4
            for cc in range(NKC):
                mm = t.matmul(
                    pqk[slot % 2][:],
                    wqkb2[pair % 2][:, which * NKC * 128 + 128 * cc:
                                    which * NKC * 128 + 128 * (cc + 1)],
                    xT[:, cc * CHUNK + 512 * tt: cc * CHUNK + 512 * (tt + 1)],
                    start=(cc == 0), stop=(cc == NKC - 1))
            n_ = c_p.inc(mm)
            if (pair, slot) in qk_pe:
                assert n_ == qk_pe[(pair, slot)], (pair, slot, n_, qk_pe[(pair, slot)])
            else:
                qk_pe[(pair, slot)] = n_

        def act_qk_slot(a, pair, slot):
            a.wait_ge(c_p.sem, qk_pe[(pair, slot)])
            which, tt = slot // 4, slot % 4
            if which == 0:
                op = a.activation(
                    out=q_bufs[pair % 2][:, 512 * tt:512 * (tt + 1)],
                    in_=pqk[slot % 2][:], func=AF.Identity, scale=0.125,
                    bias=bqe[:, pair:pair + 1])
            else:
                op = a.activation(
                    out=k_bufs[pair % 2][:, 512 * tt:512 * (tt + 1)],
                    in_=pqk[slot % 2][:], func=AF.Identity,
                    bias=bke[:, pair:pair + 1])
            n_ = c_a.inc(op)
            if (pair, slot) in qk_dv:
                assert n_ == qk_dv[(pair, slot)], (pair, slot, n_, qk_dv[(pair, slot)])
            else:
                qk_dv[(pair, slot)] = n_

        def sy_qk0(s):
            sy_qk(s, 0)
        block.sync(sy_qk0)

        for s_ in range(8):
            qk_dv[(0, s_)] = c_a.n + s_ + 1

        def pe_qk0(t):
            t.wait_ge(c_v.sem, n_v)
            for slot in range(8):
                pe_qk_slot(t, 0, slot)
        block.tensor(pe_qk0)

        def act_qk0(a):
            for slot in range(8):
                act_qk_slot(a, 0, slot)
        block.scalar(act_qk0)

        a_marks = {}
        v_marks = {}
        pe_marks = {}
        n_Y_v = None

        nA = c_a.n
        nV = c_v.n
        nP = c_p.n
        for pair in range(NPAIR):
            has_next = pair + 1 < NPAIR
            for u in range(7):
                for hh in range(2):
                    for h in range(2):
                        nA += 1
                        a_marks[(pair, u, hh, h)] = nA
                if has_next and u < 7:
                    nA += 1
                    qk_dv[(pair + 1, u)] = nA
            if has_next:
                nA += 1
                qk_dv[(pair + 1, 7)] = nA
            for u in range(7):
                nV += 1
                v_marks[(pair, "m", u)] = nV
                nV += 1
                v_marks[(pair, "y", u)] = nV
            nV += 2
            v_marks[(pair, "Y")] = nV
            for u in range(7):
                if u >= 1:
                    nP += 1
                    pe_marks[(pair, "os", u - 1)] = nP
                for hh in range(2):
                    for h in range(2):
                        nP += 1
                        pe_marks[(pair, u, hh, h)] = nP
                if has_next and u < 7:
                    nP += 1
                    qk_pe[(pair + 1, u)] = nP
            nP += 1
            pe_marks[(pair, "os", 6)] = nP
            if has_next:
                nP += 1
                qk_pe[(pair + 1, 7)] = nP

        for pair in range(NPAIR):
            if pair + 1 < NPAIR:
                def sy_qkn(s, pair=pair):
                    sy_qk(s, pair + 1)
                block.sync(sy_qkn)

            scheds = []
            for (br, seg) in unit_list():
                scheds.append((br, seg,
                               [unit_rects(br, seg, 2 * pair + hh) for hh in range(2)]))

            def pe_pair(t, pair=pair, scheds=scheds):
                if pair == 0:
                    t.wait_ge(c_v.sem, n_v)
                t.wait_ge(c_a.sem, qk_dv[(pair, 7)])
                for u, (br, seg, rr) in enumerate(scheds):
                    if u >= 1:
                        emit_os(t, pair, u - 1, scheds[u - 1])
                    for hh in range(2):
                        rb = 64 * hh
                        for h in range(2):
                            if h == 0:
                                key = (pair, u - 1, hh, 1) if u > 0 else (pair - 1, 6, hh, 1)
                                if key in a_marks:
                                    t.wait_ge(c_a.sem, a_marks[key])
                            else:
                                t.wait_ge(c_a.sem, a_marks[(pair, u, hh, 0)])
                            rects = [r for r in rr[hh] if r["h"] == h]
                            base_slot = 0 if h == 0 else 2
                            for k_, r in enumerate(rects):
                                slot = base_slot + k_
                                kst, kstep = r["k"]
                                qst, qstep = r["q"]
                                sa_col = 256 * (slot - base_slot) if h == 1 else 256 * slot
                                mm = t.matmul(
                                    SAb[hh][:, sa_col:sa_col + 256],
                                    k_bufs[pair % 2][rb:rb + 64, kst: kst + 127 * kstep + 1: kstep],
                                    q_bufs[pair % 2][rb:rb + 64, qst: qst + 255 * qstep + 1: qstep],
                                    start=True, stop=True, tile_position=(rb, 0))
                            assert c_p.inc(mm) == pe_marks[(pair, u, hh, h)]
                    if pair + 1 < NPAIR and u < 7:
                        pe_qk_slot(t, pair + 1, u)
                emit_os(t, pair, 6, scheds[6])
                if pair + 1 < NPAIR:
                    pe_qk_slot(t, pair + 1, 7)

            def emit_os(t, pair, u, sched):
                br, seg, rr = sched
                p_bufs = p_bufs2[(7 * pair + u) % 2]
                t.wait_ge(c_v.sem, v_marks[(pair, "m", u)])
                key = (pair, "y", u - 1) if u > 0 else (pair - 1, "y", 6)
                if key in v_marks:
                    t.wait_ge(c_v.sem, v_marks[key])
                for hh in range(2):
                    n_r = len(rr[hh])
                    tp = (0, 64 * hh)
                    for slot, r in enumerate(rr[hh]):
                        vt, srt = r["v"]
                        vb_ap = (vsr if srt else vpl)[:, vt * C + 128 * pair + 64 * hh:
                                                      vt * C + 128 * pair + 64 * (hh + 1)]
                        st = r["start"]
                        mv = p_bufs[hh][:, 256 * slot + st:256 * (slot + 1)]
                        t.matmul(ob[64 * hh:64 * hh + 64, 256 * r["h"] + st:256 * (r["h"] + 1)],
                                 vb_ap, mv, start=(slot == 0), stop=(slot == n_r - 1),
                                 tile_position=tp, skip_group_check=True)
                    for slot, r in enumerate(rr[hh]):
                        st = r["start"]
                        mv = p_bufs[hh][:, 256 * slot + st:256 * (slot + 1)]
                        mm = t.matmul(sb_[64 * hh:64 * hh + 64, 256 * r["h"] + st:256 * (r["h"] + 1)],
                                      ones[:, 0:64], mv,
                                      start=(slot == 0), stop=(slot == n_r - 1),
                                      tile_position=tp, skip_group_check=True)
                assert c_p.inc(mm) == pe_marks[(pair, "os", u)]

            def act_pair(a, pair=pair):
                for u in range(7):
                    g_ = 7 * pair + u
                    p_bufs = p_bufs2[g_ % 2]
                    pg2 = g_ - 2
                    okey = None
                    if pg2 >= 0:
                        okey = (pg2 // 7, "os", pg2 % 7)
                    for hh in range(2):
                        a.wait_ge(c_p.sem, pe_marks[(pair, u, hh, 0)])
                        if okey is not None and okey in pe_marks:
                            a.wait_ge(c_p.sem, pe_marks[okey])
                        c_a.inc(a.activation(out=p_bufs[hh][:, 0:512],
                                             in_=SAb[hh][:, 0:512], func=AF.Exp))
                        assert c_a.n == a_marks[(pair, u, hh, 0)]
                        a.wait_ge(c_p.sem, pe_marks[(pair, u, hh, 1)])
                        c_a.inc(a.activation(out=p_bufs[hh][:, 512:1536],
                                             in_=SAb[hh][:, 0:1024], func=AF.Exp))
                        assert c_a.n == a_marks[(pair, u, hh, 1)]
                    if pair + 1 < NPAIR and u < 7:
                        act_qk_slot(a, pair + 1, u)
                if pair + 1 < NPAIR:
                    act_qk_slot(a, pair + 1, 7)

            def dv_pair(v, pair=pair, scheds=scheds):
                for u, (br, seg, rr) in enumerate(scheds):
                    last = None
                    p_bufs = p_bufs2[(7 * pair + u) % 2]
                    for hh in range(2):
                        v.wait_ge(c_a.sem, a_marks[(pair, u, hh, 1)])
                        for slot, r in enumerate(rr[hh]):
                            for (mid, loc) in r["masks"]:
                                last = v.tensor_tensor(
                                    out=p_bufs[hh][:, 256 * slot + loc:256 * slot + loc + 128],
                                    in0=p_bufs[hh][:, 256 * slot + loc:256 * slot + loc + 128],
                                    in1=masks[:, 128 * mid:128 * (mid + 1)], op=OP.mult)
                    assert c_v.inc(last) == v_marks[(pair, "m", u)]
                    v.wait_ge(c_p.sem, pe_marks[(pair, "os", u)])
                    v.drain()
                    hd0 = 2 * pair
                    for hh in range(2):
                        yst, ystep = unit_ycols(br, seg, hd0 + hh)
                        rb = 64 * hh
                        ydst = y_pair[rb:rb + 64, yst: yst + 511 * ystep + 1: ystep]
                        sdst = s_tot[rb:rb + 64, yst: yst + 511 * ystep + 1: ystep]
                        if br == "b1":
                            v.tensor_copy(ydst, ob[rb:rb + 64, :])
                            last = v.tensor_copy(sdst, sb_[rb:rb + 64, :])
                        else:
                            v.tensor_tensor(out=ydst, in0=ydst, in1=ob[rb:rb + 64, :], op=OP.add)
                            last = v.tensor_tensor(out=sdst, in0=sdst,
                                                   in1=sb_[rb:rb + 64, :], op=OP.add)
                    assert c_v.inc(last) == v_marks[(pair, "y", u)]
                v.drain()
                c_v.inc(v.reciprocal(s_tot[:], s_tot[:]))
                v.drain()
                v.tensor_tensor(out=y_pair[:], in0=y_pair[:], in1=s_tot[:], op=OP.mult)
                v.drain()
                c_v.inc(v.tensor_scalar(
                    out=Y[:, pair * CHUNK:(pair + 1) * CHUNK], in0=y_pair[:],
                    scalar1=bve[:, pair:pair + 1], scalar2=None, op0=OP.add))
                assert c_v.n == v_marks[(pair, "Y")]

            block.tensor(pe_pair)
            block.scalar(act_pair)
            block.vector(dv_pair)

            if ch == 0 and pair == 0:
                nyv = c_v.n
                for nm, buf in (("t_q", q_bufs[0]), ("t_k", k_bufs[0]),
                                ("t_y", y_pair), ("t_s", s_tot)):
                    if nm in tap_t:
                        def sy_t4(s, nm=nm, buf=buf, nyv=nyv):
                            s.wait_ge(c_v.sem, nyv)
                            c_tap.inc(s.dma_start(out=tap_t[nm], in_=buf[:]))
                        block.sync(sy_t4)

        p3ps.close()
        n_Y_v = c_v.n
        n_att_v = c_v.n
        att.close()

        if "t_Y" in tap_t and ch == 0:
            def sy_t5(s):
                s.wait_ge(c_v.sem, n_Y_v)
                c_tap.inc(s.dma_start(out=tap_t["t_Y"], in_=Y[:]))
            block.sync(sy_t5)

        x2 = sbc("x2", [128, NKC * CHUNK], BF16)
        meanb = sbc("meanb", [128, CHUNK], F32)
        rstd = sbc("rstd", [128, CHUNK], F32)
        sq2 = sbc("sq2", [128, 2 * 512], BF16)
        r1s = ExitStack()
        r1b = r1s.enter_context(nc.sbuf_tensor(f"r1b{ch}", [128, NKC * CHUNK], BF16))

        p4s = ExitStack()
        owb = p4s.enter_context(nc.sbuf_tensor(f"owb{ch}", [128, NKC * C], BF16))
        po4 = [psc(p4s, f"po4{i}", [128, 512], F32) for i in range(4)]

        def sy_p4(s):
            s.wait_ge(c_g2.sem, n_cast)
            s.wait_ge(c_v.sem, n_Y_v)
            c_ow.inc(s.dma_start(out=_d3(owb, C), in_=_t3(ow_h)))
        block.sync(sy_p4)
        n_ow = c_ow.n

        p_b4 = c_p.n
        v_b4 = c_v.n

        def pe_p4(t):
            tw(t)
            t.wait_ge(c_ow.sem, n_ow)
            t.wait_ge(c_v.sem, n_Y_v)
            nslot = 0
            for fc in range(NKC):
                for tt in range(4):
                    if nslot >= 4:
                        t.wait_ge(c_v.sem, v_b4 + nslot - 3)
                    for cc in range(NKC):
                        mm = t.matmul(po4[nslot % 4][:],
                                      owb[:, cc * C + 128 * fc: cc * C + 128 * (fc + 1)],
                                      Y[:, cc * CHUNK + 512 * tt: cc * CHUNK + 512 * (tt + 1)],
                                      start=(cc == 0), stop=(cc == NKC - 1))
                    c_p.inc(mm)
                    nslot += 1
        block.tensor(pe_p4)

        def dv_p4(v):
            nslot = 0
            for fc in range(NKC):
                for tt in range(4):
                    v.wait_ge(c_p.sem, p_b4 + nslot + 1)
                    c_v.inc(v.scalar_tensor_tensor(
                        out=r1b[:, fc * CHUNK + 512 * tt: fc * CHUNK + 512 * (tt + 1)],
                        in0=po4[nslot % 4][:], scalar=obs[:, fc:fc + 1],
                        in1=xT[:, fc * CHUNK + 512 * tt: fc * CHUNK + 512 * (tt + 1)],
                        op0=OP.add, op1=OP.add))
                    nslot += 1
        block.vector(dv_p4)
        p4s.close()
        if "t_r1" in tap_t and ch == 0:
            def sy_t6(s, n=c_v.n):
                s.wait_ge(c_v.sem, n)
                c_tap.inc(s.dma_start(out=tap_t["t_r1"], in_=r1b[:]))
            block.sync(sy_t6)

        _ln_id = [0]

        def ln_emit(srcb, dst):
            lns = ExitStack()
            _ln_id[0] += 1
            pm = [psc(lns, f"pm{_ln_id[0]}_{i}", [128, 512], F32) for i in range(4)]
            v_r = c_v.n
            p_l0 = c_p.n

            def pe_ln(t):
                t.wait_ge(c_v.sem, v_r)
                for tt in range(4):
                    for cc in range(NKC):
                        mm = t.matmul(pm[tt][:], ones[:],
                                      srcb[:, cc * CHUNK + 512 * tt: cc * CHUNK + 512 * (tt + 1)],
                                      start=(cc == 0), stop=(cc == NKC - 1))
                    c_p.inc(mm)
            block.tensor(pe_ln)

            def act_ln(a):
                for tt in range(4):
                    a.wait_ge(c_p.sem, p_l0 + tt + 1)
                    c_a.inc(a.activation(out=meanb[:, 512 * tt:512 * (tt + 1)],
                                         in_=pm[tt][:], func=AF.Copy, scale=1.0 / C))
            block.scalar(act_ln)
            a_mean = c_a.n

            v_c0 = c_v.n

            def dv_lnc(v):
                v.wait_ge(c_a.sem, a_mean)
                v.drain()
                last = None
                for cc in range(NKC):
                    last = v.tensor_tensor(out=srcb[:, cc * CHUNK:(cc + 1) * CHUNK],
                                           in0=srcb[:, cc * CHUNK:(cc + 1) * CHUNK],
                                           in1=meanb[:], op=OP.subtract)
                c_v.inc(last)
            block.vector(dv_lnc)
            v_cent = c_v.n

            p_l1 = c_p.n
            a_l1 = c_a.n

            def act_lnsq(a):
                a.wait_ge(c_v.sem, v_cent)
                k = 0
                for tt in range(4):
                    for cc in range(NKC):
                        if k >= 2:
                            a.wait_ge(c_p.sem, p_l1 + k - 1)
                        sl = srcb[:, cc * CHUNK + 512 * tt: cc * CHUNK + 512 * (tt + 1)]
                        c_a.inc(a.activation(
                            out=sq2[:, 512 * (k % 2):512 * (k % 2 + 1)],
                            in_=sl, func=AF.Square))
                        k += 1
            block.scalar(act_lnsq)

            def pe_ln2(t):
                k = 0
                for tt in range(4):
                    for cc in range(NKC):
                        t.wait_ge(c_a.sem, a_l1 + k + 1)
                        mm = t.matmul(pm[tt][:], ones[:],
                                      sq2[:, 512 * (k % 2):512 * (k % 2 + 1)],
                                      start=(cc == 0), stop=(cc == NKC - 1))
                        c_p.inc(mm)
                        k += 1
            block.tensor(pe_ln2)

            def act_ln2(a):
                for tt in range(4):
                    a.wait_ge(c_p.sem, p_l1 + (tt + 1) * NKC)
                    c_a.inc(a.activation(out=rstd[:, 512 * tt:512 * (tt + 1)],
                                         in_=pm[tt][:], func=AF.Copy, scale=1.0 / C,
                                         bias=EPS))
            block.scalar(act_ln2)
            a_var = c_a.n

            def dv_lnr(v):
                v.wait_ge(c_a.sem, a_var)
                c_v.inc(v.reciprocal(rstd[:], rstd[:]))
            block.vector(dv_lnr)
            v_rcp = c_v.n

            def act_ln3(a):
                a.wait_ge(c_v.sem, v_rcp)
                c_a.inc(a.activation(out=rstd[:], in_=rstd[:], func=AF.Sqrt))
            block.scalar(act_ln3)
            a_rstd = c_a.n

            def dv_ln3(v):
                v.wait_ge(c_a.sem, a_rstd)
                v.drain()
                last = None
                for cc in range(NKC):
                    last = v.tensor_tensor(out=dst[:, cc * CHUNK:(cc + 1) * CHUNK],
                                           in0=srcb[:, cc * CHUNK:(cc + 1) * CHUNK],
                                           in1=rstd[:], op=OP.mult)
                c_v.inc(last)
            block.vector(dv_ln3)
            lns.close()

        ln_emit(r1b, x2)
        r1s.close()
        xys.close()
        n_x2 = c_v.n
        if "t_x2" in tap_t and ch == 0:
            def sy_t7(s):
                s.wait_ge(c_v.sem, n_x2)
                c_tap.inc(s.dma_start(out=tap_t["t_x2"], in_=x2[:]))
            block.sync(sy_t7)

        seq1 = sbc("seq1", [128, NTC * 16], BF16)
        seq2 = sbc("seq2", [128, NTC * 16], BF16)
        sw1 = sbc("sw1", [128, NTC], F32)
        sw2 = sbc("sw2", [128, NTC], F32)
        x2rows = sbc("x2rows", [128, NTC * C], BF16)

        p5a = ExitStack()
        px2 = [psc(p5a, f"px2_{i}", [128, 1024], BF16) for i in range(2)]
        p_b5a = c_p.n
        a_b5a = c_a.n

        def pe_x2r(t):
            t.wait_ge(c_v.sem, n_x2)
            for tc in range(NTC):
                if tc >= 2:
                    t.wait_ge(c_a.sem, a_b5a + tc - 1)
                for cc in range(NKC):
                    mm = t.transpose(px2[tc % 2][:, 128 * cc:128 * (cc + 1)],
                                     x2[:, cc * CHUNK + 128 * tc: cc * CHUNK + 128 * (tc + 1)],
                                     id16[:])
                c_p.inc(mm)
        block.tensor(pe_x2r)

        def act_x2r(a):
            for tc in range(NTC):
                a.wait_ge(c_p.sem, p_b5a + tc + 1)
                c_a.inc(a.activation(out=x2rows[:, tc * C:(tc + 1) * C],
                                     in_=px2[tc % 2][:], func=AF.Identity))
        block.scalar(act_x2r)
        n_x2r_a = c_a.n

        def sy_x2r(s):
            if ch == 0:
                s.wait_ge(c_g2.sem, n_cast)
            for tc in range(NTC):
                s.wait_ge(c_a.sem, a_b5a + tc + 1)
                c_x2r.inc(s.dma_start(out=x2r_h[ch][128 * tc:128 * (tc + 1), :],
                                      in_=x2rows[:, tc * C:(tc + 1) * C]))
        block.sync(sy_x2r)
        n_x2rw = c_x2r.n
        p5a.close()

        p5s = ExitStack()
        sc_rows = sbc("sc_rows", [128, NTC * 16], F32)
        sc48 = sbc("sc48", [128, NTC * 48], F32)
        rs2r = sbc("rs2r", [128, NTC * 16], F32)
        rm1 = sbc("rm1", [128, 16], F32)
        rm2 = sbc("rm2", [128, 16], F32)
        rwt = sbc("rwt", [128, 16], F32)
        sg1 = sbc("sg1", [128, 16], F32)
        sg2 = sbc("sg2", [128, 16], F32)
        plg = psc(p5s, "plg", [128, NTC * 64], F32)

        def pe_p5(t):
            tw(t)
            t.wait_ge(c_a.sem, n_x2r_a)
            t.wait_ge(c_v.sem, n_gwT)
            for tc in range(NTC):
                for cc in range(NKC):
                    mm = t.matmul(plg[:, 64 * tc:64 * tc + 48],
                                  x2[:, cc * CHUNK + 128 * tc: cc * CHUNK + 128 * (tc + 1)],
                                  gwT[:, 48 * cc:48 * (cc + 1)],
                                  start=(cc == 0), stop=(cc == NKC - 1))
                c_p.inc(mm)
        block.tensor(pe_p5)
        p_lg = c_p.n

        def act_p5(a):
            a.wait_ge(c_p.sem, p_lg)
            c_a.inc(a.activation(
                out=sc48[:].rearrange("p (tc s) -> p tc s", s=48),
                in_=plg[:].rearrange("p (tc s) -> p tc s", s=64)[:, :, 0:48],
                func=AF.Identity))
        block.scalar(act_p5)
        a_sc = c_a.n

        def dv_p5(v):
            v.wait_ge(c_a.sem, a_sc)
            v.drain()
            for tc in range(NTC):
                v.tensor_tensor(out=sc_rows[:, 16 * tc:16 * (tc + 1)],
                                in0=sc48[:, 48 * tc:48 * tc + 16],
                                in1=sc48[:, 48 * tc + 16:48 * tc + 32], op=OP.add)
            v.drain()
            for tc in range(NTC):
                sc = sc_rows[:, 16 * tc:16 * (tc + 1)]
                v.tensor_tensor(out=sc, in0=sc,
                                in1=sc48[:, 48 * tc + 32:48 * tc + 48], op=OP.add)
            v.drain()
            for tc in range(NTC):
                v.reduce_max(rm1[:, tc:tc + 1], sc_rows[:, 16 * tc:16 * (tc + 1)],
                             axis=mybir.AxisListType.X)
            v.drain()
            for tc in range(NTC):
                v.tensor_scalar(out=seq1[:, 16 * tc:16 * (tc + 1)],
                                in0=sc_rows[:, 16 * tc:16 * (tc + 1)],
                                scalar1=rm1[:, tc:tc + 1],
                                scalar2=None, op0=OP.is_equal)
            v.drain()
            for tc in range(NTC):
                v.scalar_tensor_tensor(out=rs2r[:, 16 * tc:16 * (tc + 1)],
                                       in0=seq1[:, 16 * tc:16 * (tc + 1)],
                                       scalar=-1e9,
                                       in1=sc_rows[:, 16 * tc:16 * (tc + 1)],
                                       op0=OP.mult, op1=OP.add)
            v.drain()
            for tc in range(NTC):
                v.reduce_max(rm2[:, tc:tc + 1], rs2r[:, 16 * tc:16 * (tc + 1)],
                             axis=mybir.AxisListType.X)
            v.drain()
            for tc in range(NTC):
                v.tensor_scalar(out=seq2[:, 16 * tc:16 * (tc + 1)],
                                in0=rs2r[:, 16 * tc:16 * (tc + 1)],
                                scalar1=rm2[:, tc:tc + 1],
                                scalar2=None, op0=OP.is_equal)
            v.drain()
            c_v.inc(v.tensor_copy(rwt[:], rm2[:]))
        block.vector(dv_p5)
        v_max5 = c_v.n

        def act_p5b(a):
            a.wait_ge(c_v.sem, v_max5)
            a.activation(out=sg1[:], in_=rm1[:], func=AF.Sigmoid)
            c_a.inc(a.activation(out=sg2[:], in_=rm2[:], func=AF.Sigmoid))
        block.scalar(act_p5b)
        a_sig5 = c_a.n

        def dv_p5c(v):
            v.wait_ge(c_a.sem, a_sig5)
            v.drain()
            v.tensor_tensor(out=rwt[:], in0=sg1[:], in1=sg2[:], op=OP.add)
            v.drain()
            v.tensor_scalar_add(rwt[:], rwt[:], 1e-20)
            v.drain()
            v.reciprocal(rwt[:], rwt[:])
            v.drain()
            v.tensor_scalar_mul(rwt[:], rwt[:], 2.5)
            v.drain()
            v.tensor_tensor(out=sw1[:], in0=sg1[:], in1=rwt[:], op=OP.mult)
            last = v.tensor_tensor(out=sw2[:], in0=sg2[:], in1=rwt[:], op=OP.mult)
            c_v.inc(last)
        block.vector(dv_p5c)
        n_route = c_v.n
        p5s.close()

        Dk = [sbc(f"Dk{k}", [128, NTC], F32) for k in range(2)]
        Dki = Dki2[ch]
        wbuf = [sbc(f"wbuf{k}", [128, 2 * NTC], F32) for k in range(2)]
        cpre = sbc("cpre", [128, 16], F32)
        cpre2 = sbc("cpre2", [128, 16], F32)
        rnk = sbc("rnk", [128, 16], F32)
        tot1 = sbc("tot1", [128, 16], F32)
        p5b = ExitStack()
        pdp = [psc(p5b, f"pdp{i}", [128, 256], F32) for i in range(2)]

        p_b5b = c_p.n
        v_b5b = c_v.n

        def pe_p5b(t):
            t.wait_ge(c_v.sem, n_route)
            for k in range(2):
                if k == 1:
                    t.wait_ge(c_v.sem, v_b5b + 1)
                sq = seq1 if k == 0 else seq2
                for e in range(NE):
                    mm = t.matmul(pdp[0][:, 16 * e:16 * (e + 1)], ltx[:],
                                  sq[:, e: e + 16 * (NTC - 1) + 1: 16],
                                  start=True, stop=True)
                    mm = t.matmul(pdp[1][:, 16 * e:16 * (e + 1)], ones[:],
                                  sq[:, e: e + 16 * (NTC - 1) + 1: 16],
                                  start=True, stop=True)
                c_p.inc(mm)
        block.tensor(pe_p5b)

        def dv_p5b(v):
            for k in range(2):
                v.wait_ge(c_p.sem, p_b5b + k + 1)
                v.memset(Dk[k][:], 0.0)
                for e in range(NE):
                    pp = pdp[0][:, 16 * e:16 * (e + 1)]
                    cs = pdp[1][:, 16 * e:16 * (e + 1)]
                    v.drain()
                    v.memset(cpre[:, 0:1], 0.0)
                    v.tensor_copy(cpre[:, 1:16], cs[:, 0:15])
                    v.drain()
                    v.tensor_copy(cpre2[:, 0:1], cpre[:, 0:1])
                    v.tensor_tensor(out=cpre2[:, 1:16], in0=cpre[:, 1:16],
                                    in1=cpre[:, 0:15], op=OP.add)
                    v.drain()
                    v.tensor_copy(cpre[:, 0:2], cpre2[:, 0:2])
                    v.tensor_tensor(out=cpre[:, 2:16], in0=cpre2[:, 2:16],
                                    in1=cpre2[:, 0:14], op=OP.add)
                    v.drain()
                    v.tensor_copy(cpre2[:, 0:4], cpre[:, 0:4])
                    v.tensor_tensor(out=cpre2[:, 4:16], in0=cpre[:, 4:16],
                                    in1=cpre[:, 0:12], op=OP.add)
                    v.drain()
                    v.tensor_copy(cpre[:, 0:8], cpre2[:, 0:8])
                    v.tensor_tensor(out=cpre[:, 8:16], in0=cpre2[:, 8:16],
                                    in1=cpre2[:, 0:8], op=OP.add)
                    v.drain()
                    if k == 0:
                        v.tensor_tensor(out=tot1[:, e:e + 1], in0=cpre[:, 15:16],
                                        in1=cs[:, 15:16], op=OP.add)
                        v.drain()
                    v.tensor_tensor(out=rnk[:], in0=pp, in1=cpre[:], op=OP.add)
                    v.drain()
                    if k == 0:
                        v.tensor_scalar_add(rnk[:], rnk[:], float(ECAP * e))
                    else:
                        v.tensor_scalar(out=rnk[:], in0=rnk[:], scalar1=tot1[:, e:e + 1],
                                        scalar2=float(ECAP * e), op0=OP.add, op1=OP.add)
                    v.drain()
                    v.tensor_scalar(out=rs1[:] if False else cpre[:], in0=rnk[:],
                                    scalar1=float(ECAP * (e + 1)), scalar2=None,
                                    op0=OP.is_lt)
                    v.drain()
                    tsl = trashc[:, 16 * k:16 * (k + 1)]
                    v.tensor_tensor(out=rnk[:], in0=rnk[:], in1=tsl, op=OP.subtract)
                    v.drain()
                    v.tensor_tensor(out=rnk[:], in0=rnk[:], in1=cpre[:], op=OP.mult)
                    v.drain()
                    v.tensor_tensor(out=rnk[:], in0=rnk[:], in1=tsl, op=OP.add)
                    v.drain()
                    sq = seq1 if k == 0 else seq2
                    sel_ap = sq[:, e: e + 16 * (NTC - 1) + 1: 16]
                    v.tensor_tensor(out=rnk[:], in0=rnk[:], in1=sel_ap, op=OP.mult)
                    v.drain()
                    last = v.tensor_tensor(out=Dk[k][:], in0=Dk[k][:], in1=rnk[:], op=OP.add)
                    v.drain()
                v.tensor_copy(Dki[k][:], Dk[k][:])
                v.tensor_copy(wbuf[k][:, 0:2 * NTC:2], iotac[:])
                v.drain()
                sw = sw1 if k == 0 else sw2
                last = v.tensor_copy(wbuf[k][:, 1:2 * NTC:2], sw[:])
                c_v.inc(last)
        block.vector(dv_p5b)
        n_disp = c_v.n

        def gq_p5b(g):
            g.wait_ge(c_v.sem, n_disp)
            g.wait_ge(c_x2r.sem, n_x2rw)
            for k in range(2):
                for tc in range(NTC):
                    c_ds.inc(g.indirect_dma_start(
                        out=disp_h[ch],
                        out_offset=bass.IndirectOffsetOnAxis(ap=Dki[k][:, tc:tc + 1], axis=0),
                        in_=wbuf[k][:, 2 * tc:2 * tc + 2], in_offset=None))
        block.gpsimd(gq_p5b)
        n_ds = c_ds.n
        p5b.close()

        p6sx = ExitStack()
        wsg_sb = sbc("wsg_sb", [128, NKC * MFF], BF16)
        wsu_sb = sbc("wsu_sb", [128, NKC * MFF], BF16)
        wsd_sb = sbc("wsd_sb", [128, 4 * C], BF16)

        def sy_wsld(s):
            s.wait_ge(c_g2.sem, n_cast)
            c_ws.inc(s.dma_start(out=wsg_sb[:], in_=wsg_h))
            c_ws.inc(s.dma_start(out=wsu_sb[:], in_=wsu_h))
            c_ws.inc(s.dma_start(out=wsd_sb[:], in_=wsd_h))
        block.sync(sy_wsld)
        n_ws = c_ws.n

        sg_b = sbc("sg_b", [128, 512], F32)
        h_bf = sbc("h_bf", [128, 2 * 4 * 256], BF16)
        eo_sb = [[sbc(f"eo_sb{g2}_{i2}", [128, C], F32) for i2 in range(2)]
                 for g2 in range(2)]
        pdnh = [psc(p6sx, f"pdnh{i2}", [128, 512], F32) for i2 in range(2)]
        pga = [psc(p6sx, f"pga{i2}", [128, 512], F32) for i2 in range(2)]
        pua = [psc(p6sx, f"pua{i2}", [128, 512], F32) for i2 in range(2)]

        NTAU_S = 8
        sp0, sa0, sv0 = c_p.n, c_a.n, c_v.n
        s_pe, s_a, s_v = {}, {}, {}

        def s_order():
            seq = []
            for tau in range(NTAU_S):
                if tau >= 2:
                    seq.append(("down", tau - 2))
                seq.append(("gu", tau))
            seq.append(("down", NTAU_S - 2))
            seq.append(("down", NTAU_S - 1))
            return seq

        n = sp0
        for item in s_order():
            if item[0] == "gu":
                for j in range(4):
                    n += 1
                    s_pe[("gu", item[1], j)] = n
            else:
                for q in range(4):
                    n += 1
                    s_pe[("downq", item[1], q)] = n
                s_pe[("down", item[1])] = n
        n = sa0
        for tau in range(NTAU_S):
            for mc in range(4):
                n += 1
                s_a[("silu", tau, mc)] = n
        n = sv0
        for tau in range(NTAU_S):
            for mc in range(4):
                n += 1
                s_v[("h", tau, mc)] = n
            if tau >= 1:
                for q in range(4):
                    n += 1
                    s_v[("evacq", tau - 1, q)] = n
        for q in range(4):
            n += 1
            s_v[("evacq", NTAU_S - 1, q)] = n
        n_p6s_v = n

        shw_b = (c_shw[0].n, c_shw[1].n)

        def sy_shw(s):
            for tau in range(NTAU_S):
                s.wait_ge(c_v.sem, s_v[("evacq", tau, 3)])
                for tb in range(2):
                    tc_g = 2 * tau + tb
                    c_shw[tau % 2].inc(s.dma_start(
                        out=acc_h[ch][128 * tc_g:128 * (tc_g + 1), :],
                        in_=eo_sb[tau % 2][tb][:]))
        block.sync(sy_shw)

        def pe_p6s(t):
            tw(t)
            t.wait_ge(c_ws.sem, n_ws)
            t.wait_ge(c_v.sem, n_x2)
            fdown = [True]
            dq = [0]
            for item in s_order():
                if item[0] == "gu":
                    tau = item[1]
                    for half in range(2):
                        for w_i in range(2):
                            bank = (pga if w_i == 0 else pua)[half]
                            wb_ = wsg_sb if w_i == 0 else wsu_sb
                            if tau >= 1:
                                if w_i == 0:
                                    t.wait_ge(c_a.sem, s_a[("silu", tau - 1, 2 * half + 1)])
                                else:
                                    t.wait_ge(c_v.sem, s_v[("h", tau - 1, 2 * half + 1)])
                            for mcl in range(2):
                                mc = 2 * half + mcl
                                for cc in range(NKC):
                                    mm = t.matmul(
                                        bank[:, 256 * mcl:256 * (mcl + 1)],
                                        wb_[:, cc * MFF + 128 * mc: cc * MFF + 128 * (mc + 1)],
                                        x2[:, cc * CHUNK + 256 * tau: cc * CHUNK + 256 * (tau + 1)],
                                        start=(cc == 0), stop=(cc == NKC - 1))
                            c_p.inc(mm)
                else:
                    tau = item[1]
                    if fdown[0]:
                        t.wait_ge(c_v.sem, n_disp)
                        fdown[0] = False
                    t.wait_ge(c_v.sem, s_v[("h", tau, 3)])
                    hb = 1024 * (tau % 2)
                    for q in range(4):
                        tb, fh = q % 2, q // 2
                        if dq[0] >= 2:
                            pt_, pq_ = divmod(dq[0] - 2, 4)
                            t.wait_ge(c_v.sem, s_v[("evacq", pt_, pq_)])
                        for mc in range(4):
                            mm = t.matmul(
                                pdnh[q % 2][:],
                                h_bf[:, hb + mc * 256 + 128 * tb: hb + mc * 256 + 128 * (tb + 1)],
                                wsd_sb[:, mc * C + 512 * fh: mc * C + 512 * (fh + 1)],
                                start=(mc == 0), stop=(mc == 3))
                        n_ = c_p.inc(mm)
                        assert n_ == s_pe[("downq", tau, q)]
                        dq[0] += 1
        block.tensor(pe_p6s)

        def act_p6s(a):
            for tau in range(NTAU_S):
                for mc in range(4):
                    a.wait_ge(c_p.sem, s_pe[("gu", tau, 2 * (mc // 2))])
                    gl = tau * 4 + mc
                    if gl >= 2:
                        t2, m2 = divmod(gl - 2, 4)
                        a.wait_ge(c_v.sem, s_v[("h", t2, m2)])
                    c_a.inc(a.activation(
                        out=sg_b[:, 256 * (mc % 2):256 * (mc % 2 + 1)],
                        in_=pga[mc // 2][:, 256 * (mc % 2):256 * (mc % 2 + 1)],
                        func=AF.Silu))
        block.scalar(act_p6s)

        def s_evac(v, tau):
            if tau >= 2:
                par = tau % 2
                v.wait_ge(c_shw[par].sem, shw_b[par] + 32 * (tau // 2))
            v.drain()
            for q in range(4):
                tb, fh = q % 2, q // 2
                v.wait_ge(c_p.sem, s_pe[("downq", tau, q)])
                tc_g = 2 * tau + tb
                op = v.tensor_tensor(
                    out=eo_sb[tau % 2][tb][:, 512 * fh:512 * (fh + 1)],
                    in0=pdnh[q % 2][:],
                    in1=x2rows[:, tc_g * C + 512 * fh: tc_g * C + 512 * (fh + 1)],
                    op=OP.add)
                n_ = c_v.inc(op)
                assert n_ == s_v[("evacq", tau, q)]

        def dv_p6s(v):
            for tau in range(NTAU_S):
                if tau >= 2:
                    v.wait_ge(c_p.sem, s_pe[("down", tau - 2)])
                hb = 1024 * (tau % 2)
                for mc in range(4):
                    v.wait_ge(c_a.sem, s_a[("silu", tau, mc)])
                    v.wait_ge(c_p.sem, s_pe[("gu", tau, 2 * (mc // 2) + 1)])
                    c_v.inc(v.tensor_tensor(
                        out=h_bf[:, hb + mc * 256: hb + (mc + 1) * 256],
                        in0=sg_b[:, 256 * (mc % 2):256 * (mc % 2 + 1)],
                        in1=pua[mc // 2][:, 256 * (mc % 2):256 * (mc % 2 + 1)],
                        op=OP.mult))
                if tau >= 1:
                    s_evac(v, tau - 1)
            s_evac(v, NTAU_S - 1)
        block.vector(dv_p6s)
        assert c_v.n == n_p6s_v
        p6sx.close()

        ch_marks.append(dict(n_ds=n_ds, n_x2rw=n_x2rw, n_disp=n_disp, n_x2=n_x2))


        prev_chunk_done = dict(v=c_v.n, o0=c_o[0].n, o1=c_o[1].n, p=c_p.n, a=c_a.n,
                               shw0=c_shw[0].n, shw1=c_shw[1].n,
                               ds=c_ds.n, x2r=c_x2r.n)
        chs.close()

    pass1_v = c_v.n
    p6r = ExitStack()
    sb7 = lambda name, shape, dt: p6r.enter_context(nc.sbuf_tensor(name, shape, dt))
    ps7 = lambda name, shape, dt: p6r.enter_context(nc.psum_tensor(name, shape, dt))
    wgb = [sb7(f"wgb{i}", [128, NKC * MFF], BF16) for i in range(2)]
    wub = [sb7(f"wub{i}", [128, NKC * MFF], BF16) for i in range(2)]
    wdb = [sb7(f"wdb{i}", [128, 4 * C], BF16) for i in range(2)]
    xgT = [sb7(f"xgT{i}", [128, NKC * ECAP], BF16) for i in range(4)]
    xgr = [sb7(f"xgr{i}", [128, (ECAP // 128) * C], BF16) for i in range(4)]
    idw = [sb7(f"idw{i}", [128, (ECAP // 128) * 2], F32) for i in range(4)]
    idi = [sb7(f"idi{i}", [128, ECAP // 128], mybir.dt.int32) for i in range(4)]
    sg_b2 = sb7("sg_b2", [128, 512], F32)
    h_bf2 = sb7("h_bf2", [128, 2 * 4 * 256], BF16)
    eo_bf = [[sb7(f"eo_bf{g2}_{i2}", [128, C], BF16) for i2 in range(2)]
             for g2 in range(2)]
    pdn2 = [ps7(f"pdn2_{i}", [128, 512], F32) for i in range(2)]
    ptt = [ps7(f"ptt{i}", [128, 1024], BF16) for i in range(2)]
    pga2 = [ps7(f"pga2_{i}", [128, 512], F32) for i in range(2)]
    pua2 = [ps7(f"pua2_{i}", [128, 512], F32) for i in range(2)]

    NU = 2 * NE
    ue = lambda u: u // 2
    uc = lambda u: u % 2

    t_at_unit = {u: [] for u in range(NU)}
    t_at_unit[0] = [0, 1, 2]
    for u in range(1, NU - 2):
        t_at_unit[u].append(u + 2)

    r_pe, r_a, r_v = {}, {}, {}
    tr_seq = []
    dq_seq = []
    u_items = (("gu", 0), ("gu", 1), ("down", 0), ("down", 1))
    n = c_p.n
    for u in range(NU):
        for u_t in t_at_unit[u]:
            for j in range(4):
                n += 1
                r_pe[("tr", u_t, j)] = n
                tr_seq.append((u_t, j))
        for item in u_items:
            if item[0] == "gu":
                for j in range(4):
                    n += 1
                    r_pe[("gu", u, item[1], j)] = n
            else:
                for q in range(4):
                    n += 1
                    r_pe[("downq", u, item[1], q)] = n
                    dq_seq.append((u, item[1], q))
                r_pe[("down", u, item[1])] = n
    n = c_a.n
    for u in range(NU):
        for u_t in t_at_unit[u]:
            for j in range(4):
                n += 1
                r_a[("trv", u_t, j)] = n
        for tau in range(2):
            for mc in range(4):
                n += 1
                r_a[("silu", u, tau, mc)] = n
    n = c_v.n
    for u in range(NU):
        for tau in range(2):
            for mc in range(4):
                n += 1
                r_v[("h", u, tau, mc)] = n
            if tau >= 1:
                for q in range(4):
                    n += 1
                    r_v[("evacq", u, 0, q)] = n
        for q in range(4):
            n += 1
            r_v[("evacq", u, 1, q)] = n
    n_p6r_v = n

    gx_b2 = tuple(c_gx[i].n for i in range(4))
    gx_marks2 = {u: (u % 4, gx_b2[u % 4] + 64 * (u // 4 + 1)) for u in range(NU)}
    idc_b2 = c_idc.n
    v_idc2 = {u: idc_b2 + u + 1 for u in range(NU)}
    eo_b = (c_eo[0].n, c_eo[1].n)
    evac_consumer2 = {}
    _eo_n = [0, 0]
    for u in range(NU):
        for tau in range(2):
            g_ = 2 * u + tau
            _eo_n[g_ % 2] += 2
            evac_consumer2[g_] = (g_ % 2, eo_b[g_ % 2] + 16 * _eo_n[g_ % 2])

    sy_marks = {}
    id_marks = {}

    def sy_ffnr(s):
        s.wait_ge(c_v.sem, pass1_v)
        s.wait_ge(c_shw[0].sem, c_shw[0].n)
        s.wait_ge(c_shw[1].sem, c_shw[1].n)
        s.wait_ge(c_x2r.sem, c_x2r.n)
        s.wait_ge(c_ds.sem, c_ds.n)

        def wload(e):
            if e >= 2:
                s.wait_ge(c_p.sem, r_pe[("down", 2 * (e - 2) + 1, 1)])
            ce = c_e[e % 2]
            ce.inc(s.dma_start(out=wgb[e % 2][:], in_=wg_h[128 * e:128 * (e + 1), :]))
            ce.inc(s.dma_start(out=wub[e % 2][:], in_=wu_h[128 * e:128 * (e + 1), :]))
            ce.inc(s.dma_start(out=wdb[e % 2][:], in_=wd_h[128 * e:128 * (e + 1), :]))
            sy_marks[e] = (e % 2, ce.n)

        wload(0)
        wload(1)
        for u in range(2, NU):
            if u % 2 == 0 and ue(u) + 1 < NE:
                wload(ue(u) + 1)
    block.sync(sy_ffnr)

    def gq_p6r(g):
        g.wait_ge(c_v.sem, pass1_v)
        g.wait_ge(c_x2r.sem, c_x2r.n)
        g.wait_ge(c_ds.sem, c_ds.n)
        g.wait_ge(c_shw[0].sem, c_shw[0].n)
        g.wait_ge(c_shw[1].sem, c_shw[1].n)

        def idload(u):
            b = u % 4
            if u >= 4:
                g.wait_ge(c_v.sem, r_v[("evacq", u - 4, 1, 3)])
            c_id[b].inc(g.dma_start(
                out=idw[b][:].rearrange("p (r two) -> p r two", two=2),
                in_=_t3(disp_h[uc(u)][ECAP * ue(u):ECAP * (ue(u) + 1), :])))
            id_marks[u] = (b, c_id[b].n)

        def gathers(u):
            b = u % 4
            bb = u % 4
            g.wait_ge(c_x2r.sem, ch_marks[uc(u)]["n_x2rw"])
            g.wait_ge(c_idc.sem, v_idc2[u])
            if u >= 4:
                g.wait_ge(c_p.sem, r_pe[("tr", u - 4, 3)])
            for rc in range(ECAP // 128):
                c_gx[b].inc(g.indirect_dma_start(
                    out=xgr[bb][:, rc * C:(rc + 1) * C], out_offset=None,
                    in_=x2r_h[uc(u)],
                    in_offset=bass.IndirectOffsetOnAxis(ap=idi[bb][:, rc:rc + 1], axis=0)))
            assert gx_marks2[u] == (b, c_gx[b].n)

        def eowrites(u):
            for tau in range(2):
                g.wait_ge(c_v.sem, r_v[("evacq", u, tau, 3)])
                for tb in range(2):
                    rc = 2 * tau + tb
                    c_eo[tau % 2].inc(g.dma_start(
                        out=eo_h[uc(u)][ECAP * ue(u) + 128 * rc:
                                        ECAP * ue(u) + 128 * (rc + 1), :],
                        in_=eo_bf[tau % 2][tb][:]))

        for u in range(4):
            idload(u)
        gathers(0)
        gathers(1)
        gathers(2)
        gathers(3)
        for u in range(NU):
            eowrites(u)
            if u + 4 < NU:
                idload(u + 4)
                gathers(u + 4)
    block.gpsimd(gq_p6r)
    n_eo = (c_eo[0].n, c_eo[1].n)

    def pe_p6r(t):
        tw(t)
        t.wait_ge(c_v.sem, pass1_v)
        tr_i = [0]
        dq_i = [0]
        for u in range(NU):
            e = ue(u)
            t.wait_ge(c_e[sy_marks[e][0]].sem, sy_marks[e][1])
            for u_t in t_at_unit[u]:
                bt = u_t % 4
                t.wait_ge(c_gx[gx_marks2[u_t][0]].sem, gx_marks2[u_t][1])
                for j in range(4):
                    gi = tr_i[0]
                    if gi >= 2:
                        pu_, pj_ = tr_seq[gi - 2]
                        t.wait_ge(c_a.sem, r_a[("trv", pu_, pj_)])
                    for cc in range(NKC):
                        mm = t.transpose(
                            ptt[gi % 2][:, 128 * cc:128 * (cc + 1)],
                            xgr[bt][:, j * C + 128 * cc: j * C + 128 * (cc + 1)],
                            id16[:])
                    n_ = c_p.inc(mm)
                    assert n_ == r_pe[("tr", u_t, j)]
                    tr_i[0] += 1
            t.wait_ge(c_a.sem, r_a[("trv", u, 3)])
            for item in u_items:
                if item[0] == "gu":
                    tau = item[1]
                    prev = None
                    if (u, tau) != (0, 0):
                        prev = (u, 0) if tau == 1 else (u - 1, 1)
                    for half in range(2):
                        for w_i in range(2):
                            bank = (pga2 if w_i == 0 else pua2)[half]
                            wbuf_ = (wgb if w_i == 0 else wub)[e % 2]
                            if prev is not None:
                                if w_i == 0:
                                    t.wait_ge(c_a.sem,
                                              r_a[("silu",) + prev + (2 * half + 1,)])
                                else:
                                    t.wait_ge(c_v.sem,
                                              r_v[("h",) + prev + (2 * half + 1,)])
                            for mcl in range(2):
                                mc = 2 * half + mcl
                                for cc in range(NKC):
                                    mm = t.matmul(
                                        bank[:, 256 * mcl:256 * (mcl + 1)],
                                        wbuf_[:, cc * MFF + 128 * mc: cc * MFF + 128 * (mc + 1)],
                                        xgT[u % 4][:, cc * ECAP + 256 * tau:
                                                   cc * ECAP + 256 * (tau + 1)],
                                        start=(cc == 0), stop=(cc == NKC - 1))
                            c_p.inc(mm)
                else:
                    tau = item[1]
                    t.wait_ge(c_v.sem, r_v[("h", u, tau, 3)])
                    hb = 1024 * (tau % 2)
                    for q in range(4):
                        tb, fh = q % 2, q // 2
                        gi = dq_i[0]
                        if gi >= 2:
                            k_ = dq_seq[gi - 2]
                            t.wait_ge(c_v.sem, r_v[("evacq",) + k_])
                        for mc in range(4):
                            mm = t.matmul(
                                pdn2[gi % 2][:],
                                h_bf2[:, hb + mc * 256 + 128 * tb:
                                      hb + mc * 256 + 128 * (tb + 1)],
                                wdb[e % 2][:, mc * C + 512 * fh: mc * C + 512 * (fh + 1)],
                                start=(mc == 0), stop=(mc == 3))
                        n_ = c_p.inc(mm)
                        assert n_ == r_pe[("downq", u, tau, q)]
                        dq_i[0] += 1
    block.tensor(pe_p6r)

    def act_p6r(a):
        tr_i = [0]
        for u in range(NU):
            for u_t in t_at_unit[u]:
                bt = u_t % 4
                if u_t >= 4:
                    a.wait_ge(c_p.sem, r_pe[("gu", u_t - 4, 1, 3)])
                for j in range(4):
                    a.wait_ge(c_p.sem, r_pe[("tr", u_t, j)])
                    n_ = c_a.inc(a.activation(
                        out=xgT[bt][:].rearrange("p (cc j2) -> p cc j2",
                                                 j2=ECAP)[:, :, 128 * j:128 * (j + 1)],
                        in_=ptt[tr_i[0] % 2][:], func=AF.Identity))
                    assert n_ == r_a[("trv", u_t, j)]
                    tr_i[0] += 1
            for tau in range(2):
                for mc in range(4):
                    a.wait_ge(c_p.sem, r_pe[("gu", u, tau, 2 * (mc // 2))])
                    gl = (2 * u + tau) * 4 + mc
                    if gl >= 2:
                        u2, r2 = divmod(gl - 2, 8)
                        tau2, mc2 = divmod(r2, 4)
                        a.wait_ge(c_v.sem, r_v[("h", u2, tau2, mc2)])
                    c_a.inc(a.activation(
                        out=sg_b2[:, 256 * (mc % 2):256 * (mc % 2 + 1)],
                        in_=pga2[mc // 2][:, 256 * (mc % 2):256 * (mc % 2 + 1)],
                        func=AF.Silu))
    block.scalar(act_p6r)

    def emit_cast2(v, u):
        b = u % 4
        v.wait_ge(c_id[id_marks[u][0]].sem, id_marks[u][1])
        cp = v.tensor_copy(idi[b][:], idw[b][:, 0:2 * (ECAP // 128):2])
        cp.then_inc(c_idc.sem, 1)
        c_idc.n += 1
        assert c_idc.n == v_idc2[u]

    def r_evac(v, u, tau):
        g_ = 2 * u + tau
        if g_ >= 2:
            par, cnt = evac_consumer2[g_ - 2]
            v.wait_ge(c_eo[par].sem, cnt)
        v.drain()
        for q in range(4):
            tb, fh = q % 2, q // 2
            v.wait_ge(c_p.sem, r_pe[("downq", u, tau, q)])
            rc = 2 * tau + tb
            op = v.tensor_scalar(
                out=eo_bf[tau % 2][tb][:, 512 * fh:512 * (fh + 1)],
                in0=pdn2[q % 2][:],
                scalar1=idw[u % 4][:, 2 * rc + 1:2 * rc + 2], scalar2=None,
                op0=OP.mult)
            n_ = c_v.inc(op)
            assert n_ == r_v[("evacq", u, tau, q)]

    def dv_p6r(v):
        emit_cast2(v, 0)
        emit_cast2(v, 1)
        emit_cast2(v, 2)
        emit_cast2(v, 3)
        for u in range(NU):
            for tau in range(2):
                gt_ = 2 * u + tau
                if gt_ >= 2:
                    u2, tau2 = divmod(gt_ - 2, 2)
                    v.wait_ge(c_p.sem, r_pe[("down", u2, tau2)])
                hb = 1024 * (tau % 2)
                for mc in range(4):
                    v.wait_ge(c_a.sem, r_a[("silu", u, tau, mc)])
                    v.wait_ge(c_p.sem, r_pe[("gu", u, tau, 2 * (mc // 2) + 1)])
                    c_v.inc(v.tensor_tensor(
                        out=h_bf2[:, hb + mc * 256: hb + (mc + 1) * 256],
                        in0=sg_b2[:, 256 * (mc % 2):256 * (mc % 2 + 1)],
                        in1=pua2[mc // 2][:, 256 * (mc % 2):256 * (mc % 2 + 1)],
                        op=OP.mult))
                if tau >= 1:
                    r_evac(v, u, 0)
            r_evac(v, u, 1)
            if u + 4 < NU:
                emit_cast2(v, u + 4)
    block.vector(dv_p6r)
    assert c_v.n == n_p6r_v, (c_v.n, n_p6r_v)
    p6r.close()

    p7s = ExitStack()
    accr = [p7s.enter_context(nc.sbuf_tensor(f"accr_{i}", [128, C], F32))
            for i in range(2)]
    outr = [p7s.enter_context(nc.sbuf_tensor(f"outr_{i}", [128, C], F32))
            for i in range(2)]
    bnst = p7s.enter_context(nc.sbuf_tensor("bnst", [128, 12], F32))
    bnag = p7s.enter_context(nc.sbuf_tensor("bnag", [128, 2], F32))
    rstd2 = p7s.enter_context(nc.sbuf_tensor("rstd2", [128, 1], F32))
    egb = [[p7s.enter_context(nc.sbuf_tensor(f"egb_{k}_{i}", [128, C], BF16))
            for i in range(2)] for k in range(2)]

    NT2 = NCHUNK * NTC
    v_b7 = c_v.n
    a_b7 = c_a.n
    o_b7 = [c_o[0].n, c_o[1].n]
    ac_b = [c_ac[0].n, c_ac[1].n]
    eg_b = [c_eg[0].n, c_eg[1].n]
    pe_last = r_pe[("down", NU - 1, 1)]

    def sy_p7(s):
        def store(ti):
            b = ti % 2
            s.wait_ge(c_v.sem, v_b7 + 2 * ti + 2)
            c_o[b].inc(s.dma_start(
                out=out_hbm[128 * ti: 128 * (ti + 1), :], in_=outr[b][:]))
        s.wait_ge(c_shw[0].sem, c_shw[0].n)
        s.wait_ge(c_shw[1].sem, c_shw[1].n)
        for cc_ in (c_e[0], c_e[1], c_gx[0], c_gx[1], c_gx[2], c_gx[3],
                    c_id[0], c_id[1], c_id[2], c_id[3], c_eo[0], c_eo[1]):
            s.wait_ge(cc_.sem, cc_.n)
        s.wait_ge(c_p.sem, pe_last)
        s.wait_ge(c_v.sem, n_p6r_v)
        for ti in range(NT2):
            b = ti % 2
            if ti >= 2:
                store(ti - 2)
            ch2, tc = divmod(ti, NTC)
            c_ac[b].inc(s.dma_start(out=accr[b][:],
                                    in_=acc_h[ch2][128 * tc:128 * (tc + 1), :]))
        store(NT2 - 2)
        store(NT2 - 1)
    block.sync(sy_p7)

    def gq_p7(g):
        g.wait_ge(c_eo[0].sem, n_eo[0])
        g.wait_ge(c_eo[1].sem, n_eo[1])
        for cc_ in (c_e[0], c_e[1], c_id[0], c_id[1], c_id[2], c_id[3]):
            g.wait_ge(cc_.sem, cc_.n)
        g.wait_ge(c_p.sem, pe_last)
        for ti in range(NT2):
            b = ti % 2
            ch2, tc = divmod(ti, NTC)
            if ti >= 2:
                g.wait_ge(c_v.sem, v_b7 + 2 * (ti - 2) + 1)
            for k in range(2):
                c_eg[b].inc(g.indirect_dma_start(
                    out=egb[k][b][:], out_offset=None, in_=eo_h[ch2],
                    in_offset=bass.IndirectOffsetOnAxis(ap=Dki2[ch2][k][:, tc:tc + 1],
                                                        axis=0)))
    block.gpsimd(gq_p7)

    def dv_p7(v):
        for ti in range(NT2):
            b = ti % 2
            v.wait_ge(c_ac[b].sem, ac_b[b] + 16 * (ti // 2 + 1))
            v.wait_ge(c_eg[b].sem, eg_b[b] + 32 * (ti // 2 + 1))
            v.drain()
            v.tensor_tensor(out=accr[b][:], in0=accr[b][:], in1=egb[0][b][:],
                            op=OP.add)
            v.drain()
            v.tensor_tensor(out=accr[b][:], in0=accr[b][:], in1=egb[1][b][:],
                            op=OP.add)
            v.drain()
            v.bn_stats(bnst[:, 0:6], accr[b][:, 0:512])
            v.bn_stats(bnst[:, 6:12], accr[b][:, 512:1024])
            v.drain()
            v.bn_aggr(bnag[:], bnst[:])
            v.drain()
            v.tensor_scalar_add(rstd2[:], bnag[:, 1:2], EPS)
            v.drain()
            c_v.inc(v.reciprocal(rstd2[:], rstd2[:]))
            v.wait_ge(c_a.sem, a_b7 + ti + 1)
            if ti >= 2:
                v.wait_ge(c_o[b].sem, o_b7[b] + 16 * (ti // 2))
            v.drain()
            c_v.inc(v.tensor_scalar(
                out=outr[b][:], in0=accr[b][:], scalar1=bnag[:, 0:1],
                scalar2=rstd2[:], op0=OP.subtract, op1=OP.mult))
    block.vector(dv_p7)

    def act_p7(a):
        for ti in range(NT2):
            a.wait_ge(c_v.sem, v_b7 + 2 * ti + 1)
            c_a.inc(a.activation(out=rstd2[:], in_=rstd2[:], func=AF.Sqrt))
    block.scalar(act_p7)
    p7s.close()


    def sy_final(s):
        s.wait_ge(c_o[0].sem, c_o[0].n)
        s.wait_ge(c_o[1].sem, c_o[1].n)
        if c_tap.n:
            s.wait_ge(c_tap.sem, c_tap.n)
    block.sync(sy_final)

    blk_cm.__exit__(None, None, None)
    stk.close()
    return nc


def _host_consts():
    i = np.arange(128)[:, None]
    j = np.arange(128)[None, :]
    m_list = [(j >= i)]
    for dp in (0, 1):
        for t_ in (0, 1):
            m_list.append((j + 128 * t_) >= (2 * i + dp))
    masks = np.concatenate([m.astype(np.float32) for m in m_list], axis=1)
    sel = np.zeros((16, 16 * 128), np.float32)
    for e in range(16):
        sel[e, 128 * e:128 * (e + 1)] = 1.0
    ltx = (np.arange(128)[:, None] < np.arange(128)[None, :]).astype(np.float32)
    iota = (np.arange(NTC)[None, :] * 128 + np.arange(128)[:, None]).astype(np.float32)
    dinit = np.zeros((NE * ECAP + 128, 2), np.float32)
    dinit[:, 0] = CHUNK
    return dict(
        id32=np.eye(128, dtype=np.float32),
        id16=np.eye(128, dtype=np.float32).astype(ml_dtypes.bfloat16),
        ones=np.ones((128, 128), dtype=np.float32).astype(ml_dtypes.bfloat16),
        masks=masks.astype(ml_dtypes.bfloat16),
        sel=sel.astype(ml_dtypes.bfloat16),
        ltx=ltx.astype(ml_dtypes.bfloat16),
        iota=iota,
        dinit=dinit,
        zrow=np.zeros((1, C), ml_dtypes.bfloat16),
        zrows=np.zeros((128, C), ml_dtypes.bfloat16),
        trashc=np.broadcast_to(
            NE * ECAP + np.arange(2 * NTC, dtype=np.float32)[None, :],
            (128, 2 * NTC)).copy(),
    )


def kernel(L0_attn_x, L0_attn_c_attn_w, L0_attn_c_attn_b, L0_attn_q_w, L0_attn_q_b,
           L0_attn_k_w, L0_attn_k_b, L0_attn_v_w, L0_attn_v_b, L0_attn_o_w, L0_attn_o_b,
           L0_ffn_gate_w, L0_ffn_w_gate, L0_ffn_w_up, L0_ffn_w_down,
           L0_ffn_ws_gate, L0_ffn_ws_up, L0_ffn_ws_down, _taps=(), _cores=None):
    nc = build(taps=_taps)
    xf = np.ascontiguousarray(np.asarray(L0_attn_x).reshape(B * T, C))
    consts = _host_consts()
    shared = dict(
        caw=np.ascontiguousarray(L0_attn_c_attn_w), cab=np.ascontiguousarray(L0_attn_c_attn_b),
        qw=np.ascontiguousarray(L0_attn_q_w), qb=np.ascontiguousarray(L0_attn_q_b),
        kw=np.ascontiguousarray(L0_attn_k_w), kb=np.ascontiguousarray(L0_attn_k_b),
        vw=np.ascontiguousarray(L0_attn_v_w), vb=np.ascontiguousarray(L0_attn_v_b),
        ow=np.ascontiguousarray(L0_attn_o_w), ob=np.ascontiguousarray(L0_attn_o_b),
        gw=np.ascontiguousarray(L0_ffn_gate_w), wg=np.ascontiguousarray(L0_ffn_w_gate),
        wu=np.ascontiguousarray(L0_ffn_w_up), wd=np.ascontiguousarray(L0_ffn_w_down),
        wsg=np.ascontiguousarray(L0_ffn_ws_gate), wsu=np.ascontiguousarray(L0_ffn_ws_up),
        wsd=np.ascontiguousarray(L0_ffn_ws_down), **consts)
    cores = list(range(NCORES)) if _cores is None else _cores
    in_maps = []
    for ci in range(len(cores)):
        m = dict(shared)
        m["x"] = np.ascontiguousarray(xf[ci * TPC:(ci + 1) * TPC])
        in_maps.append(m)
    res = run_bass_kernel_spmd(nc, in_maps, core_ids=cores)
    if _taps:
        return res
    out = np.concatenate([r["out"] for r in res.results], axis=0)
    return np.ascontiguousarray(out.reshape(B, T, C).astype(np.float32))
